# version 03
# baseline (speedup 1.0000x reference)
import math
from contextlib import ExitStack

import numpy as np
import concourse.bass as bass
import concourse.mybir as mybir
from concourse.bass_utils import run_bass_kernel_spmd

F32 = mybir.dt.float32
BF16 = mybir.dt.bfloat16
I16 = mybir.dt.int16
U16 = mybir.dt.uint16
I32 = mybir.dt.int32
AF = mybir.ActivationFunctionType
ALU = mybir.AluOpType
AXX = mybir.AxisListType.X

NCORES = 8
D = 1024
NDT = D // 128
B = 8
TLOC = 256
NTOK = B * TLOC
NCH = NTOK // 512
STAGE_LIMIT = 6
STAGE_SKIP = 0
ATTN_PHASES = 6
ATTN_F_MODE = 3


class _Pending:
    def __init__(self):
        self.incs = []

    def then_inc(self, sem, amt=1):
        self.incs.append((sem, amt))
        return self


_NO_DRAIN = {"wait_ge", "nop", "drain", "dma_start", "dma_start_transpose",
             "sem_inc", "sem_clear", "collective_compute", "wait_op"}


class _EngRec:
    def __init__(self, prog, auto_drain=False):
        self._prog = prog
        self._auto = auto_drain

    def __getattr__(self, name):
        if name.startswith("_"):
            raise AttributeError(name)
        def call(*a, **kw):
            p = _Pending()
            self._prog.append((name, a, kw, p))
            if self._auto and name not in _NO_DRAIN:
                self._prog.append(("drain", (), {}, _Pending()))
            return p
        return call


class K:
    def __init__(self, nc, block, stack):
        self.nc = nc
        self.block = block
        self.stack = stack
        self.prog = {n: [] for n in "tvsgy"}
        self.t = _EngRec(self.prog["t"])
        self.v = _EngRec(self.prog["v"], auto_drain=True)
        self.s = _EngRec(self.prog["s"], auto_drain=True)
        self.g = _EngRec(self.prog["g"], auto_drain=True)
        self.y = _EngRec(self.prog["y"])
        self.sems = {}
        self.c = {}
        for name in ("t", "v", "s", "gc", "yd", "gd", "sd", "cc", "xa", "xb", "xc", "xd", "xe", "xf"):
            self.sems[name] = stack.enter_context(nc.semaphore("sem_" + name))
            self.c[name] = 0
        self.psum = [stack.enter_context(
            nc.psum_tensor(f"psbank{i}", [128, 512], F32)) for i in range(7)]
        self.psum_bf = stack.enter_context(
            nc.psum_tensor("psbank_bf", [128, 1024], BF16))

    def finalize(self):
        secs = {"t": self.block.tensor, "v": self.block.vector,
                "s": self.block.scalar, "g": self.block.gpsimd,
                "y": self.block.sync}
        for en, entries in self.prog.items():
            def f(e, entries=entries):
                for (name, a, kw, p) in entries:
                    r = getattr(e, name)(*a, **kw)
                    for (sem, amt) in p.incs:
                        r.then_inc(sem, amt)
            secs[en](f)

    def inc(self, instr, sem, amt=1):
        instr.then_inc(self.sems[sem], amt)
        self.c[sem] += amt
        return self.c[sem]

    def mark(self, eng, sem):
        return self.inc(eng.drain(), sem)

    def wait(self, eng, sem, val=None):
        eng.wait_ge(self.sems[sem], self.c[sem] if val is None else val)

    def dma(self, queue, out, in_, sem=None):
        eng = {"y": self.y, "g": self.g, "s": self.s}[queue]
        if sem is None:
            sem = {"y": "yd", "g": "gd", "s": "sd"}[queue]
        instr = eng.dma_start(out=out, in_=in_)
        return self.inc(instr, sem, 16)

    def bar(self):
        engs = [("t", self.t), ("v", self.v), ("s", self.s), ("gc", self.g)]
        for sem, e in engs:
            self.mark(e, sem)
        snap = dict(self.c)
        for _, e in engs + [(None, self.y)]:
            for sem in ("t", "v", "s", "gc", "yd", "gd", "sd", "xa", "xb", "xc", "xd", "xe", "xf"):
                if snap[sem] > 0:
                    e.wait_ge(self.sems[sem], snap[sem])

    def sb(self, name, shape, dt=F32, stack=None):
        return (stack or self.stack).enter_context(
            self.nc.sbuf_tensor(name, shape, dt))


def emit_layernorm(k: K, R, xn_hbm, ones16, ones1, st, mode="ln",
                   normw_col=None, eps_col=None, eps2_col=None, uid="",
                   absmax_out=None):
    t, v, s = k.t, k.v, k.s
    if absmax_out is not None:
        am4 = k.sb("ln_am4" + uid, [128, NCH], F32, st)
    if mode == "lnrms":
        lnt = k.sb("ln_lnt" + uid, [1, NTOK], F32, st)
    xbf = k.sb("ln_xbf" + uid, [128, NDT * 512], BF16, st)
    xsq = k.sb("ln_xsq" + uid, [128, NDT * 512], BF16, st)
    stats = k.sb("ln_stats" + uid, [16, 2 * NTOK], F32, st)
    S1sq = k.sb("ln_s1sq" + uid, [1, NTOK], F32, st)
    brow = k.sb("ln_brow" + uid, [1, 4 * NTOK], BF16, st)
    xnc = k.sb("ln_xnc" + uid, [128, 2 * NDT * 512], BF16, st)
    tmp = k.sb("ln_tmp" + uid, [128, 512], F32, st)
    if mode == "lnrms":
        tmp2 = k.sb("ln_tmp2" + uid, [128, 512], F32, st)

    k.bar()
    for cidx in range(NCH):
        csl = [slice(dt * NTOK + cidx * 512, dt * NTOK + (cidx + 1) * 512)
               for dt in range(NDT)]
        if cidx > 0:
            k.wait(s, "t")
        for dt in range(NDT):
            if mode != "rms":
                s.copy(out=xbf[:, dt * 512:(dt + 1) * 512], in_=R[:, csl[dt]])
            s.activation(out=xsq[:, dt * 512:(dt + 1) * 512], in_=R[:, csl[dt]],
                         func=AF.Square)
        sr = k.mark(s, "s")
        k.wait(t, "s", sr)
        srcs = ([(0, xbf), (1, xsq)] if mode != "rms" else [(1, xsq)])
        for which, src in srcs:
            ps = k.psum[4 + which]
            if cidx > 0:
                k.wait(t, "v")
            for dt in range(NDT):
                mi = t.matmul(ps[0:16, :], lhsT=ones16,
                              rhs=src[:, dt * 512:(dt + 1) * 512],
                              start=(dt == 0), stop=(dt == NDT - 1))
            mm = k.inc(mi, "t")
            k.wait(v, "t", mm)
            v.tensor_copy(out=stats[0:16, which * NTOK + cidx * 512:
                                    which * NTOK + (cidx + 1) * 512],
                          in_=ps[0:16, :])
            k.mark(v, "v")

    S1r = stats[0:1, 0:NTOK]
    Qr = stats[0:1, NTOK:2 * NTOK]
    RSb = brow[0:1, 0:NTOK]
    NMb = brow[0:1, NTOK:2 * NTOK]
    RS2b = brow[0:1, 2 * NTOK:3 * NTOK]
    NM2b = brow[0:1, 3 * NTOK:4 * NTOK]
    k.bar()
    if mode != "rms":
        v.scalar_tensor_tensor(out=S1sq[0:1, :], in0=S1r, scalar=-1.0 / D,
                               in1=S1r, op0=ALU.mult, op1=ALU.mult)
        v.tensor_tensor(out=Qr, in0=Qr, in1=S1sq[0:1, :], op=ALU.add)
    vh = k.mark(v, "v")
    k.wait(s, "v", vh)
    if mode == "lnrms":
        s.activation(out=lnt[0:1, :], in_=Qr, func=AF.Ln, scale=1.0 / D,
                     bias=eps_col[0:1, 0:1])
        s.activation(out=S1sq[0:1, :], in_=lnt[0:1, :], func=AF.Exp,
                     scale=-1.0)
        sh = k.mark(s, "s")
        k.wait(v, "s", sh)
        v.scalar_tensor_tensor(out=S1sq[0:1, :], in0=Qr, scalar=1.0 / D,
                               in1=S1sq[0:1, :], op0=ALU.mult, op1=ALU.mult)
        vh = k.mark(v, "v")
        k.wait(s, "v", vh)
        s.activation(out=S1sq[0:1, :], in_=S1sq[0:1, :], func=AF.Ln,
                     bias=eps2_col[0:1, 0:1])
        sh = k.mark(s, "s")
        k.wait(v, "s", sh)
        v.tensor_tensor(out=Qr, in0=lnt[0:1, :], in1=S1sq[0:1, :], op=ALU.add)
        vh = k.mark(v, "v")
        k.wait(s, "v", vh)
        s.activation(out=Qr, in_=Qr, func=AF.Exp, scale=-0.5)
        s.copy(out=RSb, in_=Qr)
        s.activation(out=lnt[0:1, :], in_=lnt[0:1, :], func=AF.Exp, scale=-0.5)
        s.copy(out=RS2b, in_=lnt[0:1, :])
        sh = k.mark(s, "s")
        k.wait(v, "s", sh)
        v.scalar_tensor_tensor(out=lnt[0:1, :], in0=S1r, scalar=-1.0 / D,
                               in1=lnt[0:1, :], op0=ALU.mult, op1=ALU.mult)
        v.scalar_tensor_tensor(out=S1r, in0=S1r, scalar=-1.0 / D, in1=Qr,
                               op0=ALU.mult, op1=ALU.mult)
        vh2 = k.mark(v, "v")
        k.wait(s, "v", vh2)
        s.copy(out=NM2b, in_=lnt[0:1, :])
        s.copy(out=NMb, in_=S1r)
        k.mark(s, "s")
    else:
        s.activation(out=Qr, in_=Qr, func=AF.Ln, scale=1.0 / D,
                     bias=eps_col[0:1, 0:1])
        s.activation(out=Qr, in_=Qr, func=AF.Exp, scale=-0.5)
        s.copy(out=RSb, in_=Qr)
        sh = k.mark(s, "s")
    if mode == "ln":
        k.wait(v, "s", sh)
        v.scalar_tensor_tensor(out=S1r, in0=S1r, scalar=-1.0 / D, in1=Qr,
                               op0=ALU.mult, op1=ALU.mult)
        vh2 = k.mark(v, "v")
        k.wait(s, "v", vh2)
        s.copy(out=NMb, in_=S1r)
        k.mark(s, "s")
    k.bar()

    for cidx in range(NCH):
        par = cidx % 2
        if mode == "lnrms":
            ps_r, ps_m = k.psum[3], k.psum[4]
            ps_r2, ps_m2 = k.psum[5], k.psum[6]
            if cidx > 0:
                k.wait(t, "v")
            csl2 = slice(cidx * 512, (cidx + 1) * 512)
            t.matmul(ps_r[:, :], lhsT=ones1, rhs=RSb[:, csl2],
                     start=True, stop=True)
            t.matmul(ps_m[:, :], lhsT=ones1, rhs=NMb[:, csl2],
                     start=True, stop=True)
            t.matmul(ps_r2[:, :], lhsT=ones1, rhs=RS2b[:, csl2],
                     start=True, stop=True)
            mi = t.matmul(ps_m2[:, :], lhsT=ones1, rhs=NM2b[:, csl2],
                          start=True, stop=True)
        else:
            ps_r = k.psum[3 + par * 2]
            ps_m = k.psum[4 + par * 2]
            if cidx >= 2:
                k.wait(t, "v")
            mi = t.matmul(ps_r[:, :], lhsT=ones1,
                          rhs=RSb[:, cidx * 512:(cidx + 1) * 512],
                          start=True, stop=True)
            if mode != "rms":
                mi = t.matmul(ps_m[:, :], lhsT=ones1,
                              rhs=NMb[:, cidx * 512:(cidx + 1) * 512],
                              start=True, stop=True)
        mm = k.inc(mi, "t")
        k.wait(v, "t", mm)
        if cidx >= 2:
            k.wait(v, "yd")
        for dt in range(NDT):
            xsl = slice(dt * NTOK + cidx * 512, dt * NTOK + (cidx + 1) * 512)
            osl = slice(par * NDT * 512 + dt * 512, par * NDT * 512 + (dt + 1) * 512)
            if mode == "ln":
                v.tensor_tensor(out=tmp[:, :], in0=R[:, xsl], in1=ps_r[:, :],
                                op=ALU.mult)
                v.tensor_tensor(out=xnc[:, osl], in0=tmp[:, :], in1=ps_m[:, :],
                                op=ALU.add)
            elif mode == "lnrms":
                v.tensor_tensor(out=tmp[:, :], in0=R[:, xsl], in1=ps_r[:, :],
                                op=ALU.mult)
                v.tensor_tensor(out=tmp[:, :], in0=tmp[:, :], in1=ps_m[:, :],
                                op=ALU.add)
                v.tensor_scalar(out=xnc[:, osl], in0=tmp[:, :],
                                scalar1=normw_col[:, dt:dt + 1], scalar2=None,
                                op0=ALU.mult)
                v.tensor_tensor(out=tmp2[:, :], in0=R[:, xsl],
                                in1=ps_r2[:, :], op=ALU.mult)
                v.tensor_tensor(out=tmp2[:, :], in0=tmp2[:, :],
                                in1=ps_m2[:, :], op=ALU.add)
                v.tensor_tensor(out=R[:, xsl], in0=R[:, xsl], in1=tmp2[:, :],
                                op=ALU.add)
            else:
                v.scalar_tensor_tensor(out=xnc[:, osl], in0=R[:, xsl],
                                       scalar=normw_col[:, dt:dt + 1],
                                       in1=ps_r[:, :], op0=ALU.mult, op1=ALU.mult)
        if absmax_out is not None:
            v.tensor_reduce(out=am4[:, cidx:cidx + 1],
                            in_=xnc[:, par * NDT * 512:(par + 1) * NDT * 512],
                            axis=AXX, op=ALU.max, apply_absolute_value=True)
        vv = k.mark(v, "v")
        k.wait(k.y, "v", vv)
        src3 = xnc[:, par * NDT * 512:(par + 1) * NDT * 512].rearrange(
            "p (a c) -> p a c", a=NDT)
        dst3 = xn_hbm[:, :].rearrange("p (a n) -> p a n", a=NDT)[
            :, :, cidx * 512:(cidx + 1) * 512]
        k.dma("y", dst3, src3)
    if absmax_out is not None:
        v.tensor_reduce(out=absmax_out, in_=am4[:, :], axis=AXX, op=ALU.max)
        k.mark(v, "v")
    k.bar()


def emit_l0_ffn(k: K, R, xn_hbm, W, st, uid=""):
    t, v, s, g = k.t, k.v, k.s, k.g

    wsb = {}
    for nm in ("wfT", "wiT", "wcT", "ufT", "uiT", "ucT"):
        wsb[nm] = k.sb("ffn_" + nm + uid, [128, NDT * D], BF16, st)
    bias = k.sb("ffn_bias" + uid, [128, 4 * NDT], F32, st)
    incol = k.sb("ffn_incol" + uid, [128, NDT], F32, st)
    cc = k.sb("ffn_c" + uid, [128, NDT * TLOC], F32, st)
    cbf = k.sb("ffn_cbf" + uid, [128, NDT * TLOC], BF16, st)
    gf = k.sb("ffn_gf" + uid, [128, 2 * TLOC], F32, st)
    gi = k.sb("ffn_gi" + uid, [128, 2 * TLOC], F32, st)
    gg = k.sb("ffn_gg" + uid, [128, 2 * TLOC], F32, st)
    cuo = k.sb("ffn_cuo" + uid, [128, NDT * TLOC], F32, st)
    xstep = k.sb("ffn_xstep" + uid, [128, 2 * NDT * TLOC], BF16, st)
    xnc2 = k.sb("ffn_xnc2" + uid, [128, NDT * 512], BF16, st)

    xnh3 = xn_hbm[:, :].rearrange("p (a n) -> p a n", a=NDT)

    def load_xstep(par, step):
        dst = xstep[:, par * NDT * TLOC:(par + 1) * NDT * TLOC].rearrange(
            "p (a c) -> p a c", a=NDT)
        sem = "xa" if par == 0 else "xb"
        return (sem, k.dma("y", dst, xnh3[:, :, step * TLOC:(step + 1) * TLOC],
                           sem=sem))

    for nm in wsb:
        k.dma("g", wsb[nm][:, :], W[nm][:, :])
    k.dma("y", bias[:, :], W["bias"][:, :])
    k.dma("y", incol[:, :], W["incol"][:, :])
    xs_done = [load_xstep(0, 0), load_xstep(1, 1)]
    wload = k.c["gd"]
    k.wait(v, "yd")
    for dt in range(NDT):
        bc = incol[:, dt:dt + 1].broadcast_to([128, TLOC])
        v.tensor_copy(out=cc[:, dt * TLOC:(dt + 1) * TLOC], in_=bc)
        v.tensor_copy(out=cbf[:, dt * TLOC:(dt + 1) * TLOC], in_=bc)
    k.mark(v, "v")
    k.wait(t, "gd", wload)
    k.bar()

    names = [("wfT", "ufT"), ("wiT", "uiT"), ("wcT", "ucT")]
    gates = [gf, gi, gg]
    funcs = [AF.Sigmoid, AF.Sigmoid, AF.Tanh]
    s_marks, v_marks = [], []
    for step in range(B):
        cr = cw = cc
        spar = step % 2
        xs = xstep[:, spar * NDT * TLOC:(spar + 1) * NDT * TLOC]
        k.wait(t, xs_done[step][0], xs_done[step][1])
        if step + 2 < B:
            pass
        for et in range(NDT):
            bank0 = (et % 2) * 3
            par = et % 2
            if len(s_marks) >= 2:
                k.wait(t, "s", s_marks[-2])
            if len(v_marks) >= 2:
                k.wait(s, "v", v_marks[-2])
            for gidx in range(3):
                ps = k.psum[bank0 + gidx]
                wn, un = names[gidx]
                for dt in range(NDT):
                    t.matmul(ps[:, 0:TLOC],
                             lhsT=wsb[wn][:, dt * D + et * 128:dt * D + (et + 1) * 128],
                             rhs=xs[:, dt * TLOC:(dt + 1) * TLOC],
                             start=(dt == 0), stop=False)
                for dt in range(NDT):
                    mi = t.matmul(ps[:, 0:TLOC],
                                  lhsT=wsb[un][:, dt * D + et * 128:dt * D + (et + 1) * 128],
                                  rhs=cbf[:, dt * TLOC:(dt + 1) * TLOC],
                                  start=False, stop=(dt == NDT - 1))
            mmv = k.inc(mi, "t")
            k.wait(s, "t", mmv)
            gsl = slice(par * TLOC, (par + 1) * TLOC)
            for gidx in range(3):
                ps = k.psum[bank0 + gidx]
                s.activation(out=gates[gidx][:, gsl], in_=ps[:, 0:TLOC],
                             func=funcs[gidx],
                             bias=bias[:, gidx * NDT + et:gidx * NDT + et + 1])
            sv = k.mark(s, "s")
            s_marks.append(sv)
            k.wait(v, "s", sv)
            esl = slice(et * TLOC, (et + 1) * TLOC)
            v.tensor_tensor(out=gg[:, gsl], in0=gi[:, gsl], in1=gg[:, gsl],
                            op=ALU.mult)
            v.tensor_tensor(out=cw[:, esl], in0=gf[:, gsl], in1=cr[:, esl],
                            op=ALU.mult)
            v.tensor_tensor(out=cw[:, esl], in0=cw[:, esl], in1=gg[:, gsl],
                            op=ALU.add)
            v_marks.append(k.mark(v, "v"))
        vv = k.mark(v, "v")
        k.wait(s, "v", vv)
        for dt in range(NDT):
            s.copy(out=cbf[:, dt * TLOC:(dt + 1) * TLOC],
                   in_=cw[:, dt * TLOC:(dt + 1) * TLOC])
        sv2 = k.mark(s, "s")
        k.wait(t, "s", sv2)
        if step + 2 < B:
            k.wait(k.y, "t", k.c["t"])
            xs_done.append(load_xstep(spar, step + 2))

    k.bar()
    woT, uoT = wsb["wfT"], wsb["wiT"]
    k.dma("g", woT[:, :], W["woT"][:, :])
    k.dma("g", uoT[:, :], W["uoT"][:, :])
    wload2 = k.c["gd"]
    k.wait(t, "gd", wload2)

    for et in range(NDT):
        ps = k.psum[5 + (et % 2)]
        if et >= 2:
            k.wait(t, "v")
        for dt in range(NDT):
            mi = t.matmul(ps[:, 0:TLOC],
                          lhsT=uoT[:, dt * D + et * 128:dt * D + (et + 1) * 128],
                          rhs=cbf[:, dt * TLOC:(dt + 1) * TLOC],
                          start=(dt == 0), stop=(dt == NDT - 1))
        mmv = k.inc(mi, "t")
        k.wait(v, "t", mmv)
        v.tensor_copy(out=cuo[:, et * TLOC:(et + 1) * TLOC], in_=ps[:, 0:TLOC])
        k.mark(v, "v")
    k.bar()

    dst3 = xnc2[:, :].rearrange("p (a c) -> p a c", a=NDT)
    xld = [k.dma("y", dst3, xnh3[:, :, 0:512], sem="xa")]
    for cidx in range(NCH):
        xch = xnc2[:, :]
        k.wait(t, "xa", xld[cidx])
        for et in range(NDT):
            bank = 2 * (et % 2) + (cidx % 2) % 2
            ps = k.psum[(et % 2) * 2 + (cidx % 2)]
            if et >= 1 or cidx >= 2:
                k.wait(t, "v")
            for dt in range(NDT):
                mi = t.matmul(ps[:, :],
                              lhsT=woT[:, dt * D + et * 128:dt * D + (et + 1) * 128],
                              rhs=xch[:, dt * 512:(dt + 1) * 512],
                              start=(dt == 0), stop=(dt == NDT - 1))
            mmv = k.inc(mi, "t")
            k.wait(v, "t", mmv)
            cub = cuo[:, et * TLOC:(et + 1) * TLOC]
            cub3 = cub.rearrange("p (o t) -> p o t", o=1).broadcast_to(
                [128, 2, TLOC])
            ps3 = ps[:, :].rearrange("p (b t) -> p b t", b=2)
            v.tensor_tensor(out=ps3, in0=ps3, in1=cub3, op=ALU.add)
            vv = k.mark(v, "v")
            k.wait(s, "v", vv)
            s.activation(out=ps[:, :], in_=ps[:, :], func=AF.Tanh,
                         bias=bias[:, 3 * NDT + et:3 * NDT + et + 1])
            sv = k.mark(s, "s")
            k.wait(v, "s", sv)
            rsl = slice(et * NTOK + cidx * 512, et * NTOK + (cidx + 1) * 512)
            v.scalar_tensor_tensor(out=R[:, rsl], in0=ps[:, :], scalar=0.5,
                                   in1=R[:, rsl], op0=ALU.mult, op1=ALU.add)
            k.mark(v, "v")
        if cidx + 1 < NCH:
            k.wait(k.y, "t", k.c["t"])
            xld.append(k.dma("y", dst3,
                             xnh3[:, :, (cidx + 1) * 512:(cidx + 2) * 512],
                             sem="xa"))
    k.bar()


NBIN = 64


def emit_l0_attn(k: K, R, xn_hbm, xh_hbm, W, st, core_id_dummy=None, uid=""):
    t, v, s, g, y = k.t, k.v, k.s, k.g, k.y
    xnh3 = xn_hbm[:, :].rearrange("p (a n) -> p a n", a=NDT)
    xhh = xh_hbm

    dft = k.sb("at_dft" + uid, [128, NBIN], BF16, st)
    ib = k.sb("at_ib" + uid, [128, NBIN], BF16, st)
    xlow = k.sb("at_xlow" + uid, [128, NDT * 256], BF16, st)
    xlow1m = k.sb("at_xlow1m" + uid, [128, NDT * 256], BF16, st)
    gsb = k.sb("at_g" + uid, [128, NDT * 32], F32, st)
    alow = k.sb("at_alow" + uid, [128, NDT * 256], BF16, st)
    acol = k.sb("at_acol" + uid, [128, 2], F32, st)
    mrow = k.sb("at_mask" + uid, [128, 128], BF16, st)
    ohot = k.sb("at_ohot" + uid, [128, 16], BF16, st)

    k.dma("g", dft[:, :], W["dft"][:, :])
    k.dma("g", ib[:, :], W["ib"][:, :])
    k.dma("g", mrow[:, :], W["mask"][:, :])
    k.dma("g", ohot[:, :], W["onehot"][:, :])

    araw = k.sb("at_araw" + uid, [1, 1], F32, st)
    k.dma("y", araw[:, :], W["alpha"][:, :])
    k.wait(s, "yd")
    s.activation(out=araw[:, :], in_=araw[:, :], func=AF.Sigmoid)
    sa = k.mark(s, "s")
    k.wait(s, "s", sa)
    s.copy(out=mrow[0:1, 0:1], in_=araw[:, :]) if False else None
    ab = k.sb("at_ab" + uid, [1, 1], BF16, st)
    s.copy(out=ab[:, :], in_=araw[:, :])
    sa2 = k.mark(s, "s")
    k.wait(t, "s", sa2)
    k.wait(t, "gd")
    ones1 = W["ones1"]
    mi = t.matmul(k.psum[6][:, 0:1], lhsT=ones1, rhs=ab[:, :], start=True,
                  stop=True)
    tb = k.inc(mi, "t")
    k.wait(v, "t", tb)
    v.tensor_copy(out=acol[:, 0:1], in_=k.psum[6][:, 0:1])
    v.tensor_scalar(out=acol[:, 1:2], in0=acol[:, 0:1], scalar1=-1.0,
                    scalar2=1.0, op0=ALU.mult, op1=ALU.add)
    v.memset(xlow[:, :], 0.0)
    k.mark(v, "v")
    k.bar()

    with ExitStack() as stA:
        xnw = k.sb("at_xnw" + uid, [128, 2 * NDT * 128], BF16, stA)
        xnt = k.sb("at_xnt" + uid, [128, NDT * 128], BF16, stA)
        xhe = k.sb("at_xhe" + uid, [128, 2 * 512], BF16, stA)
        pairs = [(b, cp) for b in range(B) for cp in range(2)]

        def ld_xnw(i):
            b, cp = pairs[i]
            par = i % 2
            sem = "xa" if par == 0 else "xc"
            dst = xnw[:, par * NDT * 128:(par + 1) * NDT * 128].rearrange(
                "p (a c) -> p a c", a=NDT)
            return (sem, k.dma("y", dst,
                               xnh3[:, :, b * 256 + cp * 128:
                                    b * 256 + (cp + 1) * 128], sem=sem))

        ldcnt = [ld_xnw(0), ld_xnw(1)]
        for i, (b, cp) in enumerate(pairs):
            par = i % 2
            xbuf = xnw[:, par * NDT * 128:(par + 1) * NDT * 128]
            k.wait(t, ldcnt[i][0], ldcnt[i][1])
            for half in range(2):
                if i > 0 or half > 0:
                    k.wait(t, "v")
                for j in range(4):
                    dt = half * 4 + j
                    mi = t.transpose(k.psum_bf[:, j * 128:(j + 1) * 128],
                                     in_=xbuf[:, dt * 128:(dt + 1) * 128],
                                     identity=W["identb"])
                tc = k.inc(mi, "t")
                k.wait(v, "t", tc)
                v.tensor_copy(out=xnt[:, half * 512:(half + 1) * 512],
                              in_=k.psum_bf[:, 0:512])
                k.mark(v, "v")
            ps = k.psum[i % 2]
            k.wait(t, "v")
            for ch01 in range(2):
                ch = cp * 2 + ch01
                off = ch01 * 64
                for dt in range(NDT):
                    mi = t.matmul(
                        ps[:, dt * 64:(dt + 1) * 64] if ch01 == 0 else
                        k.psum[2 + (i % 2)][:, dt * 64:(dt + 1) * 64],
                        lhsT=xnt[off:off + 64, dt * 128:(dt + 1) * 128],
                        rhs=dft[off:off + 64, :],
                        start=(dt == 0), stop=(dt == NDT - 1))
            tm = k.inc(mi, "t")
            k.wait(v, "t", tm)
            k.wait(v, "xb")
            for ch01 in range(2):
                ch = cp * 2 + ch01
                psc = ps if ch01 == 0 else k.psum[2 + (i % 2)]
                dst = xhe[:, ch01 * 512:(ch01 + 1) * 512]
                v.tensor_copy(out=dst, in_=psc[:, :])
                re_dst = xlow[:, :].rearrange("p (a c) -> p a c", a=NDT)[
                    :, :, b * 16 + ch * 4:b * 16 + ch * 4 + 4]
                re_src = psc[:, :].rearrange("p (a c) -> p a c", a=NDT)[
                    :, :, 0:4]
                v.tensor_copy(out=re_dst, in_=re_src)
                im_dst = xlow[:, :].rearrange("p (a c) -> p a c", a=NDT)[
                    :, :, 128 + b * 16 + ch * 4 + 1:128 + b * 16 + ch * 4 + 4]
                im_src = psc[:, :].rearrange("p (a c) -> p a c", a=NDT)[
                    :, :, 33:36]
                v.tensor_copy(out=im_dst, in_=im_src)
            vv = k.mark(v, "v")
            k.wait(y, "v", vv)
            k.dma("y", xhh[:, (b * 4 + cp * 2) * 512:(b * 4 + cp * 2 + 2) * 512],
                  xhe[:, :], sem="xb")
            if i + 2 < len(pairs):
                k.wait(y, "t", k.c["t"])
                ldcnt.append(ld_xnw(i + 2))
    k.bar()

    for dt in range(NDT):
        v.tensor_scalar(out=xlow1m[:, dt * 256:(dt + 1) * 256],
                        in0=xlow[:, dt * 256:(dt + 1) * 256],
                        scalar1=acol[:, 1:2], scalar2=None, op0=ALU.mult)
    k.mark(v, "v")

    if ATTN_PHASES < 2:
        k.bar()
        return
    qsb = k.sb("at_q" + uid, [128, NDT * 256], BF16, st)
    osb = k.sb("at_o" + uid, [128, NDT * 256], BF16, st)
    ksb = k.sb("at_k" + uid, [128, NCORES * 1024], BF16, st)
    vtsb = k.sb("at_vt" + uid, [128, B * 512], BF16, st)
    with ExitStack() as stB:
        qwT = k.sb("at_qwT" + uid, [128, NDT * D], BF16, stB)
        kwT = k.sb("at_kwT" + uid, [128, NDT * 512], BF16, stB)
        vwT = k.sb("at_vwT" + uid, [128, NDT * 256], BF16, stB)
        gwT = k.sb("at_gwT" + uid, [128, NDT * D], BF16, stB)
        kloc = k.sb("at_kloc" + uid, [128, 1024], BF16, stB)
        vtloc = k.sb("at_vtloc" + uid, [128, 512], BF16, stB)
        k.dma("g", qwT[:, :], W["qwT"][:, :])
        k.dma("g", kwT[:, :], W["kwT"][:, :])
        k.dma("g", vwT[:, :], W["vwT"][:, :])
        k.dma("g", gwT[:, :], W["gwT"][:, :])
        k.wait(t, "gd")
        for et in range(NDT):
            ps = k.psum[et % 2]
            if et >= 2:
                k.wait(t, "v")
            for dt in range(NDT):
                mi = t.matmul(ps[:, 0:256],
                              lhsT=qwT[:, dt * D + et * 128:dt * D + (et + 1) * 128],
                              rhs=xlow[:, dt * 256:(dt + 1) * 256],
                              start=(dt == 0), stop=(dt == NDT - 1))
            tm = k.inc(mi, "t")
            k.wait(v, "t", tm)
            v.tensor_scalar(out=qsb[:, et * 256:(et + 1) * 256], in0=ps[:, 0:256],
                            scalar1=0.125, scalar2=None, op0=ALU.mult)
            k.mark(v, "v")
        for et in range(4):
            ps = k.psum[2 + (et % 2)]
            if et >= 2:
                k.wait(t, "v")
            for dt in range(NDT):
                mi = t.matmul(ps[:, 0:256],
                              lhsT=kwT[:, dt * 512 + et * 128:dt * 512 + (et + 1) * 128],
                              rhs=xlow[:, dt * 256:(dt + 1) * 256],
                              start=(dt == 0), stop=(dt == NDT - 1))
            tm = k.inc(mi, "t")
            k.wait(v, "t", tm)
            v.tensor_copy(out=kloc[:, et * 256:(et + 1) * 256], in_=ps[:, 0:256])
            k.mark(v, "v")
        for ri in range(2):
            ps = k.psum[4 + ri]
            for dt in range(NDT):
                mi = t.matmul(ps[:, 0:256],
                              lhsT=xlow[:, dt * 256 + ri * 128:dt * 256 + ri * 128 + 128],
                              rhs=vwT[:, dt * 256:(dt + 1) * 256],
                              start=(dt == 0), stop=(dt == NDT - 1))
            tm = k.inc(mi, "t")
            k.wait(v, "t", tm)
            v.tensor_copy(out=vtloc[:, ri * 256:(ri + 1) * 256], in_=ps[:, 0:256])
            k.mark(v, "v")
        for et in range(NDT):
            ps = k.psum[6]
            if et > 0:
                k.wait(t, "s")
            for dt in range(NDT):
                rhs = xlow[:, :].rearrange("p (a r b c) -> p a r b c", a=NDT,
                                           r=2, b=B)[dt, :, :, :] if False else None
                rhs = xlow[:, dt * 256:dt * 256 + 128].rearrange(
                    "p (b c) -> p b c", b=B)[:, :, 0::4]
                mi = t.matmul(ps[:, 0:32],
                              lhsT=gwT[:, dt * D + et * 128:dt * D + (et + 1) * 128],
                              rhs=rhs,
                              start=(dt == 0), stop=(dt == NDT - 1))
            tm = k.inc(mi, "t")
            k.wait(s, "t", tm)
            s.activation(out=gsb[:, et * 32:(et + 1) * 32], in_=ps[:, 0:32],
                         func=AF.Sigmoid, scale=1.0 / 64.0)
            k.mark(s, "s")
        vv = k.c["v"]
        k.wait(g, "v", vv)
        k.wait(g, "s")
        k.dma("g", W["cc_in"][:, 0:1024], kloc[:, :])
        k.dma("g", W["cc_in"][:, 1024:1536], vtloc[:, :])
        g.wait_ge(k.sems["gd"], k.c["gd"])
        ci = g.collective_compute(
            "AllGather", ALU.bypass,
            replica_groups=[list(range(NCORES))],
            ins=[W["cc_in"].opt()],
            outs=[W["cc_out"].opt()],
        )
        k.inc(ci, "cc")
        ccn = k.c["cc"]
        k.wait(y, "cc", ccn)
        cco = W["cc_out"]
        for r in range(NCORES):
            k.dma("y", ksb[:, r * 1024:(r + 1) * 1024],
                  cco[128 * r:128 * (r + 1), 0:1024], sem="xa")
            dstv = vtsb[16 * r:16 * r + 16, :].rearrange(
                "p (b c) -> p b c", b=B)
            inv = cco[128 * r:128 * (r + 1), 1024:1536].rearrange(
                "(b q) c -> q b c", b=B)
            k.dma("y", dstv, inv, sem="xa")

    if ATTN_PHASES < 3:
        k.bar()
        return
    ahig = k.sb("at_ahig" + uid, [128, NDT * 2 * 29 * 32], BF16, st)
    with ExitStack() as stC:
        pwrT = k.sb("at_pwrT" + uid, [128, NDT * D], BF16, stC)
        pwiT = k.sb("at_pwiT" + uid, [128, NDT * D], BF16, stC)
        dwc = k.sb("at_dw" + uid, [128, 2 * NDT * 7], F32, stC)
        gin = k.sb("at_gin" + uid, [128, NDT * 29 * 32], BF16, stC)
        cin = k.sb("at_cin" + uid, [128, 2 * 29 * 32], BF16, stC)
        acc = k.sb("at_acc" + uid, [128, 29 * 32], F32, stC)
        k.dma("g", pwrT[:, :], W["pwrT"][:, :])
        k.dma("g", pwiT[:, :], W["pwiT"][:, :])
        k.dma("g", dwc[:, :], W["dw"][:, :])
        gwl = k.c["gd"]
        k.wait(v, "gd", gwl)
        k.wait(v, "xb")
        for ri in range(2):
            pwT = pwrT if ri == 0 else pwiT
            if ri == 1:
                k.wait(s, "t")
            for dt in range(NDT):
                par = dt % 2
                cbuf = cin[:, par * 928:(par + 1) * 928]
                cb3 = cbuf.rearrange("p (w c) -> p w c", w=32)
                k.wait(y, "v")
                if ri == 0:
                    src = xhh[:, :].rearrange("p (w c) -> p w c", w=32)[
                        :, :, dt * 64 + 4:dt * 64 + 33]
                    ld = k.dma("y", cb3, src, sem="xb")
                else:
                    src = xhh[:, :].rearrange("p (w c) -> p w c", w=32)[
                        :, :, dt * 64 + 36:dt * 64 + 64]
                    ld = k.dma("y", cb3[:, :, 0:28], src, sem="xb")
                k.wait(v, "xb", ld)
                if ri == 1:
                    v.memset(cb3[:, :, 28:29], 0.0)
                dcol = lambda j: dwc[:, ri * NDT * 7 + dt * 7 + j:
                                     ri * NDT * 7 + dt * 7 + j + 1]
                a3 = acc.rearrange("p (w c) -> p w c", w=32)
                k.wait(v, "s")
                v.tensor_scalar(out=a3, in0=cb3, scalar1=dcol(6), scalar2=None,
                                op0=ALU.mult)
                for j in range(5, -1, -1):
                    w = 23 + j
                    v.scalar_tensor_tensor(out=a3[:, :, 6 - j:29],
                                           in0=cb3[:, :, 0:w], scalar=dcol(j),
                                           in1=a3[:, :, 6 - j:29],
                                           op0=ALU.mult, op1=ALU.add)
                vm = k.mark(v, "v")
                k.wait(s, "v", vm)
                s.activation(out=gin[:, dt * 928:(dt + 1) * 928], in_=acc[:, :],
                             func=AF.Gelu)
                k.mark(s, "s")
            sv = k.c["s"]
            k.wait(t, "s", sv)
            for et in range(NDT):
                for half in range(2):
                    ps = k.psum[(et % 2) * 2 + half]
                    csl = slice(half * 464, (half + 1) * 464)
                    if et >= 2:
                        k.wait(t, "v")
                    for dt in range(NDT):
                        mi = t.matmul(
                            ps[:, 0:464],
                            lhsT=pwT[:, dt * D + et * 128:dt * D + (et + 1) * 128],
                            rhs=gin[:, dt * 928 + half * 464:
                                    dt * 928 + (half + 1) * 464],
                            start=(dt == 0), stop=(dt == NDT - 1))
                    tm = k.inc(mi, "t")
                    par2 = (et * 2 + half) % 2
                    xbuf = cin[:, par2 * 928 + 0:par2 * 928 + 464]
                    xb3 = xbuf.rearrange("p (w c) -> p w c", w=16)
                    k.wait(y, "v")
                    if ri == 0:
                        src = xhh[:, :].rearrange("p (w c) -> p w c", w=32)[
                            :, half * 16:(half + 1) * 16,
                            et * 64 + 4:et * 64 + 33]
                        ld = k.dma("y", xb3, src, sem="xb")
                    else:
                        src = xhh[:, :].rearrange("p (w c) -> p w c", w=32)[
                            :, half * 16:(half + 1) * 16,
                            et * 64 + 36:et * 64 + 64]
                        ld = k.dma("y", xb3[:, :, 0:28], src, sem="xb")
                    k.wait(v, "xb", ld)
                    if ri == 1:
                        v.memset(xb3[:, :, 28:29], 0.0)
                    k.wait(v, "t", tm)
                    gap = gsb[:, et * 32 + half * 16:et * 32 + (half + 1) * 16]
                    ahsl = ahig[:, (et * 2 + ri) * 928 + half * 464:
                                (et * 2 + ri) * 928 + (half + 1) * 464]
                    ah3 = ahsl.rearrange("p (w c) -> p w c", w=16)
                    ps3 = ps[:, 0:464].rearrange("p (w c) -> p w c", w=16)
                    gbc = gap.rearrange("p (w o) -> p w o", o=1).broadcast_to(
                        [128, 16, 29])
                    v.scalar_tensor_tensor(out=ah3, in0=ps3, scalar=1.0,
                                           in1=gbc, op0=ALU.mult, op1=ALU.mult)
                    v.tensor_tensor(out=ah3, in0=ah3, in1=xb3, op=ALU.add)
                    k.mark(v, "v")
    k.bar()

    if ATTN_PHASES < 4:
        k.bar()
        return
    with ExitStack() as stD:
        praw = k.sb("at_praw" + uid, [16, 8 * 128], BF16, stD)
        ptsb = k.sb("at_pt" + uid, [128, 2 * 16], BF16, stD)
        smal = k.sb("at_smal" + uid, [16, 3 * 8], F32, stD)
        k.wait(t, "xa")
        k.bar()
        itr = 0
        for h in range(16):
            for ri in range(2):
                et, po = h // 2, (h % 2) * 64
                for b in range(B):
                    ps = k.psum[b // 2][0:16, (b % 2) * 128:(b % 2) * 128 + 128]
                    q_ap = qsb[po:po + 64,
                               et * 256 + ri * 128 + b * 16:
                               et * 256 + ri * 128 + b * 16 + 16]
                    kh = h // 4
                    k_ap = ksb[po:po + 64, :].rearrange(
                        "p (r x) -> p r x", r=NCORES)[
                        :, :, kh * 256 + ri * 128 + b * 16:
                        kh * 256 + ri * 128 + b * 16 + 16]
                    t.matmul(ps, lhsT=q_ap, rhs=k_ap, start=(b % 2 == 0),
                             stop=False, skip_group_check=True)
                    mi = t.matmul(ps, lhsT=ohot[po:po + 4, :],
                                  rhs=mrow[po:po + 4, :],
                                  start=False, stop=(b % 2 == 1),
                                  skip_group_check=True)
                tm = k.inc(mi, "t")
                k.wait(v, "t", tm)
                k.wait(v, "s")
                for bb in range(4):
                    ps = k.psum[bb]
                    v.tensor_reduce(out=smal[0:16, bb * 2:bb * 2 + 2],
                                    in_=ps[0:16, 0:256].rearrange(
                                        "p (b c) -> p b c", b=2),
                                    axis=AXX, op=ALU.max, negate=True)
                vm = k.mark(v, "v")
                k.wait(s, "v", vm)
                k.wait(s, "t")
                for b in range(B):
                    ps = k.psum[b // 2][0:16, (b % 2) * 128:(b % 2) * 128 + 128]
                    s.activation(out=praw[0:16, b * 128:(b + 1) * 128], in_=ps,
                                 func=AF.Exp, bias=smal[0:16, b:b + 1],
                                 accum_out=smal[0:16, 8 + b:9 + b])
                sm = k.mark(s, "s")
                k.wait(v, "s", sm)
                v.reciprocal(out=smal[0:16, 16:24], in_=smal[0:16, 8:16])
                v.tensor_scalar(out=praw[0:16, :].rearrange(
                                    "p (b c) -> p b c", b=B),
                                in0=praw[0:16, :].rearrange(
                                    "p (b c) -> p b c", b=B),
                                scalar1=None, scalar2=None, op0=ALU.mult) \
                    if False else None
                v.tensor_tensor(out=praw[0:16, :].rearrange(
                                    "p (b c) -> p b c", b=B),
                                in0=praw[0:16, :].rearrange(
                                    "p (b c) -> p b c", b=B),
                                in1=smal[0:16, 16:24].rearrange(
                                    "p (b o) -> p b o", b=B).broadcast_to(
                                    [16, B, 128]),
                                op=ALU.mult)
                vm2 = k.mark(v, "v")
                k.wait(t, "v", vm2)
                for b in range(B):
                    par = b % 2
                    mi = t.transpose(k.psum_bf[:, par * 512:par * 512 + 16],
                                     in_=praw[0:16, b * 128:(b + 1) * 128],
                                     identity=W["identb"][0:16, 0:16])
                    tt = k.inc(mi, "t")
                    k.wait(s, "t", tt)
                    s.copy(out=ptsb[:, par * 16:(par + 1) * 16],
                           in_=k.psum_bf[:, par * 512:par * 512 + 16])
                    sm2 = k.mark(s, "s")
                    k.wait(t, "s", sm2)
                    mi = t.matmul(k.psum[4 + par][0:64, 0:16],
                                  lhsT=vtsb[:, b * 512 + ri * 256 + (h // 4) * 64:
                                            b * 512 + ri * 256 + (h // 4) * 64 + 64],
                                  rhs=ptsb[:, par * 16:(par + 1) * 16],
                                  start=True, stop=True)
                    tv = k.inc(mi, "t")
                    k.wait(s, "t", tv)
                    s.copy(out=osb[po:po + 64,
                                   et * 256 + ri * 128 + b * 16:
                                   et * 256 + ri * 128 + b * 16 + 16],
                           in_=k.psum[4 + par][0:64, 0:16])
                    k.mark(s, "s")
                k.wait(t, "s")
                itr += 1
    k.bar()

    if ATTN_PHASES < 5:
        k.bar()
        return
    with ExitStack() as stE:
        owT = k.sb("at_owT" + uid, [128, NDT * D], BF16, stE)
        k.dma("g", owT[:, :], W["owT"][:, :])
        k.wait(t, "gd")
        for et in range(NDT):
            ps = k.psum[et % 2]
            if et >= 2:
                k.wait(t, "v")
            for dt in range(NDT):
                mi = t.matmul(ps[:, 0:256],
                              lhsT=owT[:, dt * D + et * 128:dt * D + (et + 1) * 128],
                              rhs=osb[:, dt * 256:(dt + 1) * 256],
                              start=(dt == 0), stop=(dt == NDT - 1))
            tm = k.inc(mi, "t")
            k.wait(v, "t", tm)
            v.scalar_tensor_tensor(out=alow[:, et * 256:(et + 1) * 256],
                                   in0=ps[:, 0:256], scalar=acol[:, 0:1],
                                   in1=xlow1m[:, et * 256:(et + 1) * 256],
                                   op0=ALU.mult, op1=ALU.add)
            k.mark(v, "v")
    k.bar()

    if ATTN_PHASES < 6:
        k.bar()
        return
    with ExitStack() as stF:
        yh = k.sb("at_yh" + uid, [128, 2 * 256], BF16, stF)
        yht = k.sb("at_yht" + uid, [128, 256], BF16, stF)
        for dt in range(NDT):
            for b in range(B):
                par = (dt * B + b) % 2
                ybuf = yh[:, par * 256:(par + 1) * 256]
                y3 = ybuf.rearrange("p (c x) -> p c x", c=4)
                if dt * B + b >= 2:
                    k.wait(v, "t")
                v.tensor_copy(out=y3[:, :, 0:4],
                              in_=alow[:, dt * 256 + b * 16:
                                       dt * 256 + b * 16 + 16].rearrange(
                                  "p (c x) -> p c x", c=4))
                v.tensor_copy(out=y3[:, :, 33:36],
                              in_=alow[:, dt * 256 + 128 + b * 16:
                                       dt * 256 + 128 + b * 16 + 16].rearrange(
                                  "p (c x) -> p c x", c=4)[:, :, 1:4])
                v.tensor_copy(out=y3[:, :, 4:33],
                              in_=ahig[:, (dt * 2 + 0) * 928 + b * 4 * 29:
                                       (dt * 2 + 0) * 928 + (b * 4 + 4) * 29]
                              .rearrange("p (c x) -> p c x", c=4))
                v.tensor_copy(out=y3[:, :, 36:64],
                              in_=ahig[:, (dt * 2 + 1) * 928 + b * 4 * 29:
                                       (dt * 2 + 1) * 928 + (b * 4 + 4) * 29]
                              .rearrange("p (c x) -> p c x", c=4)[:, :, 0:28])
                vm = k.mark(v, "v")
                k.wait(t, "v", vm)
                for hf in range(2):
                    mi = t.transpose(k.psum_bf[:, hf * 512:hf * 512 + 128],
                                     in_=ybuf[:, hf * 128:(hf + 1) * 128],
                                     identity=W["identb"])
                tt = k.inc(mi, "t")
                k.wait(s, "t", tt)
                s.copy(out=yht[:, 0:128], in_=k.psum_bf[:, 0:128])
                s.copy(out=yht[:, 128:256], in_=k.psum_bf[:, 512:640])
                sm = k.mark(s, "s")
                if ATTN_F_MODE < 2:
                    continue
                k.wait(t, "s", sm)
                k.wait(t, "v")
                psE = k.psum[2 + par]
                psO = k.psum[4 + par]
                for ch01 in range(4):
                    off = (ch01 % 2) * 64
                    psx = psE if ch01 % 2 == 0 else psO
                    mi = t.matmul(psx[:, (ch01 // 2) * 64:(ch01 // 2) * 64 + 64],
                                  lhsT=yht[off:off + 64,
                                           (ch01 // 2) * 128:
                                           (ch01 // 2) * 128 + 128],
                                  rhs=ib[off:off + 64, :],
                                  start=(ch01 < 2), stop=(ch01 >= 2))
                tm = k.inc(mi, "t")
                if ATTN_F_MODE < 3:
                    continue
                k.wait(v, "t", tm)
                rsl = R[:, dt * NTOK + b * 256:dt * NTOK + (b + 1) * 256]
                r4 = rsl.rearrange("p (c x) -> p c x", c=4)
                v.tensor_tensor(out=r4[:, 0::2, :], in0=r4[:, 0::2, :],
                                in1=psE[:, 0:128].rearrange(
                                    "p (c x) -> p c x", c=2), op=ALU.add)
                v.tensor_tensor(out=r4[:, 1::2, :], in0=r4[:, 1::2, :],
                                in1=psO[:, 0:128].rearrange(
                                    "p (c x) -> p c x", c=2), op=ALU.add)
                k.mark(v, "v")
    k.bar()


def emit_l1_ffn(k: K, R, xn_hbm, W, st, xnmax_col, first, uid="", dbg=None):
    t, v, s, g, y = k.t, k.v, k.s, k.g, k.y
    IFF = 3072
    NIT = IFF // 128
    QM = 32767.0

    scr = k.sb("ff_scr" + uid, [1, 32], F32, st)
    scb = k.sb("ff_scb" + uid, [128, 16], F32, st)
    ones1f = k.sb("ff_o1f" + uid, [1, 128], F32, st)
    v.memset(ones1f[:, :], 1.0)
    v.memset(scr[0:1, :], 0.0)

    MX, MG, MU, M2, M3 = 0, 1, 2, 3, 4
    SX, IX, SG, IG, SU, IU, C2, C3, S3 = 5, 6, 7, 8, 9, 10, 11, 12, 13

    prow = k.sb("ff_prow" + uid, [1, 128], F32, st)

    def pcol_to_scr(col_ap, slot):
        k.wait(y, "v")
        wr = k.dma("y", W["colb"][0:1, :], col_ap, sem="xd")
        k.wait(y, "xd", wr)
        rd = k.dma("y", prow[0:1, :], W["colb"][0:1, :], sem="xd")
        k.wait(v, "xd", rd)
        v.tensor_reduce(out=scr[0:1, slot:slot + 1], in_=prow[0:1, :],
                        axis=AXX, op=ALU.max)
        return k.mark(v, "v")

    def allreduce_slots(lo, n):
        vv = k.c["v"]
        k.wait(g, "v", vv)
        k.dma("g", W["cc2_in"][0:1, 0:16], scr[0:1, 0:16])
        g.wait_ge(k.sems["gd"], k.c["gd"])
        ci = g.collective_compute(
            "AllReduce", ALU.max, replica_groups=[list(range(NCORES))],
            ins=[W["cc2_in"].opt()], outs=[W["cc2_out"].opt()])
        k.inc(ci, "cc")
        k.wait(g, "cc", k.c["cc"])
        k.dma("g", scr[0:1, lo:lo + n], W["cc2_out"][0:1, lo:lo + n])
        gd = k.c["gd"]
        k.wait(v, "gd", gd)
        k.wait(s, "gd", gd)

    def bcast_cols(slots):
        k.wait(t, "v")
        k.wait(t, "s")
        for sl in slots:
            mi = t.matmul(k.psum[6][:, sl:sl + 1], lhsT=ones1f[0:1, :],
                          rhs=scr[0:1, sl:sl + 1], start=True, stop=True)
        tm = k.inc(mi, "t")
        k.wait(v, "t", tm)
        for sl in slots:
            v.tensor_copy(out=scb[:, sl:sl + 1], in_=k.psum[6][:, sl:sl + 1])
        k.mark(v, "v")

    pcol_to_scr(xnmax_col, MX)
    allreduce_slots(MX, 1)
    v.tensor_scalar(out=scr[0:1, SX:SX + 1], in0=scr[0:1, MX:MX + 1],
                    scalar1=1.0 / QM, scalar2=1e-12, op0=ALU.mult, op1=ALU.max)
    v.reciprocal(out=scr[0:1, IX:IX + 1], in_=scr[0:1, SX:SX + 1])
    k.mark(v, "v")
    bcast_cols([SX, IX])

    if first:
        with ExitStack() as stW:
            wbuf = k.sb("ff_wbuf" + uid, [128, 2 * IFF], F32, stW)
            sa = k.sb("ff_sa" + uid, [128, 96], F32, stW)
            sr = k.sb("ff_sr" + uid, [128, 96], F32, stW)
            qi = k.sb("ff_qi" + uid, [128, IFF], I16, stW)
            wqb = k.sb("ff_wqb" + uid, [128, IFF], BF16, stW)
            stg = k.sb("ff_stg" + uid, [128, 2 * 512], BF16, stW)
            for wname, nt, ncols in (("wgate", NIT, D), ("wup", NIT, D),
                                     ("wdown", NDT, IFF)):
                nblk = ncols // 32
                for ot in range(nt):
                    par = ot % 2
                    wb = wbuf[:, par * IFF:par * IFF + ncols]
                    k.wait(y, "v")
                    ld = k.dma("y", wb, W[wname][:, ot * ncols:(ot + 1) * ncols],
                               sem="xd")
                    k.wait(v, "xd", ld)
                    v.tensor_reduce(out=sa[:, 0:nblk],
                                    in_=wb.rearrange("p (b c) -> p b c", b=nblk),
                                    axis=AXX, op=ALU.max,
                                    apply_absolute_value=True)
                    v.tensor_scalar(out=sa[:, 0:nblk], in0=sa[:, 0:nblk],
                                    scalar1=1.0 / 7.0, scalar2=1e-12,
                                    op0=ALU.mult, op1=ALU.max)
                    v.reciprocal(out=sr[:, 0:nblk], in_=sa[:, 0:nblk])
                    sr3 = sr[:, 0:nblk].rearrange("p (b o) -> p b o", o=1) \
                        .broadcast_to([128, nblk, 32])
                    sa3 = sa[:, 0:nblk].rearrange("p (b o) -> p b o", o=1) \
                        .broadcast_to([128, nblk, 32])
                    w3 = wb.rearrange("p (b c) -> p b c", b=nblk)
                    q3 = qi[:, 0:ncols].rearrange("p (b c) -> p b c", b=nblk)
                    wq3 = wqb[:, 0:ncols].rearrange("p (b c) -> p b c", b=nblk)
                    v.tensor_tensor(out=q3, in0=w3, in1=sr3, op=ALU.mult)
                    v.tensor_tensor(out=wq3, in0=q3, in1=sa3, op=ALU.mult)
                    vq = k.mark(v, "v")
                    k.wait(t, "v", vq)
                    nic = ncols // 128
                    for icg in range(nic // 4):
                        if ot > 0 or icg > 0:
                            k.wait(t, "s")
                        for j in range(4):
                            ic = icg * 4 + j
                            mi = t.transpose(
                                k.psum_bf[:, j * 128:(j + 1) * 128],
                                in_=wqb[:, ic * 128:(ic + 1) * 128],
                                identity=W["identb"])
                        tm = k.inc(mi, "t")
                        k.wait(s, "t", tm)
                        k.wait(s, "xd")
                        spar = icg % 2
                        s.copy(out=stg[:, spar * 512:(spar + 1) * 512],
                               in_=k.psum_bf[:, 0:512])
                        sm = k.mark(s, "s")
                        k.wait(y, "s", sm)
                        k.wait(y, "xd")
                        k.dma("y", W["wqT_" + wname][
                            :, ot * ncols + icg * 512:ot * ncols + (icg + 1) * 512],
                            stg[:, spar * 512:(spar + 1) * 512], sem="xd")
    k.bar()

    stQ = ExitStack()
    xq = k.sb("ff_xq" + uid, [128, NDT * NTOK], BF16, stQ)
    with ExitStack() as stX:
        xin = k.sb("ff_xin" + uid, [128, 2 * NDT * 512], BF16, stX)
        xi16 = k.sb("ff_xi16" + uid, [128, NDT * 512], I16, stX)
        xnh3 = xn_hbm[:, :].rearrange("p (a n) -> p a n", a=NDT)
        def ld_xq(cidx):
            par = cidx % 2
            dst = xin[:, par * NDT * 512:(par + 1) * NDT * 512].rearrange(
                "p (a c) -> p a c", a=NDT)
            sem = "xa" if par == 0 else "xc"
            return (sem, k.dma("y", dst,
                               xnh3[:, :, cidx * 512:(cidx + 1) * 512],
                               sem=sem))
        lds = [ld_xq(0), ld_xq(1)]
        for cidx in range(NCH):
            par = cidx % 2
            xb = xin[:, par * NDT * 512:(par + 1) * NDT * 512]
            k.wait(v, lds[cidx][0], lds[cidx][1])
            v.tensor_scalar(out=xi16[:, :], in0=xb, scalar1=scb[:, IX:IX + 1],
                            scalar2=None, op0=ALU.mult)
            for dt in range(NDT):
                v.tensor_scalar(
                    out=xq[:, dt * NTOK + cidx * 512:dt * NTOK + (cidx + 1) * 512],
                    in0=xi16[:, dt * 512:(dt + 1) * 512],
                    scalar1=scb[:, SX:SX + 1], scalar2=None, op0=ALU.mult)
            k.mark(v, "v")
            if cidx + 2 < NCH:
                k.wait(y, "v", k.c["v"])
                lds.append(ld_xq(cidx + 2))
    k.bar()

    gm = k.sb("ff_gm" + uid, [128, 2 * NIT * NCH], F32, stQ)
    with ExitStack() as stM:
        wl = k.sb("ff_wl" + uid, [128, 2 * NDT * 128], BF16, stM)
        ev = k.sb("ff_ev" + uid, [128, 2 * 512], BF16, stM)
        for wi, (wname, park) in enumerate((("wgate", "pk_gate"),
                                            ("wup", "pk_up"))):
            wq_h = W["wqT_" + wname]
            for et in range(NIT):
                par = et % 2
                wsl = wl[:, par * 1024:(par + 1) * 1024]
                sem = "xa" if par == 0 else "xc"
                k.wait(y, "t", k.c["t"])
                ldw = k.dma("y", wsl, wq_h[:, et * 1024:(et + 1) * 1024],
                            sem=sem)
                k.wait(t, sem, ldw)
                for c4 in range(NCH):
                    ps = k.psum[c4 % 2]
                    if et > 0 or c4 >= 2 or wi > 0:
                        k.wait(t, "v")
                    for dt in range(NDT):
                        mi = t.matmul(ps[:, :],
                                      lhsT=wsl[:, dt * 128:(dt + 1) * 128],
                                      rhs=xq[:, dt * NTOK + c4 * 512:
                                             dt * NTOK + (c4 + 1) * 512],
                                      start=(dt == 0), stop=(dt == NDT - 1))
                    tm = k.inc(mi, "t")
                    k.wait(v, "t", tm)
                    epar = c4 % 2
                    k.wait(v, "xd")
                    v.tensor_copy(out=ev[:, epar * 512:(epar + 1) * 512],
                                  in_=ps[:, :])
                    v.tensor_reduce(
                        out=gm[:, (wi * NIT + et) * NCH + c4:
                               (wi * NIT + et) * NCH + c4 + 1],
                        in_=ps[:, :], axis=AXX, op=ALU.max,
                        apply_absolute_value=True)
                    vv = k.mark(v, "v")
                    k.wait(y, "v", vv)
                    k.dma("y", W[park][:, et * NTOK + c4 * 512:
                                       et * NTOK + (c4 + 1) * 512],
                          ev[:, epar * 512:(epar + 1) * 512], sem="xd")
            k.wait(v, "v")
            v.tensor_reduce(out=gm[:, 0:1] if False else scb[:, 15:16],
                            in_=gm[:, wi * NIT * NCH:(wi + 1) * NIT * NCH],
                            axis=AXX, op=ALU.max)
            k.mark(v, "v")
            pcol_to_scr(scb[:, 15:16], MG + wi)
    stQ.close()
    allreduce_slots(MG, 2)
    v.tensor_scalar(out=scr[0:1, SG:SG + 1], in0=scr[0:1, MG:MG + 1],
                    scalar1=1.0 / QM, scalar2=1e-12, op0=ALU.mult, op1=ALU.max)
    v.reciprocal(out=scr[0:1, IG:IG + 1], in_=scr[0:1, SG:SG + 1])
    v.tensor_scalar(out=scr[0:1, SU:SU + 1], in0=scr[0:1, MU:MU + 1],
                    scalar1=1.0 / QM, scalar2=1e-12, op0=ALU.mult, op1=ALU.max)
    v.reciprocal(out=scr[0:1, IU:IU + 1], in_=scr[0:1, SU:SU + 1])
    k.mark(v, "v")
    bcast_cols([SG, IG, SU, IU])
    k.bar()

    CT = 256
    NCK = NTOK // CT
    NC6 = NIT * CT
    pkg = W["pk_gate"].rearrange("p (e n) -> p e n", e=NIT)
    pku = W["pk_up"].rearrange("p (e n) -> p e n", e=NIT)
    pkq = W["pk_gq"].rearrange("p (e n) -> p e n", e=NIT)
    pk2 = W["pk_g2"].rearrange("p (e n) -> p e n", e=NIT)
    with ExitStack() as stP:
        pin = k.sb("ff_pin" + uid, [128, 2 * NC6], BF16, stP)
        pq = k.sb("ff_pq" + uid, [128, 2 * NC6], I16, stP)
        pu16 = k.sb("ff_pu16" + uid, [128, NC6], I16, stP)
        psg = k.sb("ff_psg" + uid, [128, NC6], U16, stP)
        pt2 = k.sb("ff_pt2" + uid, [128, NC6], F32, stP)
        pm = k.sb("ff_pm" + uid, [128, NCK], F32, stP)

        def sig_t2(qb):
            vv = k.mark(v, "v")
            k.wait(s, "v", vv)
            s.activation(out=pt2[:, :], in_=qb, func=AF.Sigmoid,
                         scale=scb[:, SG:SG + 1])
            sm = k.mark(s, "s")
            k.wait(v, "s", sm)
            v.tensor_scalar(out=psg[:, :], in0=pt2[:, :], scalar1=65536.0,
                            scalar2=None, op0=ALU.mult)
            v.tensor_tensor(out=pt2[:, :], in0=qb, in1=psg[:, :], op=ALU.mult)

        def ld_p2(ck):
            par = ck % 2
            sem = "xa" if par == 0 else "xc"
            return (sem, k.dma(
                "y", pin[:, par * NC6:(par + 1) * NC6].rearrange(
                    "p (e n) -> p e n", e=NIT),
                pkg[:, :, ck * CT:(ck + 1) * CT], sem=sem))
        lds = [ld_p2(0), ld_p2(1)]
        for ck in range(NCK):
            par = ck % 2
            pb = pin[:, par * NC6:(par + 1) * NC6]
            qb = pq[:, par * NC6:(par + 1) * NC6]
            k.wait(v, lds[ck][0], lds[ck][1])
            k.wait(v, "xd")
            v.tensor_scalar(out=qb, in0=pb, scalar1=scb[:, IG:IG + 1],
                            scalar2=None, op0=ALU.mult)
            sig_t2(qb)
            v.tensor_reduce(out=pm[:, ck:ck + 1], in_=pt2[:, :], axis=AXX,
                            op=ALU.max, apply_absolute_value=True)
            vv = k.mark(v, "v")
            k.wait(y, "v", vv)
            k.dma("y", pkq[:, :, ck * CT:(ck + 1) * CT],
                  qb.rearrange("p (e n) -> p e n", e=NIT), sem="xd")
            if ck + 2 < NCK:
                lds.append(ld_p2(ck + 2))
        k.wait(v, "v")
        v.tensor_reduce(out=scb[:, 15:16], in_=pm[:, :], axis=AXX, op=ALU.max)
        k.mark(v, "v")
        pcol_to_scr(scb[:, 15:16], M2)
        allreduce_slots(M2, 1)
        v.tensor_tensor(out=scr[0:1, 16:17], in0=scr[0:1, M2:M2 + 1],
                        in1=scr[0:1, SG:SG + 1], op=ALU.mult)
        v.tensor_scalar(out=scr[0:1, 17:18], in0=scr[0:1, 16:17],
                        scalar1=1.0 / (65536.0 * QM), scalar2=1e-12,
                        op0=ALU.mult, op1=ALU.max)
        v.reciprocal(out=scr[0:1, 18:19], in_=scr[0:1, 17:18])
        v.tensor_tensor(out=scr[0:1, C2:C2 + 1], in0=scr[0:1, SG:SG + 1],
                        in1=scr[0:1, 18:19], op=ALU.mult)
        v.tensor_scalar(out=scr[0:1, C2:C2 + 1], in0=scr[0:1, C2:C2 + 1],
                        scalar1=1.0 / 65536.0, scalar2=None, op0=ALU.mult)
        k.mark(v, "v")
        bcast_cols([C2])
        k.bar()

        def ld_p3(ck):
            par = ck % 2
            sem = "xa" if par == 0 else "xc"
            k.dma("y", pq[:, par * NC6:(par + 1) * NC6].rearrange(
                "p (e n) -> p e n", e=NIT),
                pkq[:, :, ck * CT:(ck + 1) * CT], sem=sem)
            return (sem, k.dma(
                "y", pin[:, par * NC6:(par + 1) * NC6].rearrange(
                    "p (e n) -> p e n", e=NIT),
                pku[:, :, ck * CT:(ck + 1) * CT], sem=sem))
        lds = [ld_p3(0), ld_p3(1)]
        for ck in range(NCK):
            par = ck % 2
            qb = pq[:, par * NC6:(par + 1) * NC6]
            ub = pin[:, par * NC6:(par + 1) * NC6]
            k.wait(v, lds[ck][0], lds[ck][1])
            k.wait(v, "xd")
            sig_t2(qb)
            v.tensor_scalar(out=qb, in0=pt2[:, :],
                            scalar1=scb[:, C2:C2 + 1], scalar2=None,
                            op0=ALU.mult)
            v.tensor_scalar(out=pu16[:, :], in0=ub, scalar1=scb[:, IU:IU + 1],
                            scalar2=None, op0=ALU.mult)
            v.tensor_tensor(out=pt2[:, :], in0=qb, in1=pu16[:, :],
                            op=ALU.mult)
            v.tensor_reduce(out=pm[:, ck:ck + 1], in_=pt2[:, :], axis=AXX,
                            op=ALU.max, apply_absolute_value=True)
            vv = k.mark(v, "v")
            k.wait(y, "v", vv)
            k.dma("y", pk2[:, :, ck * CT:(ck + 1) * CT],
                  qb.rearrange("p (e n) -> p e n", e=NIT), sem="xd")
            if ck + 2 < NCK:
                k.wait(y, "xd", k.c["xd"])
                lds.append(ld_p3(ck + 2))
        k.wait(v, "v")
        v.tensor_reduce(out=scb[:, 15:16], in_=pm[:, :], axis=AXX, op=ALU.max)
        k.mark(v, "v")
        pcol_to_scr(scb[:, 15:16], M3)
        allreduce_slots(M3, 1)
        v.tensor_tensor(out=scr[0:1, 19:20], in0=scr[0:1, 17:18],
                        in1=scr[0:1, SU:SU + 1], op=ALU.mult)
        v.tensor_tensor(out=scr[0:1, 16:17], in0=scr[0:1, M3:M3 + 1],
                        in1=scr[0:1, 19:20], op=ALU.mult)
        v.tensor_scalar(out=scr[0:1, S3:S3 + 1], in0=scr[0:1, 16:17],
                        scalar1=1.0 / QM, scalar2=1e-12,
                        op0=ALU.mult, op1=ALU.max)
        v.reciprocal(out=scr[0:1, 20:21], in_=scr[0:1, S3:S3 + 1])
        v.tensor_tensor(out=scr[0:1, C3:C3 + 1], in0=scr[0:1, 19:20],
                        in1=scr[0:1, 20:21], op=ALU.mult)
        k.mark(v, "v")
        bcast_cols([C3, S3])
        k.bar()

        obuf = k.sb("ff_ob" + uid, [128, 2 * NC6], BF16, stP)
        wdl = k.sb("ff_wdl" + uid, [128, 2 * NIT * 128], BF16, stP)
        wdq = W["wqT_wdown"]
        def ld_p4(ck):
            par = ck % 2
            sem = "xa" if par == 0 else "xc"
            k.dma("y", pq[:, par * NC6:(par + 1) * NC6].rearrange(
                "p (e n) -> p e n", e=NIT),
                pk2[:, :, ck * CT:(ck + 1) * CT], sem=sem)
            return (sem, k.dma(
                "y", pin[:, par * NC6:(par + 1) * NC6].rearrange(
                    "p (e n) -> p e n", e=NIT),
                pku[:, :, ck * CT:(ck + 1) * CT], sem=sem))
        lds = [ld_p4(0), ld_p4(1)]
        for ck in range(NCK):
            par = ck % 2
            g2 = pq[:, par * NC6:(par + 1) * NC6]
            ub = pin[:, par * NC6:(par + 1) * NC6]
            ob = obuf[:, par * NC6:(par + 1) * NC6]
            k.wait(v, lds[ck][0], lds[ck][1])
            k.wait(v, "t")
            v.tensor_scalar(out=pu16[:, :], in0=ub, scalar1=scb[:, IU:IU + 1],
                            scalar2=None, op0=ALU.mult)
            v.tensor_tensor(out=pt2[:, :], in0=g2, in1=pu16[:, :], op=ALU.mult)
            v.tensor_scalar(out=pu16[:, :], in0=pt2[:, :],
                            scalar1=scb[:, C3:C3 + 1], scalar2=None,
                            op0=ALU.mult)
            v.tensor_scalar(out=ob, in0=pu16[:, :], scalar1=scb[:, S3:S3 + 1],
                            scalar2=None, op0=ALU.mult)
            vv = k.mark(v, "v")
            if ck + 2 < NCK:
                k.wait(y, "v", vv)
                lds.append(ld_p4(ck + 2))
            for et in range(NDT):
                wpar = et % 2
                wsem = "xe" if wpar == 0 else "xf"
                wsl = wdl[:, wpar * NIT * 128:(wpar + 1) * NIT * 128]
                if ck > 0 or et >= 2:
                    k.wait(y, "t")
                ldw = k.dma("y", wsl, wdq[:, et * IFF:(et + 1) * IFF],
                            sem=wsem)
                k.wait(t, wsem, ldw)
                k.wait(t, "v", vv)
                ps = k.psum[4 + et % 2]
                if ck > 0 or et >= 2:
                    k.wait(t, "v")
                for ic in range(NIT):
                    mi = t.matmul(ps[:, 0:CT],
                                  lhsT=wsl[:, ic * 128:(ic + 1) * 128],
                                  rhs=ob[:, ic * CT:(ic + 1) * CT],
                                  start=(ic == 0), stop=(ic == NIT - 1))
                tm = k.inc(mi, "t")
                k.wait(v, "t", tm)
                rsl = R[:, et * NTOK + ck * CT:et * NTOK + (ck + 1) * CT]
                v.scalar_tensor_tensor(out=rsl, in0=ps[:, 0:CT], scalar=0.5,
                                       in1=rsl, op0=ALU.mult, op1=ALU.add)
                k.mark(v, "v")
    if dbg is not None:
        k.wait(v, "v")
        v.tensor_copy(out=dbg[0:1, :], in_=scr[0:1, :])
        k.mark(v, "v")
    k.bar()


DIN = 2048
NET = DIN // 128
LCH = 64
NBT = 16


def emit_l1_attn(k: K, R, xn_hbm, W, st, uid=""):
    t, v, s, g, y = k.t, k.v, k.s, k.g, k.y
    xnh3 = xn_hbm[:, :].rearrange("p (a n) -> p a n", a=NDT)
    xih = W["xi_hbm"].rearrange("p (a n) -> p a n", a=NET)
    zsh = W["zs_hbm"].rearrange("p (a n) -> p a n", a=NET)
    slh = W["sl_hbm"].rearrange("p (a n) -> p a n", a=NET)
    xph = W["xp_hbm"]
    g1h = W["g1_hbm"]

    Bin = k.sb("la_bin" + uid, [128, DIN], F32, st)
    Cin = k.sb("la_cin" + uid, [128, DIN], F32, st)
    abp = k.sb("la_abp" + uid, [128, 5 * LCH], F32, st)
    abc = k.sb("la_abc" + uid, [128, 8], F32, st)
    abrep = k.sb("la_abrep" + uid, [128, 32], F32, st)
    dcol = k.sb("la_dcol" + uid, [128, 16], F32, st)
    thc = k.sb("la_thc" + uid, [128, B * 8 * 16], F32, st)
    aec = k.sb("la_aec" + uid, [128, B * 8 * 16], F32, st)
    tri = k.sb("la_tri" + uid, [128, 128], BF16, st)
    kmask = k.sb("la_kmask" + uid, [128, LCH], BF16, st)
    taus = k.sb("la_tau" + uid, [128, 128], F32, st)
    ab1 = k.sb("la_ab1" + uid, [128, 1], F32, st)
    ones1f = k.sb("la_o1f" + uid, [1, 128], F32, st)
    prow = k.sb("la_prow" + uid, [1, 32], F32, st)
    v.memset(ones1f[:, :], 1.0)

    k.dma("g", tri[:, :], W["tri"][:, :])
    k.dma("g", kmask[:, :], W["kmask"][:, :])
    k.dma("y", taus[:, :], W["tau"][:, :])
    k.dma("y", ab1[:, :], W["abase"][:, :])
    k.dma("y", dcol[:, :], W["dcolh"][:, :])
    k.bar()

    s.activation(out=abc[:, 0:1], in_=ab1[:, 0:1], func=AF.Sigmoid)
    sm = k.mark(s, "s")
    k.wait(s, "s", sm)
    s.activation(out=abc[:, 1:2], in_=abc[:, 0:1], func=AF.Ln)
    sm = k.mark(s, "s")
    k.wait(s, "s", sm)
    s.activation(out=abc[:, 2:3], in_=abc[:, 1:2], func=AF.Exp, scale=64.0)
    s.activation(out=abc[:, 3:4], in_=abc[:, 1:2], func=AF.Exp, scale=256.0)
    sm = k.mark(s, "s")
    k.wait(v, "s", sm)
    v.tensor_scalar(out=abp[:, 4 * LCH:5 * LCH], in0=taus[:, 0:LCH],
                    scalar1=abc[:, 1:2], scalar2=None, op0=ALU.mult)
    vv = k.mark(v, "v")
    k.wait(s, "v", vv)
    s.activation(out=abp[:, 0:LCH], in_=abp[:, 4 * LCH:5 * LCH], func=AF.Exp,
                 scale=-1.0)
    s.activation(out=abp[:, LCH:2 * LCH], in_=abp[:, 4 * LCH:5 * LCH],
                 func=AF.Exp)
    sm = k.mark(s, "s")
    k.wait(v, "s", sm)
    v.tensor_scalar(out=abp[:, 2 * LCH:3 * LCH], in0=abp[:, LCH:2 * LCH],
                    scalar1=abc[:, 0:1], scalar2=None, op0=ALU.mult)
    v.tensor_scalar(out=abp[:, 4 * LCH:5 * LCH], in0=taus[:, LCH:2 * LCH],
                    scalar1=abc[:, 1:2], scalar2=None, op0=ALU.mult)
    vv = k.mark(v, "v")
    k.wait(s, "v", vv)
    s.activation(out=abp[:, 3 * LCH:4 * LCH], in_=abp[:, 4 * LCH:5 * LCH],
                 func=AF.Exp)
    k.mark(s, "s")
    k.wait(y, "s")
    wr = k.dma("y", W["colb"][0:1, 0:32], abc[0:16, 2:4], sem="xd")
    k.wait(y, "xd", wr)
    rd = k.dma("y", prow[0:1, :], W["colb"][0:1, 0:32], sem="xd")
    k.wait(t, "xd", rd)
    mi = t.matmul(k.psum[6][:, 0:16], lhsT=ones1f[0:1, :],
                  rhs=prow[0:1, 0:32].rearrange("o (p c) -> o (c p)", p=16)[
                      :, 0:16] if False else
                  prow[0:1, 0:32].rearrange("o (p c) -> o p c", p=16)[:, :, 0],
                  start=True, stop=True)
    mi = t.matmul(k.psum[6][:, 16:32], lhsT=ones1f[0:1, :],
                  rhs=prow[0:1, 0:32].rearrange("o (p c) -> o p c", p=16)[:, :, 1],
                  start=False, stop=True, skip_group_check=True)
    tm = k.inc(mi, "t")
    k.wait(v, "t", tm)
    v.tensor_copy(out=abrep[:, :], in_=k.psum[6][:, 0:32])
    k.mark(v, "v")
    k.bar()

    with ExitStack() as stP:
        wbuf = k.sb("la_wb" + uid, [128, NDT * DIN], BF16, stP)
        xnc = k.sb("la_xnc" + uid, [128, 2 * NDT * 512], BF16, stP)
        ev = k.sb("la_ev" + uid, [128, 2 * 512], BF16, stP)
        for half in range(2):
            k.wait(g, "t", k.c["t"])
            hw = k.dma("g", wbuf[:, :],
                       W["ipw"].rearrange("p (a n) -> p a n", a=NDT)[
                           :, :, half * DIN:(half + 1) * DIN], sem="xf")
            k.wait(t, "xf", hw)
            for c4 in range(NCH):
                par = c4 % 2
                xb = xnc[:, par * NDT * 512:(par + 1) * NDT * 512]
                sem = "xa" if par == 0 else "xc"
                k.wait(y, "t", k.c["t"])
                ld = k.dma("y", xb.rearrange("p (a c) -> p a c", a=NDT),
                           xnh3[:, :, c4 * 512:(c4 + 1) * 512], sem=sem)
                k.wait(t, sem, ld)
                for et in range(NET):
                    ps = k.psum[et % 2]
                    if et >= 2 or c4 >= 1 or half > 0:
                        k.wait(t, "s")
                    for dt in range(NDT):
                        mi = t.matmul(
                            ps[:, :],
                            lhsT=wbuf[:, dt * DIN + half * 0 + et * 128:
                                      dt * DIN + (et + 1) * 128],
                            rhs=xb[:, dt * 512:(dt + 1) * 512],
                            start=(dt == 0), stop=(dt == NDT - 1))
                    tm = k.inc(mi, "t")
                    k.wait(s, "t", tm)
                    epar = et % 2
                    k.wait(s, "xd")
                    s.activation(out=ev[:, epar * 512:(epar + 1) * 512],
                                 in_=ps[:, :], func=AF.Silu)
                    sm = k.mark(s, "s")
                    k.wait(y, "s", sm)
                    dsth = xih if half == 0 else zsh
                    k.dma("y", dsth[:, half * 0 + et, c4 * 512:(c4 + 1) * 512],
                          ev[:, epar * 512:(epar + 1) * 512], sem="xd")
    k.bar()

    with ExitStack() as stA:
        wbuf = k.sb("la_wb2" + uid, [128, NET * 1024], BF16, stA)
        bwc = k.sb("la_bwc" + uid, [128, 2 * NET * 16], BF16, stA)
        xic = k.sb("la_xic" + uid, [128, 2 * NET * 256], BF16, stA)
        ev = k.sb("la_ev2" + uid, [128, 2 * 512], BF16, stA)
        acol = k.sb("la_acol" + uid, [128, 16], F32, stA)
        k.dma("y", acol[:, :], W["acolh"][:, :])
        k.dma("g", bwc[:, 0:NET * 16], W["bw"][:, :])
        k.dma("g", bwc[:, NET * 16:2 * NET * 16], W["cw"][:, :])
        k.wait(t, "gd")
        k.wait(s, "yd")
        for half in range(2):
            k.wait(g, "t", k.c["t"])
            hw = k.dma("g", wbuf[:, :],
                       W["apw"].rearrange("p (a n) -> p a n", a=NET)[
                           :, :, half * 1024:(half + 1) * 1024], sem="xf")
            k.wait(t, "xf", hw)
            for c8 in range(8):
                par = c8 % 2
                xb = xic[:, par * NET * 256:(par + 1) * NET * 256]
                sem = "xa" if par == 0 else "xc"
                k.wait(y, "t", k.c["t"])
                ld = k.dma("y", xb.rearrange("p (a c) -> p a c", a=NET),
                           xih[:, :, c8 * 256:(c8 + 1) * 256], sem=sem)
                k.wait(t, sem, ld)
                for eh in range(8):
                    et = half * 8 + eh
                    ps = k.psum[eh % 2]
                    if eh >= 2 or c8 >= 1 or half > 0:
                        k.wait(t, "s")
                    for dt in range(NET):
                        mi = t.matmul(
                            ps[:, 0:256],
                            lhsT=wbuf[:, dt * 1024 + eh * 128:
                                      dt * 1024 + (eh + 1) * 128],
                            rhs=xb[:, dt * 256:(dt + 1) * 256],
                            start=(dt == 0), stop=(dt == NET - 1))
                    tm = k.inc(mi, "t")
                    k.wait(s, "t", tm)
                    epar = eh % 2
                    k.wait(s, "xd")
                    s.activation(out=ev[:, epar * 512:epar * 512 + 256],
                                 in_=ps[:, 0:256], func=AF.Sigmoid,
                                 bias=acol[:, et:et + 1])
                    s.activation(out=ev[:, epar * 512:epar * 512 + 256],
                                 in_=ev[:, epar * 512:epar * 512 + 256],
                                 func=AF.Ln)
                    sm = k.mark(s, "s")
                    k.wait(y, "s", sm)
                    k.dma("y", slh[:, et, c8 * 256:(c8 + 1) * 256],
                          ev[:, epar * 512:epar * 512 + 256], sem="xd")
                if half == 0:
                    k.wait(t, "v")
                    for which, bank in ((0, 2), (1, 3)):
                        for dt in range(NET):
                            mi = t.matmul(
                                k.psum[bank][0:16, 0:256],
                                lhsT=bwc[:, which * NET * 16 + dt * 16:
                                         which * NET * 16 + (dt + 1) * 16],
                                rhs=xb[:, dt * 256:(dt + 1) * 256],
                                start=(dt == 0), stop=(dt == NET - 1))
                        tm = k.inc(mi, "t")
                        k.wait(v, "t", tm)
                        dst = Bin if which == 0 else Cin
                        v.tensor_copy(out=dst[0:16, c8 * 256:(c8 + 1) * 256],
                                      in_=k.psum[bank][0:16, 0:256])
                        k.mark(v, "v")
        k.wait(y, "v", k.c["v"])
        k.dma("y", Bin[64:80, :], Bin[0:16, :])
        k.dma("y", Cin[64:80, :], Cin[0:16, :])
    k.bar()

    NCHB = 4
    hincD = k.sb("la_hincD" + uid, [128, B * NCHB * 256], BF16, st)
    smry = k.sb("la_smry" + uid, [128, B * 272], F32, st)
    hin = k.sb("la_hin" + uid, [128, B * 256], F32, st)
    with ExitStack() as stS:
        slb = k.sb("la_slb" + uid, [128, 2 * DIN], BF16, stS)
        xib = k.sb("la_xib" + uid, [128, 2 * DIN], BF16, stS)
        sltT = k.sb("la_sltT" + uid, [128, DIN], BF16, stS)
        xinT = k.sb("la_xinT" + uid, [128, DIN], BF16, stS)
        exb = k.sb("la_exb" + uid, [128, 2 * DIN], BF16, stS)
        xpr = k.sb("la_xpr" + uid, [128, DIN], BF16, stS)
        erb = k.sb("la_erb" + uid, [128, 512], BF16, stS)
        bb2 = k.sb("la_bb2" + uid, [128, LCH], BF16, stS)
        lhsb = k.sb("la_lhsb" + uid, [128, 32], BF16, stS)
        hl = k.sb("la_hl" + uid, [128, 256], F32, stS)
        ecol = k.sb("la_ecol" + uid, [128, 64], F32, stS)
        stg = k.sb("la_stg" + uid, [128, 512], BF16, stS)
        v.memset(lhsb[:, :], 0.0)
        k.mark(v, "v")
        for bb in range(B):
            v.memset(hl[:, :], 0.0)
            k.mark(v, "v")
            for tt in range(2):
                bt = bb * 2 + tt
                par = bt % 2
                sem = "xa" if par == 0 else "xc"
                sem2 = "xb" if par == 0 else "xd"
                sb_ = slb[:, par * DIN:(par + 1) * DIN]
                xb_ = xib[:, par * DIN:(par + 1) * DIN]
                k.wait(y, "v", k.c["v"])
                k.wait(y, "t", k.c["t"])
                ld = k.dma("y", sb_.rearrange("p (a c) -> p a c", a=NET),
                           slh[:, :, bb * 256 + tt * 128:
                               bb * 256 + tt * 128 + 128], sem=sem)
                ld2 = k.dma("y", xb_.rearrange("p (a c) -> p a c", a=NET),
                            xih[:, :, bb * 256 + tt * 128:
                                bb * 256 + tt * 128 + 128], sem=sem2)
                k.wait(s, "xe")
                k.wait(v, "xe")
                k.wait(v, sem, ld)
                for ch01 in range(2):
                    c = tt * 2 + ch01
                    base = (bb * NCHB + c) * 16
                    v.tensor_reduce(
                        out=thc[:, base:base + 16],
                        in_=sb_.rearrange("p (a c) -> p a c", a=NET)[
                            :, :, ch01 * 64:ch01 * 64 + 32],
                        axis=AXX, op=ALU.add)
                    v.tensor_reduce(
                        out=aec[:, base:base + 16],
                        in_=sb_.rearrange("p (a c) -> p a c", a=NET)[
                            :, :, ch01 * 64:(ch01 + 1) * 64],
                        axis=AXX, op=ALU.add)
                vv = k.mark(v, "v")
                k.wait(t, "v", vv)
                k.wait(t, sem2, ld2)
                for src_t, dst_t in ((sb_, sltT), (xb_, xinT)):
                    for grp in range(4):
                        k.wait(t, "v")
                        for j in range(4):
                            et = grp * 4 + j
                            mi = t.transpose(
                                k.psum_bf[:, j * 128:(j + 1) * 128],
                                in_=src_t[:, et * 128:(et + 1) * 128],
                                identity=W["identb"])
                        tm = k.inc(mi, "t")
                        k.wait(v, "t", tm)
                        v.tensor_copy(out=dst_t[:, grp * 512:(grp + 1) * 512],
                                      in_=k.psum_bf[:, 0:512])
                        k.mark(v, "v")
                k.wait(t, "v")
                for ch01 in range(2):
                    off = ch01 * 64
                    for nch in range(4):
                        ps = k.psum[2 + ch01]
                        ps_s = k.psum[4 + ch01]
                        k.wait(t, "s")
                        rh = sltT[off:off + 64, nch * 512:(nch + 1) * 512]
                        t.matmul(ps[off:off + 64, :],
                                 lhsT=tri[off:off + 64, 0:64], rhs=rh,
                                 start=True, stop=True)
                        mi = t.matmul(ps_s[off:off + 64, :],
                                      lhsT=tri[off:off + 64, 64:128], rhs=rh,
                                      start=True, stop=True)
                        tm = k.inc(mi, "t")
                        k.wait(s, "t", tm)
                        k.wait(s, "v")
                        nsl = slice(nch * 512, (nch + 1) * 512)
                        s.activation(out=exb[off:off + 64, nsl],
                                     in_=ps[off:off + 64, :], func=AF.Exp,
                                     scale=-1.0)
                        s.activation(out=exb[off:off + 64, DIN + nch * 512:
                                             DIN + (nch + 1) * 512],
                                     in_=ps[off:off + 64, :], func=AF.Exp)
                        s.activation(out=erb[off:off + 64, 0:512],
                                     in_=ps_s[off:off + 64, :],
                                     func=AF.Exp)
                        sm = k.mark(s, "s")
                        k.wait(v, "s", sm)
                        v.tensor_tensor(out=xpr[off:off + 64, nsl],
                                        in0=xinT[off:off + 64, nsl],
                                        in1=erb[off:off + 64, 0:512],
                                        op=ALU.mult)
                        k.mark(v, "v")
                k.wait(v, "v")
                for q in range(4):
                    qs = slice(q * 512, (q + 1) * 512)
                    v.tensor_tensor(out=exb[:, qs], in0=xinT[:, qs],
                                    in1=exb[:, qs], op=ALU.mult)
                vv = k.mark(v, "v")
                k.wait(y, "v", vv)
                k.dma("y", xph[:, bt * DIN:(bt + 1) * DIN], exb[:, 0:DIN],
                      sem="xe")
                k.dma("y", g1h[:, bt * DIN:(bt + 1) * DIN],
                      exb[:, DIN:2 * DIN], sem="xe")
                for ch01 in range(2):
                    off = ch01 * 64
                    k.wait(v, "t")
                    v.tensor_tensor(
                        out=bb2[off:off + 16, :],
                        in0=Bin[off:off + 16,
                                bb * 256 + tt * 128 + ch01 * 64:
                                bb * 256 + tt * 128 + (ch01 + 1) * 64],
                        in1=abp[off:off + 16, 3 * LCH:4 * LCH], op=ALU.mult)
                    vv = k.mark(v, "v")
                    k.wait(t, "v", vv)
                    mi = t.transpose(k.psum_bf[off:off + 64, 768:784],
                                     in_=bb2[off:off + 16, :],
                                     identity=W["identb"][off:off + 16,
                                                          off:off + 16])
                    tm = k.inc(mi, "t")
                    k.wait(v, "t", tm)
                    v.tensor_copy(out=lhsb[off:off + 64,
                                           ch01 * 16:(ch01 + 1) * 16],
                                  in_=k.psum_bf[off:off + 64, 768:784])
                    k.mark(v, "v")
                vv = k.c["v"]
                k.wait(t, "v", vv)
                for nch in range(4):
                    ps6 = k.psum[6]
                    k.wait(t, "s")
                    mi = t.matmul(ps6[0:32, 0:512], lhsT=lhsb[:, :],
                                  rhs=xpr[:, nch * 512:(nch + 1) * 512],
                                  start=True, stop=True)
                    tm = k.inc(mi, "t")
                    k.wait(s, "t", tm)
                    s.copy(out=stg[0:32, 0:512], in_=ps6[0:32, 0:512])
                    sm = k.mark(s, "s")
                    k.wait(t, "s", sm)
                    k.wait(t, "v")
                    for j in range(4):
                        mi = t.transpose(k.psum_bf[:, 768 + j * 32:
                                                   768 + (j + 1) * 32],
                                         in_=stg[0:32, j * 128:(j + 1) * 128],
                                         identity=W["identb"][0:32, 0:32])
                    tm = k.inc(mi, "t")
                    k.wait(v, "t", tm)
                    for j in range(4):
                        et = nch * 4 + j
                        for ch01 in range(2):
                            c = tt * 2 + ch01
                            dst = hincD[:, ((bb * NCHB + c) * 16 + et) * 16:
                                        ((bb * NCHB + c) * 16 + et) * 16 + 16]
                            v.tensor_copy(
                                out=dst,
                                in_=k.psum_bf[:, 768 + j * 32 + ch01 * 16:
                                              768 + j * 32 + (ch01 + 1) * 16])
                    k.mark(v, "v")
                for ch01 in range(2):
                    c = tt * 2 + ch01
                    base = (bb * NCHB + c) * 16
                    k.wait(s, "v")
                    s.activation(out=ecol[:, 0:16], in_=aec[:, base:base + 16],
                                 func=AF.Exp)
                    sm = k.mark(s, "s")
                    k.wait(v, "s", sm)
                    v.tensor_tensor(
                        out=hl[:, :].rearrange("p (e n) -> p e n", e=16),
                        in0=hl[:, :].rearrange("p (e n) -> p e n", e=16),
                        in1=ecol[:, 0:16].rearrange("p (e o) -> p e o", o=1)
                        .broadcast_to([128, 16, 16]), op=ALU.mult)
                    v.tensor_tensor(
                        out=hl[:, :].rearrange("p (e n) -> p e n", e=16),
                        in0=hl[:, :].rearrange("p (e n) -> p e n", e=16),
                        in1=abrep[:, 0:16].rearrange("p (o n) -> p o n", o=1)
                        .broadcast_to([128, 16, 16]), op=ALU.mult)
                    v.tensor_tensor(out=hl[:, :], in0=hl[:, :],
                                    in1=hincD[:, base * 16:base * 16 + 256],
                                    op=ALU.add)
                    k.mark(v, "v")
            base0 = (bb * NCHB) * 16
            k.wait(v, "v")
            v.tensor_tensor(out=ecol[:, 16:32], in0=aec[:, base0:base0 + 16],
                            in1=aec[:, base0 + 16:base0 + 32], op=ALU.add)
            v.tensor_tensor(out=ecol[:, 32:48],
                            in0=aec[:, base0 + 32:base0 + 48],
                            in1=aec[:, base0 + 48:base0 + 64], op=ALU.add)
            v.tensor_tensor(out=ecol[:, 16:32], in0=ecol[:, 16:32],
                            in1=ecol[:, 32:48], op=ALU.add)
            vv = k.mark(v, "v")
            k.wait(s, "v", vv)
            s.activation(out=smry[:, bb * 272:bb * 272 + 16],
                         in_=ecol[:, 16:32], func=AF.Exp)
            k.mark(s, "s")
            k.wait(v, "s")
            v.tensor_copy(out=smry[:, bb * 272 + 16:bb * 272 + 272],
                          in_=hl[:, :])
            k.mark(v, "v")
    k.bar()

    foldm = k.sb("la_foldm" + uid, [128, 16], F32, st)
    smb = k.sb("la_smb" + uid, [128, B * 272], BF16, st)
    k.dma("y", foldm[:, :], W["foldmh"][:, :])
    k.wait(v, "yd")
    for bb in range(B):
        v.tensor_copy(out=smb[:, bb * 272:(bb + 1) * 272],
                      in_=smry[:, bb * 272:(bb + 1) * 272])
    vv = k.mark(v, "v")
    k.wait(g, "v", vv)
    k.dma("g", W["cc3_in"][:, :], smb[:, :])
    g.wait_ge(k.sems["gd"], k.c["gd"])
    ci = g.collective_compute(
        "AllGather", ALU.bypass, replica_groups=[list(range(NCORES))],
        ins=[W["cc3_in"].opt()], outs=[W["cc3_out"].opt()])
    k.inc(ci, "cc")
    k.wait(y, "cc", k.c["cc"])
    with ExitStack() as stF:
        rsm = k.sb("la_rsm" + uid, [128, B * 272], BF16, stF)
        dmt = k.sb("la_dmt" + uid, [128, 256], F32, stF)
        abn = k.sb("la_abn" + uid, [128, 256], F32, stF)
        v.memset(hin[:, :], 0.0)
        for e in range(16):
            v.tensor_copy(out=abn[:, e * 16:(e + 1) * 16],
                          in_=abrep[:, 16:32])
        k.mark(v, "v")
        for r in range(NCORES - 1):
            k.wait(y, "v", k.c["v"])
            ldr = k.dma("y", rsm[:, :],
                        W["cc3_out"][128 * r:128 * (r + 1), :], sem="xa")
            k.wait(v, "xa", ldr)
            for bb in range(B):
                ap_ = rsm[:, bb * 272:bb * 272 + 16]
                hf_ = rsm[:, bb * 272 + 16:bb * 272 + 272]
                hb = hin[:, bb * 256:(bb + 1) * 256]
                v.tensor_tensor(
                    out=dmt[:, :].rearrange("p (e n) -> p e n", e=16),
                    in0=abn[:, :].rearrange("p (e n) -> p e n", e=16),
                    in1=ap_.rearrange("p (e o) -> p e o", o=1)
                    .broadcast_to([128, 16, 16]), op=ALU.mult)
                v.tensor_scalar(out=dmt[:, :], in0=dmt[:, :],
                                scalar1=foldm[:, r:r + 1],
                                scalar2=foldm[:, 8 + r:9 + r],
                                op0=ALU.mult, op1=ALU.add)
                v.tensor_tensor(out=hb, in0=hb, in1=dmt[:, :], op=ALU.mult)
                v.scalar_tensor_tensor(out=hb, in0=hf_,
                                       scalar=foldm[:, r:r + 1], in1=hb,
                                       op0=ALU.mult, op1=ALU.add)
            k.mark(v, "v")
    k.bar()

    with ExitStack() as stB:
        opw = k.sb("la_opw" + uid, [128, NET * 1024], BF16, stB)
        xpb = k.sb("la_xpb" + uid, [128, 2 * DIN], BF16, stB)
        g1b = k.sb("la_g1b" + uid, [128, 2 * DIN], BF16, stB)
        bcc = k.sb("la_bcc" + uid, [128, 2 * LCH], BF16, stB)
        c3b = k.sb("la_c3b" + uid, [128, LCH], BF16, stB)
        ktb = k.sb("la_ktb" + uid, [128, LCH], BF16, stB)
        httr = k.sb("la_httr" + uid, [128, DIN], BF16, stB)
        ecol2 = k.sb("la_ecol2" + uid, [128, 48], F32, stB)
        htl = k.sb("la_htl" + uid, [128, 256], BF16, stB)
        yrow = k.sb("la_yrow" + uid, [128, DIN], BF16, stB)
        yT = k.sb("la_yT" + uid, [128, 2 * DIN], BF16, stB)
        zsb = k.sb("la_zsb" + uid, [128, DIN], BF16, stB)
        xrb = k.sb("la_xrb" + uid, [128, DIN], BF16, stB)
        ht = k.sb("la_ht" + uid, [128, 256], F32, stB)
        k.dma("g", opw[:, :], W["opw"][:, :], sem="gd")
        for bb in range(B):
            k.wait(v, "v")
            k.wait(v, "t")
            v.tensor_copy(out=ht[:, :], in_=hin[:, bb * 256:(bb + 1) * 256])
            k.mark(v, "v")
            for tt in range(2):
                bt = bb * 2 + tt
                par = tt % 2
                xp_ = xpb[:, par * DIN:(par + 1) * DIN]
                g1_ = g1b[:, par * DIN:(par + 1) * DIN]
                k.wait(y, "t", k.c["t"])
                k.wait(y, "v", k.c["v"])
                k.dma("y", xp_, xph[:, bt * DIN:(bt + 1) * DIN], sem="xa")
                l2 = k.dma("y", g1_, g1h[:, bt * DIN:(bt + 1) * DIN], sem="xa")
                for ch01 in range(2):
                    off = ch01 * 64
                    c = tt * 2 + ch01
                    base = (bb * NCHB + c) * 16
                    tksl = slice(bb * 256 + tt * 128 + ch01 * 64,
                                 bb * 256 + tt * 128 + (ch01 + 1) * 64)
                    k.wait(v, "t")
                    v.tensor_tensor(out=bcc[off:off + 16, 0:LCH],
                                    in0=Bin[off:off + 16, tksl],
                                    in1=abp[off:off + 16, 0:LCH], op=ALU.mult)
                    v.tensor_tensor(out=bcc[off:off + 16, LCH:2 * LCH],
                                    in0=Cin[off:off + 16, tksl],
                                    in1=abp[off:off + 16, LCH:2 * LCH],
                                    op=ALU.mult)
                    v.tensor_tensor(out=c3b[off:off + 16, :],
                                    in0=Cin[off:off + 16, tksl],
                                    in1=abp[off:off + 16, 2 * LCH:3 * LCH],
                                    op=ALU.mult)
                    vv = k.mark(v, "v")
                    k.wait(s, "v", vv)
                    s.activation(out=ecol2[:, 0:16],
                                 in_=thc[:, base:base + 16], func=AF.Exp)
                    s.activation(out=ecol2[:, 16:32],
                                 in_=aec[:, base:base + 16], func=AF.Exp)
                    sm = k.mark(s, "s")
                    k.wait(v, "s", sm)
                    v.tensor_tensor(
                        out=htl[:, :].rearrange("p (e n) -> p e n", e=16),
                        in0=ht[:, :].rearrange("p (e n) -> p e n", e=16),
                        in1=ecol2[:, 0:16].rearrange("p (e o) -> p e o", o=1)
                        .broadcast_to([128, 16, 16]), op=ALU.mult)
                    v.tensor_tensor(
                        out=ht[:, :].rearrange("p (e n) -> p e n", e=16),
                        in0=ht[:, :].rearrange("p (e n) -> p e n", e=16),
                        in1=ecol2[:, 16:32].rearrange("p (e o) -> p e o", o=1)
                        .broadcast_to([128, 16, 16]), op=ALU.mult)
                    v.tensor_tensor(
                        out=ht[:, :].rearrange("p (e n) -> p e n", e=16),
                        in0=ht[:, :].rearrange("p (e n) -> p e n", e=16),
                        in1=abrep[:, 0:16].rearrange("p (o n) -> p o n", o=1)
                        .broadcast_to([128, 16, 16]), op=ALU.mult)
                    v.tensor_tensor(out=ht[:, :], in0=ht[:, :],
                                    in1=hincD[:, base * 16:base * 16 + 256],
                                    op=ALU.add)
                    vv = k.mark(v, "v")
                    k.wait(t, "v", vv)
                    for grp in range(2):
                        k.wait(t, "s")
                        for j in range(8):
                            et = grp * 8 + j
                            mi = t.transpose(
                                k.psum_bf[off:off + 16,
                                          j * 128:(j + 1) * 128],
                                in_=htl[:, et * 16:(et + 1) * 16],
                                identity=W["identb"])
                        tm = k.inc(mi, "t")
                        k.wait(s, "t", tm)
                        s.copy(out=httr[off:off + 16,
                                        grp * 1024:(grp + 1) * 1024],
                               in_=k.psum_bf[off:off + 16, 0:1024])
                        k.mark(s, "s")
                    k.wait(t, "s")
                    k.wait(t, "v")
                    mi = t.matmul(k.psum[2 + ch01][off:off + 64, 0:64],
                                  lhsT=bcc[off:off + 16, 0:LCH],
                                  rhs=bcc[off:off + 16, LCH:2 * LCH],
                                  start=True, stop=True)
                    tm = k.inc(mi, "t")
                    k.wait(v, "t", tm)
                    v.tensor_tensor(out=ktb[off:off + 64, :],
                                    in0=k.psum[2 + ch01][off:off + 64, 0:64],
                                    in1=kmask[off:off + 64, :], op=ALU.mult)
                    k.mark(v, "v")
                k.wait(t, "xa", l2)
                k.wait(t, "v")
                for nch in range(4):
                    nsl = slice(nch * 512, (nch + 1) * 512)
                    k.wait(t, "v")
                    for ch01 in range(2):
                        off = ch01 * 64
                        psy = k.psum[4 + ch01]
                        t.matmul(psy[off:off + 64, 0:512],
                                 lhsT=ktb[off:off + 64, :],
                                 rhs=xp_[off:off + 64, nsl],
                                 start=True, stop=False,
                                 skip_group_check=True)
                    for ch01 in range(2):
                        off = ch01 * 64
                        psy = k.psum[4 + ch01]
                        mi = t.matmul(psy[off:off + 64, 0:512],
                                      lhsT=c3b[off:off + 16, :],
                                      rhs=httr[off:off + 16, nsl],
                                      start=False, stop=True,
                                      skip_group_check=True)
                    tm = k.inc(mi, "t")
                    k.wait(v, "t", tm)
                    for ch01 in range(2):
                        off = ch01 * 64
                        psy = k.psum[4 + ch01]
                        v.tensor_tensor(out=yrow[off:off + 64, nsl],
                                        in0=psy[off:off + 64, 0:512],
                                        in1=g1_[off:off + 64, nsl],
                                        op=ALU.mult)
                    k.mark(v, "v")
                k.wait(t, "v")
                for grp in range(4):
                    k.wait(t, "v")
                    for j in range(4):
                        et = grp * 4 + j
                        mi = t.transpose(k.psum_bf[:, j * 128:(j + 1) * 128],
                                         in_=yrow[:, et * 128:(et + 1) * 128],
                                         identity=W["identb"])
                    tm = k.inc(mi, "t")
                    k.wait(v, "t", tm)
                    v.tensor_copy(
                        out=yT[:, tt * DIN + grp * 512:tt * DIN + (grp + 1) * 512],
                        in_=k.psum_bf[:, 0:512])
                    k.mark(v, "v")
                k.wait(y, "v", k.c["v"])
                l3 = k.dma("y", zsb[:, :].rearrange("p (a c) -> p a c", a=NET),
                           zsh[:, :, bb * 256 + tt * 128:
                               bb * 256 + tt * 128 + 128], sem="xc")
                l4 = k.dma("y", xrb[:, :].rearrange("p (a c) -> p a c", a=NET),
                           xih[:, :, bb * 256 + tt * 128:
                               bb * 256 + tt * 128 + 128], sem="xc")
                k.wait(v, "xc", l4)
                ysl = yT[:, tt * DIN:(tt + 1) * DIN]
                v.tensor_tensor(
                    out=xrb[:, :].rearrange("p (e c) -> p e c", e=NET),
                    in0=xrb[:, :].rearrange("p (e c) -> p e c", e=NET),
                    in1=dcol[:, :].rearrange("p (e o) -> p e o", o=1)
                    .broadcast_to([128, NET, 128]), op=ALU.mult)
                v.tensor_tensor(out=ysl, in0=xrb[:, :], in1=ysl, op=ALU.add)
                v.tensor_tensor(out=ysl, in0=ysl, in1=zsb[:, :], op=ALU.mult)
                k.mark(v, "v")
            k.wait(t, "gd")
            k.wait(t, "v")
            for etp in range(NDT):
                ps = k.psum[etp % 2]
                if etp >= 2:
                    k.wait(t, "v")
                for tt2 in range(2):
                    for dt in range(NET):
                        mi = t.matmul(
                            ps[:, tt2 * 128:(tt2 + 1) * 128],
                            lhsT=opw[:, dt * 1024 + etp * 128:
                                     dt * 1024 + (etp + 1) * 128],
                            rhs=yT[:, tt2 * DIN + dt * 128:
                                   tt2 * DIN + (dt + 1) * 128],
                            start=(dt == 0 and tt2 == 0),
                            stop=(dt == NET - 1),
                            skip_group_check=True)
                tm = k.inc(mi, "t")
                k.wait(v, "t", tm)
                rsl = R[:, etp * NTOK + bb * 256:etp * NTOK + (bb + 1) * 256]
                v.tensor_tensor(out=rsl, in0=rsl, in1=ps[:, 0:256], op=ALU.add)
                k.mark(v, "v")
    k.bar()


def build_nc(stage_limit=STAGE_LIMIT):
    nc = bass.Bass("TRN2", target_bir_lowering=False)
    dram = {}

    def din(name, shape, dt=F32):
        dram[name] = nc.dram_tensor(name, shape, dt, kind="ExternalInput").ap()
        return dram[name]

    x_h = din("x", [128, NDT * NTOK])
    out_h = nc.dram_tensor("out", [128, NDT * NTOK], F32,
                           kind="ExternalOutput").ap()
    dbg_h = nc.dram_tensor("dbg", [1, 32], F32, kind="ExternalOutput").ap()
    xn_hbm = nc.dram_tensor("xn_hbm", [128, NDT * NTOK], BF16).ap()
    xh_hbm = nc.dram_tensor("xh_hbm", [128, 32 * 512], BF16).ap()
    cc_in = nc.dram_tensor("cc_kv_in", [128, 1536], BF16).ap()
    cc_out = nc.dram_tensor("cc_kv_out", [128 * NCORES, 1536], BF16).ap()
    for nm in ("wfT", "wiT", "wcT", "ufT", "uiT", "ucT", "woT", "uoT"):
        din("l0f_" + nm, [128, NDT * D])
    din("l0f_bias", [128, 4 * NDT])
    din("l0f_incol", [128, NDT])
    for nm in ("qwT", "gwT", "owT", "pwrT", "pwiT"):
        din("l0a_" + nm, [128, NDT * D])
    din("l0a_kwT", [128, NDT * 512])
    din("l0a_vwT", [128, NDT * 256])
    din("l0a_dw", [128, 2 * NDT * 7])
    din("l0a_dft", [128, NBIN])
    din("l0a_ib", [128, NBIN])
    din("l0a_mask", [128, 128])
    din("l0a_onehot", [128, 16])
    din("l0a_alpha", [1, 1])
    din("l1f_wgate", [128, 24 * 1024])
    din("l1f_wup", [128, 24 * 1024])
    din("l1f_wdown", [128, 8 * 3072])
    wq_g = nc.dram_tensor("wqT_gate", [128, 24 * 1024], BF16).ap()
    wq_u = nc.dram_tensor("wqT_up", [128, 24 * 1024], BF16).ap()
    wq_d = nc.dram_tensor("wqT_down", [128, 8 * 3072], BF16).ap()
    pk_gate = nc.dram_tensor("pk_gate", [128, 24 * NTOK], BF16).ap()
    pk_up = nc.dram_tensor("pk_up", [128, 24 * NTOK], BF16).ap()
    pk_gq = nc.dram_tensor("pk_gq", [128, 24 * NTOK], I16).ap()
    pk_g2 = nc.dram_tensor("pk_g2", [128, 24 * NTOK], I16).ap()
    din("l1a_ipw", [128, NDT * 4096])
    din("l1a_apw", [128, NET * DIN])
    din("l1a_acolh", [128, 16])
    din("l1a_bw", [128, NET * 16])
    din("l1a_cw", [128, NET * 16])
    din("l1a_opw", [128, NET * 1024])
    din("l1a_tau", [128, 128])
    din("l1a_abase", [128, 1])
    din("l1a_tri", [128, 128])
    din("l1a_kmask", [128, LCH])
    din("l1a_dcolh", [128, 16])
    din("l1a_foldmh", [128, 16])
    din("l1a_nwcol", [128, NDT])
    xi_hbm = nc.dram_tensor("xi_hbm", [128, NET * NTOK], BF16).ap()
    zs_hbm = nc.dram_tensor("zs_hbm", [128, NET * NTOK], BF16).ap()
    sl_hbm = nc.dram_tensor("sl_hbm", [128, NET * NTOK], BF16).ap()
    xp_hbm = nc.dram_tensor("xp_hbm", [128, NBT * DIN], BF16).ap()
    g1_hbm = nc.dram_tensor("g1_hbm", [128, NBT * DIN], BF16).ap()
    cc3_in = nc.dram_tensor("cc3_in", [128, B * 272], BF16).ap()
    cc3_out = nc.dram_tensor("cc3_out", [128 * NCORES, B * 272], BF16).ap()
    cc2_in = nc.dram_tensor("cc2_in", [1, 16], F32).ap()
    cc2_out = nc.dram_tensor("cc2_out", [1, 16], F32).ap()
    colb = nc.dram_tensor("colb", [1, 128], F32).ap()

    with ExitStack() as stack:
        block = stack.enter_context(nc.Block())
        k = K(nc, block, stack)
        R = k.sb("R", [128, NDT * NTOK], F32)
        ones16 = k.sb("ones16", [128, 16], BF16)
        ones1 = k.sb("ones1", [1, 128], BF16)
        dbg_sb = k.sb("dbg_sb", [1, 32], F32)
        k.v.memset(dbg_sb[0:1, :], 0.0)
        cst = k.sb("cst", [128, 2], F32)
        k.v.memset(cst[:, 0:1], 1e-5)
        k.v.memset(cst[:, 1:2], 1e-6)
        identb = k.sb("identb", [128, 128], BF16)
        colidx = k.sb("colidx", [128, 128], I32)
        pcol = k.sb("pcol", [128, 1], F32)
        pcoli = k.sb("pcoli", [128, 1], I32)
        k.g.iota(colidx[:, :], pattern=[[1, 128]], base=0, channel_multiplier=0)
        k.g.iota(pcoli[:, :], pattern=[[0, 1]], base=0, channel_multiplier=1)
        gm = k.mark(k.g, "gc")
        k.wait(k.v, "gc", gm)
        k.v.tensor_copy(out=pcol[:, :], in_=pcoli[:, :])
        k.v.tensor_scalar(out=identb[:, :], in0=colidx[:, :],
                          scalar1=pcol[:, 0:1], scalar2=None, op0=ALU.is_equal)

        for dt in range(NDT):
            k.dma("y", R[:, dt * NTOK:(dt + 1) * NTOK],
                  x_h[:, dt * NTOK:(dt + 1) * NTOK])
        k.v.memset(ones16[:, :], 1.0)
        k.v.memset(ones1[:, :], 1.0)
        k.mark(k.v, "v")
        k.bar()

        W0 = {"wfT": dram["l0f_wfT"], "wiT": dram["l0f_wiT"],
              "wcT": dram["l0f_wcT"], "ufT": dram["l0f_ufT"],
              "uiT": dram["l0f_uiT"], "ucT": dram["l0f_ucT"],
              "woT": dram["l0f_woT"], "uoT": dram["l0f_uoT"],
              "bias": dram["l0f_bias"], "incol": dram["l0f_incol"]}

        if stage_limit >= 1 and STAGE_SKIP < 1:
            with ExitStack() as st:
                emit_layernorm(k, R[:, :], xn_hbm, ones16[:, :], ones1[:, :],
                               st, mode="ln", eps_col=cst[:, 0:1], uid="a")
            with ExitStack() as st:
                emit_l0_ffn(k, R[:, :], xn_hbm, W0, st, uid="a")

        if stage_limit >= 2 and STAGE_SKIP < 2:
            WA = {nm: dram["l0a_" + nm] for nm in
                  ("qwT", "kwT", "vwT", "owT", "gwT", "pwrT", "pwiT", "dw",
                   "dft", "ib", "mask", "onehot", "alpha")}
            WA["cc_in"] = cc_in
            WA["cc_out"] = cc_out
            WA["ones1"] = ones1[:, :]
            WA["identb"] = identb[:, :]
            with ExitStack() as st:
                emit_layernorm(k, R[:, :], xn_hbm, ones16[:, :], ones1[:, :],
                               st, mode="ln", eps_col=cst[:, 0:1], uid="b")
            with ExitStack() as st:
                emit_l0_attn(k, R[:, :], xn_hbm, xh_hbm, WA, st, uid="b")

        if stage_limit >= 3 and STAGE_SKIP < 3:
            with ExitStack() as st:
                emit_layernorm(k, R[:, :], xn_hbm, ones16[:, :], ones1[:, :],
                               st, mode="ln", eps_col=cst[:, 0:1], uid="c")
            with ExitStack() as st:
                emit_l0_ffn(k, R[:, :], xn_hbm, W0, st, uid="c")

        WF = {"wgate": dram["l1f_wgate"], "wup": dram["l1f_wup"],
              "wdown": dram["l1f_wdown"], "wqT_wgate": wq_g,
              "wqT_wup": wq_u, "wqT_wdown": wq_d,
              "pk_gate": pk_gate, "pk_up": pk_up, "pk_gq": pk_gq,
              "pk_g2": pk_g2, "cc2_in": cc2_in, "cc2_out": cc2_out,
              "colb": colb, "identb": identb[:, :]}
        if stage_limit >= 4 and STAGE_SKIP < 4:
            with ExitStack() as st:
                xnmax = k.sb("xnmax4", [128, 1], F32, st)
                with ExitStack() as st2:
                    emit_layernorm(k, R[:, :], xn_hbm, ones16[:, :],
                                   ones1[:, :], st2, mode="ln",
                                   eps_col=cst[:, 0:1], uid="d",
                                   absmax_out=xnmax[:, 0:1])
                emit_l1_ffn(k, R[:, :], xn_hbm, WF, st, xnmax[:, 0:1],
                            first=True, uid="d", dbg=dbg_sb)

        if stage_limit >= 5 and STAGE_SKIP < 5:
            WA5 = {"ipw": dram["l1a_ipw"], "apw": dram["l1a_apw"],
                   "acolh": dram["l1a_acolh"], "bw": dram["l1a_bw"],
                   "cw": dram["l1a_cw"], "opw": dram["l1a_opw"],
                   "tau": dram["l1a_tau"], "abase": dram["l1a_abase"],
                   "tri": dram["l1a_tri"], "kmask": dram["l1a_kmask"],
                   "dcolh": dram["l1a_dcolh"], "foldmh": dram["l1a_foldmh"],
                   "colb": colb, "identb": identb[:, :],
                   "xi_hbm": xi_hbm, "zs_hbm": zs_hbm, "sl_hbm": sl_hbm,
                   "xp_hbm": xp_hbm, "g1_hbm": g1_hbm,
                   "cc3_in": cc3_in, "cc3_out": cc3_out}
            with ExitStack() as st:
                nwc = k.sb("l1a_nwc", [128, NDT], F32, st)
                k.dma("y", nwc[:, :], dram["l1a_nwcol"][:, :])
                with ExitStack() as st2:
                    emit_layernorm(k, R[:, :], xn_hbm, ones16[:, :],
                                   ones1[:, :], st2, mode="lnrms",
                                   normw_col=nwc[:, :], eps_col=cst[:, 0:1],
                                   eps2_col=cst[:, 1:2], uid="e")
                emit_l1_attn(k, R[:, :], xn_hbm, WA5, st, uid="e")

        if stage_limit >= 6 and STAGE_SKIP < 6:
            with ExitStack() as st:
                xnmax = k.sb("xnmax6", [128, 1], F32, st)
                with ExitStack() as st2:
                    emit_layernorm(k, R[:, :], xn_hbm, ones16[:, :],
                                   ones1[:, :], st2, mode="ln",
                                   eps_col=cst[:, 0:1], uid="f",
                                   absmax_out=xnmax[:, 0:1])
                emit_l1_ffn(k, R[:, :], xn_hbm, WF, st, xnmax[:, 0:1],
                            first=False, uid="f")

        k.bar()
        for dt in range(NDT):
            k.dma("y", out_h[:, dt * NTOK:(dt + 1) * NTOK],
                  R[:, dt * NTOK:(dt + 1) * NTOK])
        k.dma("y", dbg_h[0:1, :], dbg_sb[0:1, :])
        k.wait(k.y, "yd")
        k.wait(k.g, "yd")
        k.finalize()
    return nc


def _dtmajor(w):
    Dd, E = w.shape
    return np.ascontiguousarray(
        w.reshape(Dd // 128, 128, E).transpose(1, 0, 2).reshape(
            128, (Dd // 128) * E))


def _coltile(vec):
    E = vec.shape[0]
    return np.ascontiguousarray(vec.reshape(E // 128, 128).T)


def _prep_common(inputs):
    d = {}
    for nm in ("wf", "wi", "wc", "uf", "ui", "uc", "wo", "uo"):
        d["l0f_" + nm + "T"] = _dtmajor(np.ascontiguousarray(
            inputs["L0_ffn_" + nm].T))
    d["l0f_bias"] = np.concatenate(
        [_coltile(inputs["L0_ffn_b" + g]) for g in ("f", "i", "c", "o")], axis=1)
    d["l0f_incol"] = _coltile(inputs["L0_ffn_in_c"])
    d["l0a_qwT"] = _dtmajor(np.ascontiguousarray(inputs["L0_attn_q_w"].T))
    kw2 = np.repeat(inputs["L0_attn_k_w"].reshape(4, 1, 64, D), 2,
                    axis=1).reshape(512, D)
    d["l0a_kwT"] = _dtmajor(np.ascontiguousarray(kw2.T))
    d["l0a_vwT"] = _dtmajor(np.ascontiguousarray(inputs["L0_attn_v_w"].T))
    d["l0a_owT"] = _dtmajor(np.ascontiguousarray(inputs["L0_attn_o_w"].T))
    d["l0a_gwT"] = _dtmajor(np.ascontiguousarray(inputs["L0_attn_gate_w"].T))
    d["l0a_pwrT"] = _dtmajor(np.ascontiguousarray(inputs["L0_attn_pw_r"].T))
    d["l0a_pwiT"] = _dtmajor(np.ascontiguousarray(inputs["L0_attn_pw_i"].T))
    dwr = _dtmajor(inputs["L0_attn_dw_r"].reshape(1024, 7))
    dwi = _dtmajor(inputs["L0_attn_dw_i"].reshape(1024, 7))
    d["l0a_dw"] = np.concatenate([dwr, dwi], axis=1)
    tt = np.arange(64.0)
    cols = [np.cos(2 * np.pi * m * tt / 64.0) for m in range(33)]
    cols += [-np.sin(2 * np.pi * m * tt / 64.0) for m in range(1, 32)]
    dft = np.stack(cols, axis=1).astype(np.float32)
    d["l0a_dft"] = np.concatenate([dft, dft], axis=0)
    ib = np.zeros((64, 64), np.float32)
    for j in range(64):
        vv = np.zeros(33, complex)
        if j <= 32:
            vv[j] = 1.0
        else:
            vv[j - 32] = 1j
        ib[j] = np.fft.irfft(vv, 64).astype(np.float32)
    d["l0a_ib"] = np.concatenate([ib, ib], axis=0)
    oh = np.zeros((128, 16), np.float32)
    oh[0:4] = np.repeat(np.eye(4, dtype=np.float32), 4, axis=1)
    oh[64:68] = oh[0:4]
    d["l0a_onehot"] = oh
    d["l0a_alpha"] = inputs["L0_attn_alpha_raw"].reshape(1, 1)
    d["l1a_ipw"] = _dtmajor(inputs["L1_attn_in_proj_w"])
    d["l1a_apw"] = _dtmajor(inputs["L1_attn_A_proj_w"])
    d["l1a_acolh"] = _coltile(inputs["L1_attn_A_proj_b"])
    d["l1a_bw"] = _dtmajor(inputs["L1_attn_B_proj_w"])
    d["l1a_cw"] = _dtmajor(inputs["L1_attn_C_proj_w"])
    d["l1a_opw"] = _dtmajor(inputs["L1_attn_out_proj_w"])
    d["l1a_dcolh"] = _coltile(inputs["L1_attn_D"])
    d["l1a_nwcol"] = _coltile(inputs["L1_attn_norm_w"])
    d["l1a_abase"] = np.ascontiguousarray(
        np.tile(np.asarray(inputs["L1_attn_A_base"], np.float32),
                8).reshape(128, 1))
    tauv = np.concatenate([np.arange(64.0),
                           63.0 - np.arange(64.0)]).astype(np.float32)
    d["l1a_tau"] = np.ascontiguousarray(np.tile(tauv[None, :], (128, 1)))
    jj, ss = np.meshgrid(np.arange(64), np.arange(64), indexing="ij")
    M1 = (jj <= ss).astype(np.float32) - (jj <= 31).astype(np.float32)
    M2 = (jj > ss).astype(np.float32)
    d["l1a_tri"] = np.ascontiguousarray(
        np.tile(np.concatenate([M1, M2], axis=1), (2, 1)))
    d["l1a_kmask"] = np.ascontiguousarray(
        np.tile((jj <= ss).astype(np.float32), (2, 1)))
    d["l1f_wgate"] = _dtmajor(inputs["L1_ffn_w_gate"])
    d["l1f_wup"] = _dtmajor(inputs["L1_ffn_w_up"])
    d["l1f_wdown"] = _dtmajor(inputs["L1_ffn_w_down"])
    return d




def _core_inputs(ci, x, common):
    sl = x[:, ci * TLOC:(ci + 1) * TLOC, :]
    xT = np.ascontiguousarray(sl.transpose(2, 0, 1).reshape(D, NTOK))
    m = {"x": _dtmajor(xT)}
    m.update(common)
    kchunk = np.arange(128) // 4
    mrows = np.stack(
        [np.where(kchunk <= 4 * ci + c, 0.0, -1e30).astype(np.float32)
         for c in range(4)], axis=0)
    mk = np.zeros((128, 128), np.float32)
    mk[0:4] = mrows
    mk[64:68] = mrows
    m["l0a_mask"] = mk
    fm = np.zeros((128, 16), np.float32)
    for r in range(8):
        fm[:, r] = 1.0 if r < ci else 0.0
        fm[:, 8 + r] = 1.0 - fm[0, r]
    m["l1a_foldmh"] = fm
    return m

def kernel(**inputs):
    inputs = {k_: np.asarray(v, dtype=np.float32) for k_, v in inputs.items()}
    x = inputs["L0_attn_hidden_states"]
    common = _prep_common(inputs)
    nc = build_nc(STAGE_LIMIT)
    in_maps = []
    for ci in range(NCORES):
        in_maps.append(_core_inputs(ci, x, common))
    res = run_bass_kernel_spmd(nc, in_maps, core_ids=list(range(NCORES)))
    full = np.empty((B, 2048, D), dtype=np.float32)
    for ci in range(NCORES):
        o = np.asarray(res.results[ci]["out"]).reshape(128, NDT, B, TLOC)
        full[:, ci * TLOC:(ci + 1) * TLOC, :] = (
            o.transpose(2, 3, 1, 0).reshape(B, TLOC, D))
    return full


# version 04
# speedup vs baseline: 1.0692x; 1.0692x over previous
import math
from contextlib import ExitStack

import numpy as np
import concourse.bass as bass
import concourse.mybir as mybir
from concourse.bass_utils import run_bass_kernel_spmd

F32 = mybir.dt.float32
BF16 = mybir.dt.bfloat16
I16 = mybir.dt.int16
U16 = mybir.dt.uint16
I32 = mybir.dt.int32
AF = mybir.ActivationFunctionType
ALU = mybir.AluOpType
AXX = mybir.AxisListType.X

NCORES = 8
D = 1024
NDT = D // 128
B = 8
TLOC = 256
NTOK = B * TLOC
NCH = NTOK // 512
STAGE_LIMIT = 6
STAGE_SKIP = 0
ATTN_PHASES = 6
ATTN_F_MODE = 3


class _Pending:
    def __init__(self):
        self.incs = []

    def then_inc(self, sem, amt=1):
        self.incs.append((sem, amt))
        return self


_NO_DRAIN = {"wait_ge", "nop", "drain", "dma_start", "dma_start_transpose",
             "sem_inc", "sem_clear", "collective_compute", "wait_op"}


class _EngRec:
    def __init__(self, prog, auto_drain=False):
        self._prog = prog
        self._auto = auto_drain

    def __getattr__(self, name):
        if name.startswith("_"):
            raise AttributeError(name)
        def call(*a, **kw):
            p = _Pending()
            self._prog.append((name, a, kw, p))
            if self._auto and name not in _NO_DRAIN:
                self._prog.append(("drain", (), {}, _Pending()))
            return p
        return call


class K:
    def __init__(self, nc, block, stack):
        self.nc = nc
        self.block = block
        self.stack = stack
        self.prog = {n: [] for n in "tvsgy"}
        self.t = _EngRec(self.prog["t"])
        self.v = _EngRec(self.prog["v"], auto_drain=True)
        self.s = _EngRec(self.prog["s"], auto_drain=True)
        self.g = _EngRec(self.prog["g"], auto_drain=True)
        self.y = _EngRec(self.prog["y"])
        self.sems = {}
        self.c = {}
        for name in ("t", "v", "s", "gc", "yd", "gd", "sd", "cc", "xa", "xb", "xc", "xd", "xe", "xf"):
            self.sems[name] = stack.enter_context(nc.semaphore("sem_" + name))
            self.c[name] = 0
        self.psum = [stack.enter_context(
            nc.psum_tensor(f"psbank{i}", [128, 512], F32)) for i in range(7)]
        self.psum_bf = stack.enter_context(
            nc.psum_tensor("psbank_bf", [128, 1024], BF16))

    def scope(self, name):
        for en in "tvsgy":
            self.prog[en].append(("SCOPE", name, None, None))

    def finalize(self):
        secs = {"t": self.block.tensor, "v": self.block.vector,
                "s": self.block.scalar, "g": self.block.gpsimd,
                "y": self.block.sync}
        for en, entries in self.prog.items():
            def f(e, entries=entries):
                open_sc = None
                for (name, a, kw, p) in entries:
                    if name == "SCOPE":
                        if open_sc is not None:
                            self.nc.leave_named_scope(open_sc[0], open_sc[1], False)
                        sid, _ = self.nc.enter_named_scope(a, False)
                        open_sc = (a, sid)
                        continue
                    r = getattr(e, name)(*a, **kw)
                    for (sem, amt) in p.incs:
                        r.then_inc(sem, amt)
                if open_sc is not None:
                    self.nc.leave_named_scope(open_sc[0], open_sc[1], False)
            secs[en](f)

    def inc(self, instr, sem, amt=1):
        instr.then_inc(self.sems[sem], amt)
        self.c[sem] += amt
        return self.c[sem]

    def mark(self, eng, sem):
        return self.inc(eng.drain(), sem)

    def wait(self, eng, sem, val=None):
        eng.wait_ge(self.sems[sem], self.c[sem] if val is None else val)

    def dma(self, queue, out, in_, sem=None):
        eng = {"y": self.y, "g": self.g, "s": self.s}[queue]
        if sem is None:
            sem = {"y": "yd", "g": "gd", "s": "sd"}[queue]
        instr = eng.dma_start(out=out, in_=in_)
        return self.inc(instr, sem, 16)

    def bar(self):
        engs = [("t", self.t), ("v", self.v), ("s", self.s), ("gc", self.g)]
        for sem, e in engs:
            self.mark(e, sem)
        snap = dict(self.c)
        for _, e in engs + [(None, self.y)]:
            for sem in ("t", "v", "s", "gc", "yd", "gd", "sd", "xa", "xb", "xc", "xd", "xe", "xf"):
                if snap[sem] > 0:
                    e.wait_ge(self.sems[sem], snap[sem])

    def sb(self, name, shape, dt=F32, stack=None):
        return (stack or self.stack).enter_context(
            self.nc.sbuf_tensor(name, shape, dt))


def emit_layernorm(k: K, R, xn_hbm, ones16, ones1, st, mode="ln",
                   normw_col=None, eps_col=None, eps2_col=None, uid="",
                   absmax_out=None):
    t, v, s = k.t, k.v, k.s
    if absmax_out is not None:
        am4 = k.sb("ln_am4" + uid, [128, NCH], F32, st)
    if mode == "lnrms":
        lnt = k.sb("ln_lnt" + uid, [1, NTOK], F32, st)
    xbf = k.sb("ln_xbf" + uid, [128, NDT * 512], BF16, st)
    xsq = k.sb("ln_xsq" + uid, [128, NDT * 512], BF16, st)
    stats = k.sb("ln_stats" + uid, [16, 2 * NTOK], F32, st)
    S1sq = k.sb("ln_s1sq" + uid, [1, NTOK], F32, st)
    brow = k.sb("ln_brow" + uid, [1, 4 * NTOK], BF16, st)
    xnc = k.sb("ln_xnc" + uid, [128, 2 * NDT * 512], BF16, st)
    tmp = k.sb("ln_tmp" + uid, [128, 512], F32, st)
    if mode == "lnrms":
        tmp2 = k.sb("ln_tmp2" + uid, [128, 512], F32, st)

    k.scope("ln" + uid)
    k.bar()
    for cidx in range(NCH):
        csl = [slice(dt * NTOK + cidx * 512, dt * NTOK + (cidx + 1) * 512)
               for dt in range(NDT)]
        if cidx > 0:
            k.wait(s, "t")
        for dt in range(NDT):
            if mode != "rms":
                s.copy(out=xbf[:, dt * 512:(dt + 1) * 512], in_=R[:, csl[dt]])
            s.activation(out=xsq[:, dt * 512:(dt + 1) * 512], in_=R[:, csl[dt]],
                         func=AF.Square)
        sr = k.mark(s, "s")
        k.wait(t, "s", sr)
        srcs = ([(0, xbf), (1, xsq)] if mode != "rms" else [(1, xsq)])
        for which, src in srcs:
            ps = k.psum[4 + which]
            if cidx > 0:
                k.wait(t, "v")
            for dt in range(NDT):
                mi = t.matmul(ps[0:16, :], lhsT=ones16,
                              rhs=src[:, dt * 512:(dt + 1) * 512],
                              start=(dt == 0), stop=(dt == NDT - 1))
            mm = k.inc(mi, "t")
            k.wait(v, "t", mm)
            v.tensor_copy(out=stats[0:16, which * NTOK + cidx * 512:
                                    which * NTOK + (cidx + 1) * 512],
                          in_=ps[0:16, :])
            k.mark(v, "v")

    S1r = stats[0:1, 0:NTOK]
    Qr = stats[0:1, NTOK:2 * NTOK]
    RSb = brow[0:1, 0:NTOK]
    NMb = brow[0:1, NTOK:2 * NTOK]
    RS2b = brow[0:1, 2 * NTOK:3 * NTOK]
    NM2b = brow[0:1, 3 * NTOK:4 * NTOK]
    k.bar()
    if mode != "rms":
        v.scalar_tensor_tensor(out=S1sq[0:1, :], in0=S1r, scalar=-1.0 / D,
                               in1=S1r, op0=ALU.mult, op1=ALU.mult)
        v.tensor_tensor(out=Qr, in0=Qr, in1=S1sq[0:1, :], op=ALU.add)
    vh = k.mark(v, "v")
    k.wait(s, "v", vh)
    if mode == "lnrms":
        s.activation(out=lnt[0:1, :], in_=Qr, func=AF.Ln, scale=1.0 / D,
                     bias=eps_col[0:1, 0:1])
        s.activation(out=S1sq[0:1, :], in_=lnt[0:1, :], func=AF.Exp,
                     scale=-1.0)
        sh = k.mark(s, "s")
        k.wait(v, "s", sh)
        v.scalar_tensor_tensor(out=S1sq[0:1, :], in0=Qr, scalar=1.0 / D,
                               in1=S1sq[0:1, :], op0=ALU.mult, op1=ALU.mult)
        vh = k.mark(v, "v")
        k.wait(s, "v", vh)
        s.activation(out=S1sq[0:1, :], in_=S1sq[0:1, :], func=AF.Ln,
                     bias=eps2_col[0:1, 0:1])
        sh = k.mark(s, "s")
        k.wait(v, "s", sh)
        v.tensor_tensor(out=Qr, in0=lnt[0:1, :], in1=S1sq[0:1, :], op=ALU.add)
        vh = k.mark(v, "v")
        k.wait(s, "v", vh)
        s.activation(out=Qr, in_=Qr, func=AF.Exp, scale=-0.5)
        s.copy(out=RSb, in_=Qr)
        s.activation(out=lnt[0:1, :], in_=lnt[0:1, :], func=AF.Exp, scale=-0.5)
        s.copy(out=RS2b, in_=lnt[0:1, :])
        sh = k.mark(s, "s")
        k.wait(v, "s", sh)
        v.scalar_tensor_tensor(out=lnt[0:1, :], in0=S1r, scalar=-1.0 / D,
                               in1=lnt[0:1, :], op0=ALU.mult, op1=ALU.mult)
        v.scalar_tensor_tensor(out=S1r, in0=S1r, scalar=-1.0 / D, in1=Qr,
                               op0=ALU.mult, op1=ALU.mult)
        vh2 = k.mark(v, "v")
        k.wait(s, "v", vh2)
        s.copy(out=NM2b, in_=lnt[0:1, :])
        s.copy(out=NMb, in_=S1r)
        k.mark(s, "s")
    else:
        s.activation(out=Qr, in_=Qr, func=AF.Ln, scale=1.0 / D,
                     bias=eps_col[0:1, 0:1])
        s.activation(out=Qr, in_=Qr, func=AF.Exp, scale=-0.5)
        s.copy(out=RSb, in_=Qr)
        sh = k.mark(s, "s")
    if mode == "ln":
        k.wait(v, "s", sh)
        v.scalar_tensor_tensor(out=S1r, in0=S1r, scalar=-1.0 / D, in1=Qr,
                               op0=ALU.mult, op1=ALU.mult)
        vh2 = k.mark(v, "v")
        k.wait(s, "v", vh2)
        s.copy(out=NMb, in_=S1r)
        k.mark(s, "s")
    k.bar()

    for cidx in range(NCH):
        par = cidx % 2
        if mode == "lnrms":
            ps_r, ps_m = k.psum[3], k.psum[4]
            ps_r2, ps_m2 = k.psum[5], k.psum[6]
            if cidx > 0:
                k.wait(t, "v")
            csl2 = slice(cidx * 512, (cidx + 1) * 512)
            t.matmul(ps_r[:, :], lhsT=ones1, rhs=RSb[:, csl2],
                     start=True, stop=True)
            t.matmul(ps_m[:, :], lhsT=ones1, rhs=NMb[:, csl2],
                     start=True, stop=True)
            t.matmul(ps_r2[:, :], lhsT=ones1, rhs=RS2b[:, csl2],
                     start=True, stop=True)
            mi = t.matmul(ps_m2[:, :], lhsT=ones1, rhs=NM2b[:, csl2],
                          start=True, stop=True)
        else:
            ps_r = k.psum[3 + par * 2]
            ps_m = k.psum[4 + par * 2]
            if cidx >= 2:
                k.wait(t, "v")
            mi = t.matmul(ps_r[:, :], lhsT=ones1,
                          rhs=RSb[:, cidx * 512:(cidx + 1) * 512],
                          start=True, stop=True)
            if mode != "rms":
                mi = t.matmul(ps_m[:, :], lhsT=ones1,
                              rhs=NMb[:, cidx * 512:(cidx + 1) * 512],
                              start=True, stop=True)
        mm = k.inc(mi, "t")
        k.wait(v, "t", mm)
        if cidx >= 2:
            k.wait(v, "yd")
        for dt in range(NDT):
            xsl = slice(dt * NTOK + cidx * 512, dt * NTOK + (cidx + 1) * 512)
            osl = slice(par * NDT * 512 + dt * 512, par * NDT * 512 + (dt + 1) * 512)
            if mode == "ln":
                v.tensor_tensor(out=tmp[:, :], in0=R[:, xsl], in1=ps_r[:, :],
                                op=ALU.mult)
                v.tensor_tensor(out=xnc[:, osl], in0=tmp[:, :], in1=ps_m[:, :],
                                op=ALU.add)
            elif mode == "lnrms":
                v.tensor_tensor(out=tmp[:, :], in0=R[:, xsl], in1=ps_r[:, :],
                                op=ALU.mult)
                v.tensor_tensor(out=tmp[:, :], in0=tmp[:, :], in1=ps_m[:, :],
                                op=ALU.add)
                v.tensor_scalar(out=xnc[:, osl], in0=tmp[:, :],
                                scalar1=normw_col[:, dt:dt + 1], scalar2=None,
                                op0=ALU.mult)
                v.tensor_tensor(out=tmp2[:, :], in0=R[:, xsl],
                                in1=ps_r2[:, :], op=ALU.mult)
                v.tensor_tensor(out=tmp2[:, :], in0=tmp2[:, :],
                                in1=ps_m2[:, :], op=ALU.add)
                v.tensor_tensor(out=R[:, xsl], in0=R[:, xsl], in1=tmp2[:, :],
                                op=ALU.add)
            else:
                v.scalar_tensor_tensor(out=xnc[:, osl], in0=R[:, xsl],
                                       scalar=normw_col[:, dt:dt + 1],
                                       in1=ps_r[:, :], op0=ALU.mult, op1=ALU.mult)
        if absmax_out is not None:
            v.tensor_reduce(out=am4[:, cidx:cidx + 1],
                            in_=xnc[:, par * NDT * 512:(par + 1) * NDT * 512],
                            axis=AXX, op=ALU.max, apply_absolute_value=True)
        vv = k.mark(v, "v")
        k.wait(k.y, "v", vv)
        src3 = xnc[:, par * NDT * 512:(par + 1) * NDT * 512].rearrange(
            "p (a c) -> p a c", a=NDT)
        dst3 = xn_hbm[:, :].rearrange("p (a n) -> p a n", a=NDT)[
            :, :, cidx * 512:(cidx + 1) * 512]
        k.dma("y", dst3, src3)
    if absmax_out is not None:
        v.tensor_reduce(out=absmax_out, in_=am4[:, :], axis=AXX, op=ALU.max)
        k.mark(v, "v")
    k.bar()


def emit_l0_ffn(k: K, R, xn_hbm, W, st, uid=""):
    t, v, s, g = k.t, k.v, k.s, k.g
    k.scope("f_main" + uid)

    wsb = {}
    for nm in ("wfT", "wiT", "wcT", "ufT", "uiT", "ucT"):
        wsb[nm] = k.sb("ffn_" + nm + uid, [128, NDT * D], BF16, st)
    bias = k.sb("ffn_bias" + uid, [128, 4 * NDT], F32, st)
    incol = k.sb("ffn_incol" + uid, [128, NDT], F32, st)
    cc = k.sb("ffn_c" + uid, [128, NDT * TLOC], F32, st)
    cbf = k.sb("ffn_cbf" + uid, [128, NDT * TLOC], BF16, st)
    gf = k.sb("ffn_gf" + uid, [128, 2 * TLOC], F32, st)
    gi = k.sb("ffn_gi" + uid, [128, 2 * TLOC], F32, st)
    gg = k.sb("ffn_gg" + uid, [128, 2 * TLOC], F32, st)
    cuo = k.sb("ffn_cuo" + uid, [128, NDT * TLOC], F32, st)
    xstep = k.sb("ffn_xstep" + uid, [128, 2 * NDT * TLOC], BF16, st)
    xnc2 = k.sb("ffn_xnc2" + uid, [128, NDT * 512], BF16, st)

    xnh3 = xn_hbm[:, :].rearrange("p (a n) -> p a n", a=NDT)

    def load_xstep(par, step):
        dst = xstep[:, par * NDT * TLOC:(par + 1) * NDT * TLOC].rearrange(
            "p (a c) -> p a c", a=NDT)
        sem = "xa" if par == 0 else "xb"
        return (sem, k.dma("y", dst, xnh3[:, :, step * TLOC:(step + 1) * TLOC],
                           sem=sem))

    for nm in wsb:
        k.dma("g", wsb[nm][:, :], W[nm][:, :])
    k.dma("y", bias[:, :], W["bias"][:, :])
    k.dma("y", incol[:, :], W["incol"][:, :])
    xs_done = [load_xstep(0, 0), load_xstep(1, 1)]
    wload = k.c["gd"]
    k.wait(v, "yd")
    for dt in range(NDT):
        bc = incol[:, dt:dt + 1].broadcast_to([128, TLOC])
        v.tensor_copy(out=cc[:, dt * TLOC:(dt + 1) * TLOC], in_=bc)
        v.tensor_copy(out=cbf[:, dt * TLOC:(dt + 1) * TLOC], in_=bc)
    k.mark(v, "v")
    k.wait(t, "gd", wload)
    k.bar()

    names = [("wfT", "ufT"), ("wiT", "uiT"), ("wcT", "ucT")]
    gates = [gf, gi, gg]
    funcs = [AF.Sigmoid, AF.Sigmoid, AF.Tanh]
    s_marks, v_marks = [], []
    for step in range(B):
        cr = cw = cc
        spar = step % 2
        xs = xstep[:, spar * NDT * TLOC:(spar + 1) * NDT * TLOC]
        k.wait(t, xs_done[step][0], xs_done[step][1])
        if step + 2 < B:
            pass
        for et in range(NDT):
            bank0 = (et % 2) * 3
            par = et % 2
            if len(s_marks) >= 2:
                k.wait(t, "s", s_marks[-2])
            if len(v_marks) >= 2:
                k.wait(s, "v", v_marks[-2])
            for gidx in range(3):
                ps = k.psum[bank0 + gidx]
                wn, un = names[gidx]
                for dt in range(NDT):
                    t.matmul(ps[:, 0:TLOC],
                             lhsT=wsb[wn][:, dt * D + et * 128:dt * D + (et + 1) * 128],
                             rhs=xs[:, dt * TLOC:(dt + 1) * TLOC],
                             start=(dt == 0), stop=False)
                for dt in range(NDT):
                    mi = t.matmul(ps[:, 0:TLOC],
                                  lhsT=wsb[un][:, dt * D + et * 128:dt * D + (et + 1) * 128],
                                  rhs=cbf[:, dt * TLOC:(dt + 1) * TLOC],
                                  start=False, stop=(dt == NDT - 1))
            mmv = k.inc(mi, "t")
            k.wait(s, "t", mmv)
            gsl = slice(par * TLOC, (par + 1) * TLOC)
            for gidx in range(3):
                ps = k.psum[bank0 + gidx]
                s.activation(out=gates[gidx][:, gsl], in_=ps[:, 0:TLOC],
                             func=funcs[gidx],
                             bias=bias[:, gidx * NDT + et:gidx * NDT + et + 1])
            sv = k.mark(s, "s")
            s_marks.append(sv)
            k.wait(v, "s", sv)
            esl = slice(et * TLOC, (et + 1) * TLOC)
            v.tensor_tensor(out=gg[:, gsl], in0=gi[:, gsl], in1=gg[:, gsl],
                            op=ALU.mult)
            v.tensor_tensor(out=cw[:, esl], in0=gf[:, gsl], in1=cr[:, esl],
                            op=ALU.mult)
            v.tensor_tensor(out=cw[:, esl], in0=cw[:, esl], in1=gg[:, gsl],
                            op=ALU.add)
            v_marks.append(k.mark(v, "v"))
        vv = k.mark(v, "v")
        k.wait(s, "v", vv)
        for dt in range(NDT):
            s.copy(out=cbf[:, dt * TLOC:(dt + 1) * TLOC],
                   in_=cw[:, dt * TLOC:(dt + 1) * TLOC])
        sv2 = k.mark(s, "s")
        k.wait(t, "s", sv2)
        if step + 2 < B:
            k.wait(k.y, "t", k.c["t"])
            xs_done.append(load_xstep(spar, step + 2))

    k.bar()
    woT, uoT = wsb["wfT"], wsb["wiT"]
    k.dma("g", woT[:, :], W["woT"][:, :])
    k.dma("g", uoT[:, :], W["uoT"][:, :])
    wload2 = k.c["gd"]
    k.wait(t, "gd", wload2)

    for et in range(NDT):
        ps = k.psum[5 + (et % 2)]
        if et >= 2:
            k.wait(t, "v")
        for dt in range(NDT):
            mi = t.matmul(ps[:, 0:TLOC],
                          lhsT=uoT[:, dt * D + et * 128:dt * D + (et + 1) * 128],
                          rhs=cbf[:, dt * TLOC:(dt + 1) * TLOC],
                          start=(dt == 0), stop=(dt == NDT - 1))
        mmv = k.inc(mi, "t")
        k.wait(v, "t", mmv)
        v.tensor_copy(out=cuo[:, et * TLOC:(et + 1) * TLOC], in_=ps[:, 0:TLOC])
        k.mark(v, "v")
    k.bar()

    dst3 = xnc2[:, :].rearrange("p (a c) -> p a c", a=NDT)
    xld = [k.dma("y", dst3, xnh3[:, :, 0:512], sem="xa")]
    for cidx in range(NCH):
        xch = xnc2[:, :]
        k.wait(t, "xa", xld[cidx])
        for et in range(NDT):
            bank = 2 * (et % 2) + (cidx % 2) % 2
            ps = k.psum[(et % 2) * 2 + (cidx % 2)]
            if et >= 1 or cidx >= 2:
                k.wait(t, "v")
            for dt in range(NDT):
                mi = t.matmul(ps[:, :],
                              lhsT=woT[:, dt * D + et * 128:dt * D + (et + 1) * 128],
                              rhs=xch[:, dt * 512:(dt + 1) * 512],
                              start=(dt == 0), stop=(dt == NDT - 1))
            mmv = k.inc(mi, "t")
            k.wait(v, "t", mmv)
            cub = cuo[:, et * TLOC:(et + 1) * TLOC]
            cub3 = cub.rearrange("p (o t) -> p o t", o=1).broadcast_to(
                [128, 2, TLOC])
            ps3 = ps[:, :].rearrange("p (b t) -> p b t", b=2)
            v.tensor_tensor(out=ps3, in0=ps3, in1=cub3, op=ALU.add)
            vv = k.mark(v, "v")
            k.wait(s, "v", vv)
            s.activation(out=ps[:, :], in_=ps[:, :], func=AF.Tanh,
                         bias=bias[:, 3 * NDT + et:3 * NDT + et + 1])
            sv = k.mark(s, "s")
            k.wait(v, "s", sv)
            rsl = slice(et * NTOK + cidx * 512, et * NTOK + (cidx + 1) * 512)
            v.scalar_tensor_tensor(out=R[:, rsl], in0=ps[:, :], scalar=0.5,
                                   in1=R[:, rsl], op0=ALU.mult, op1=ALU.add)
            k.mark(v, "v")
        if cidx + 1 < NCH:
            k.wait(k.y, "t", k.c["t"])
            xld.append(k.dma("y", dst3,
                             xnh3[:, :, (cidx + 1) * 512:(cidx + 2) * 512],
                             sem="xa"))
    k.bar()


NBIN = 64


def emit_l0_attn(k: K, R, xn_hbm, xh_hbm, W, st, core_id_dummy=None, uid=""):
    t, v, s, g, y = k.t, k.v, k.s, k.g, k.y
    xnh3 = xn_hbm[:, :].rearrange("p (a n) -> p a n", a=NDT)
    xhh = xh_hbm

    dft = k.sb("at_dft" + uid, [128, NBIN], BF16, st)
    ib = k.sb("at_ib" + uid, [128, NBIN], BF16, st)
    xlow = k.sb("at_xlow" + uid, [128, NDT * 256], BF16, st)
    xlow1m = k.sb("at_xlow1m" + uid, [128, NDT * 256], BF16, st)
    gsb = k.sb("at_g" + uid, [128, NDT * 32], F32, st)
    alow = k.sb("at_alow" + uid, [128, NDT * 256], BF16, st)
    acol = k.sb("at_acol" + uid, [128, 2], F32, st)
    mrow = k.sb("at_mask" + uid, [128, 128], BF16, st)
    ohot = k.sb("at_ohot" + uid, [128, 16], BF16, st)

    k.dma("g", dft[:, :], W["dft"][:, :])
    k.dma("g", ib[:, :], W["ib"][:, :])
    k.dma("g", mrow[:, :], W["mask"][:, :])
    k.dma("g", ohot[:, :], W["onehot"][:, :])

    araw = k.sb("at_araw" + uid, [1, 1], F32, st)
    k.dma("y", araw[:, :], W["alpha"][:, :])
    k.wait(s, "yd")
    s.activation(out=araw[:, :], in_=araw[:, :], func=AF.Sigmoid)
    sa = k.mark(s, "s")
    k.wait(s, "s", sa)
    s.copy(out=mrow[0:1, 0:1], in_=araw[:, :]) if False else None
    ab = k.sb("at_ab" + uid, [1, 1], BF16, st)
    s.copy(out=ab[:, :], in_=araw[:, :])
    sa2 = k.mark(s, "s")
    k.wait(t, "s", sa2)
    k.wait(t, "gd")
    ones1 = W["ones1"]
    mi = t.matmul(k.psum[6][:, 0:1], lhsT=ones1, rhs=ab[:, :], start=True,
                  stop=True)
    tb = k.inc(mi, "t")
    k.wait(v, "t", tb)
    v.tensor_copy(out=acol[:, 0:1], in_=k.psum[6][:, 0:1])
    v.tensor_scalar(out=acol[:, 1:2], in0=acol[:, 0:1], scalar1=-1.0,
                    scalar2=1.0, op0=ALU.mult, op1=ALU.add)
    v.memset(xlow[:, :], 0.0)
    k.mark(v, "v")
    k.bar()

    k.scope("atA" + uid)
    with ExitStack() as stA:
        xnw = k.sb("at_xnw" + uid, [128, 2 * NDT * 128], BF16, stA)
        xnt = k.sb("at_xnt" + uid, [128, NDT * 128], BF16, stA)
        xhe = k.sb("at_xhe" + uid, [128, 2 * 512], BF16, stA)
        pairs = [(b, cp) for b in range(B) for cp in range(2)]

        def ld_xnw(i):
            b, cp = pairs[i]
            par = i % 2
            sem = "xa" if par == 0 else "xc"
            dst = xnw[:, par * NDT * 128:(par + 1) * NDT * 128].rearrange(
                "p (a c) -> p a c", a=NDT)
            return (sem, k.dma("y", dst,
                               xnh3[:, :, b * 256 + cp * 128:
                                    b * 256 + (cp + 1) * 128], sem=sem))

        ldcnt = [ld_xnw(0), ld_xnw(1)]
        for i, (b, cp) in enumerate(pairs):
            par = i % 2
            xbuf = xnw[:, par * NDT * 128:(par + 1) * NDT * 128]
            k.wait(t, ldcnt[i][0], ldcnt[i][1])
            for half in range(2):
                if i > 0 or half > 0:
                    k.wait(t, "v")
                for j in range(4):
                    dt = half * 4 + j
                    mi = t.transpose(k.psum_bf[:, j * 128:(j + 1) * 128],
                                     in_=xbuf[:, dt * 128:(dt + 1) * 128],
                                     identity=W["identb"])
                tc = k.inc(mi, "t")
                k.wait(v, "t", tc)
                v.tensor_copy(out=xnt[:, half * 512:(half + 1) * 512],
                              in_=k.psum_bf[:, 0:512])
                k.mark(v, "v")
            ps = k.psum[i % 2]
            k.wait(t, "v")
            for ch01 in range(2):
                ch = cp * 2 + ch01
                off = ch01 * 64
                for dt in range(NDT):
                    mi = t.matmul(
                        ps[:, dt * 64:(dt + 1) * 64] if ch01 == 0 else
                        k.psum[2 + (i % 2)][:, dt * 64:(dt + 1) * 64],
                        lhsT=xnt[off:off + 64, dt * 128:(dt + 1) * 128],
                        rhs=dft[off:off + 64, :],
                        start=(dt == 0), stop=(dt == NDT - 1))
            tm = k.inc(mi, "t")
            k.wait(v, "t", tm)
            k.wait(v, "xb")
            for ch01 in range(2):
                ch = cp * 2 + ch01
                psc = ps if ch01 == 0 else k.psum[2 + (i % 2)]
                dst = xhe[:, ch01 * 512:(ch01 + 1) * 512]
                v.tensor_copy(out=dst, in_=psc[:, :])
                re_dst = xlow[:, :].rearrange("p (a c) -> p a c", a=NDT)[
                    :, :, b * 16 + ch * 4:b * 16 + ch * 4 + 4]
                re_src = psc[:, :].rearrange("p (a c) -> p a c", a=NDT)[
                    :, :, 0:4]
                v.tensor_copy(out=re_dst, in_=re_src)
                im_dst = xlow[:, :].rearrange("p (a c) -> p a c", a=NDT)[
                    :, :, 128 + b * 16 + ch * 4 + 1:128 + b * 16 + ch * 4 + 4]
                im_src = psc[:, :].rearrange("p (a c) -> p a c", a=NDT)[
                    :, :, 33:36]
                v.tensor_copy(out=im_dst, in_=im_src)
            vv = k.mark(v, "v")
            k.wait(y, "v", vv)
            k.dma("y", xhh[:, (b * 4 + cp * 2) * 512:(b * 4 + cp * 2 + 2) * 512],
                  xhe[:, :], sem="xb")
            if i + 2 < len(pairs):
                k.wait(y, "t", k.c["t"])
                ldcnt.append(ld_xnw(i + 2))
    k.bar()

    for dt in range(NDT):
        v.tensor_scalar(out=xlow1m[:, dt * 256:(dt + 1) * 256],
                        in0=xlow[:, dt * 256:(dt + 1) * 256],
                        scalar1=acol[:, 1:2], scalar2=None, op0=ALU.mult)
    k.mark(v, "v")

    if ATTN_PHASES < 2:
        k.bar()
        return
    k.scope("atB" + uid)
    qsb = k.sb("at_q" + uid, [128, NDT * 256], BF16, st)
    osb = k.sb("at_o" + uid, [128, NDT * 256], BF16, st)
    ksb = k.sb("at_k" + uid, [128, NCORES * 1024], BF16, st)
    vtsb = k.sb("at_vt" + uid, [128, B * 512], BF16, st)
    with ExitStack() as stB:
        qwT = k.sb("at_qwT" + uid, [128, NDT * D], BF16, stB)
        kwT = k.sb("at_kwT" + uid, [128, NDT * 512], BF16, stB)
        vwT = k.sb("at_vwT" + uid, [128, NDT * 256], BF16, stB)
        gwT = k.sb("at_gwT" + uid, [128, NDT * D], BF16, stB)
        kloc = k.sb("at_kloc" + uid, [128, 1024], BF16, stB)
        vtloc = k.sb("at_vtloc" + uid, [128, 512], BF16, stB)
        k.dma("g", qwT[:, :], W["qwT"][:, :])
        k.dma("g", kwT[:, :], W["kwT"][:, :])
        k.dma("g", vwT[:, :], W["vwT"][:, :])
        k.dma("g", gwT[:, :], W["gwT"][:, :])
        k.wait(t, "gd")
        for et in range(NDT):
            ps = k.psum[et % 2]
            if et >= 2:
                k.wait(t, "v")
            for dt in range(NDT):
                mi = t.matmul(ps[:, 0:256],
                              lhsT=qwT[:, dt * D + et * 128:dt * D + (et + 1) * 128],
                              rhs=xlow[:, dt * 256:(dt + 1) * 256],
                              start=(dt == 0), stop=(dt == NDT - 1))
            tm = k.inc(mi, "t")
            k.wait(v, "t", tm)
            v.tensor_scalar(out=qsb[:, et * 256:(et + 1) * 256], in0=ps[:, 0:256],
                            scalar1=0.125, scalar2=None, op0=ALU.mult)
            k.mark(v, "v")
        for et in range(4):
            ps = k.psum[2 + (et % 2)]
            if et >= 2:
                k.wait(t, "v")
            for dt in range(NDT):
                mi = t.matmul(ps[:, 0:256],
                              lhsT=kwT[:, dt * 512 + et * 128:dt * 512 + (et + 1) * 128],
                              rhs=xlow[:, dt * 256:(dt + 1) * 256],
                              start=(dt == 0), stop=(dt == NDT - 1))
            tm = k.inc(mi, "t")
            k.wait(v, "t", tm)
            v.tensor_copy(out=kloc[:, et * 256:(et + 1) * 256], in_=ps[:, 0:256])
            k.mark(v, "v")
        for ri in range(2):
            ps = k.psum[4 + ri]
            for dt in range(NDT):
                mi = t.matmul(ps[:, 0:256],
                              lhsT=xlow[:, dt * 256 + ri * 128:dt * 256 + ri * 128 + 128],
                              rhs=vwT[:, dt * 256:(dt + 1) * 256],
                              start=(dt == 0), stop=(dt == NDT - 1))
            tm = k.inc(mi, "t")
            k.wait(v, "t", tm)
            v.tensor_copy(out=vtloc[:, ri * 256:(ri + 1) * 256], in_=ps[:, 0:256])
            k.mark(v, "v")
        for et in range(NDT):
            ps = k.psum[6]
            if et > 0:
                k.wait(t, "s")
            for dt in range(NDT):
                rhs = xlow[:, :].rearrange("p (a r b c) -> p a r b c", a=NDT,
                                           r=2, b=B)[dt, :, :, :] if False else None
                rhs = xlow[:, dt * 256:dt * 256 + 128].rearrange(
                    "p (b c) -> p b c", b=B)[:, :, 0::4]
                mi = t.matmul(ps[:, 0:32],
                              lhsT=gwT[:, dt * D + et * 128:dt * D + (et + 1) * 128],
                              rhs=rhs,
                              start=(dt == 0), stop=(dt == NDT - 1))
            tm = k.inc(mi, "t")
            k.wait(s, "t", tm)
            s.activation(out=gsb[:, et * 32:(et + 1) * 32], in_=ps[:, 0:32],
                         func=AF.Sigmoid, scale=1.0 / 64.0)
            k.mark(s, "s")
        vv = k.c["v"]
        k.wait(g, "v", vv)
        k.wait(g, "s")
        k.dma("g", W["cc_in"][:, 0:1024], kloc[:, :])
        k.dma("g", W["cc_in"][:, 1024:1536], vtloc[:, :])
        g.wait_ge(k.sems["gd"], k.c["gd"])
        ci = g.collective_compute(
            "AllGather", ALU.bypass,
            replica_groups=[list(range(NCORES))],
            ins=[W["cc_in"].opt()],
            outs=[W["cc_out"].opt()],
        )
        k.inc(ci, "cc")
        ccn = k.c["cc"]
        k.wait(y, "cc", ccn)
        cco = W["cc_out"]
        for r in range(NCORES):
            k.dma("y", ksb[:, r * 1024:(r + 1) * 1024],
                  cco[128 * r:128 * (r + 1), 0:1024], sem="xa")
            dstv = vtsb[16 * r:16 * r + 16, :].rearrange(
                "p (b c) -> p b c", b=B)
            inv = cco[128 * r:128 * (r + 1), 1024:1536].rearrange(
                "(b q) c -> q b c", b=B)
            k.dma("y", dstv, inv, sem="xa")

    if ATTN_PHASES < 3:
        k.bar()
        return
    k.scope("atC" + uid)
    ahig = k.sb("at_ahig" + uid, [128, NDT * 2 * 29 * 32], BF16, st)
    with ExitStack() as stC:
        pwrT = k.sb("at_pwrT" + uid, [128, NDT * D], BF16, stC)
        pwiT = k.sb("at_pwiT" + uid, [128, NDT * D], BF16, stC)
        dwc = k.sb("at_dw" + uid, [128, 2 * NDT * 7], F32, stC)
        gin = k.sb("at_gin" + uid, [128, NDT * 29 * 32], BF16, stC)
        cin = k.sb("at_cin" + uid, [128, 2 * 29 * 32], BF16, stC)
        acc = k.sb("at_acc" + uid, [128, 29 * 32], F32, stC)
        k.dma("g", pwrT[:, :], W["pwrT"][:, :])
        k.dma("g", pwiT[:, :], W["pwiT"][:, :])
        k.dma("g", dwc[:, :], W["dw"][:, :])
        gwl = k.c["gd"]
        k.wait(v, "gd", gwl)
        k.wait(v, "xb")
        for ri in range(2):
            pwT = pwrT if ri == 0 else pwiT
            if ri == 1:
                k.wait(s, "t")
            for dt in range(NDT):
                par = dt % 2
                cbuf = cin[:, par * 928:(par + 1) * 928]
                cb3 = cbuf.rearrange("p (w c) -> p w c", w=32)
                k.wait(y, "v")
                if ri == 0:
                    src = xhh[:, :].rearrange("p (w c) -> p w c", w=32)[
                        :, :, dt * 64 + 4:dt * 64 + 33]
                    ld = k.dma("y", cb3, src, sem="xb")
                else:
                    src = xhh[:, :].rearrange("p (w c) -> p w c", w=32)[
                        :, :, dt * 64 + 36:dt * 64 + 64]
                    ld = k.dma("y", cb3[:, :, 0:28], src, sem="xb")
                k.wait(v, "xb", ld)
                if ri == 1:
                    v.memset(cb3[:, :, 28:29], 0.0)
                dcol = lambda j: dwc[:, ri * NDT * 7 + dt * 7 + j:
                                     ri * NDT * 7 + dt * 7 + j + 1]
                a3 = acc.rearrange("p (w c) -> p w c", w=32)
                k.wait(v, "s")
                v.tensor_scalar(out=a3, in0=cb3, scalar1=dcol(6), scalar2=None,
                                op0=ALU.mult)
                for j in range(5, -1, -1):
                    w = 23 + j
                    v.scalar_tensor_tensor(out=a3[:, :, 6 - j:29],
                                           in0=cb3[:, :, 0:w], scalar=dcol(j),
                                           in1=a3[:, :, 6 - j:29],
                                           op0=ALU.mult, op1=ALU.add)
                vm = k.mark(v, "v")
                k.wait(s, "v", vm)
                s.activation(out=gin[:, dt * 928:(dt + 1) * 928], in_=acc[:, :],
                             func=AF.Gelu)
                k.mark(s, "s")
            sv = k.c["s"]
            k.wait(t, "s", sv)
            for et in range(NDT):
                for half in range(2):
                    ps = k.psum[(et % 2) * 2 + half]
                    csl = slice(half * 464, (half + 1) * 464)
                    if et >= 2:
                        k.wait(t, "v")
                    for dt in range(NDT):
                        mi = t.matmul(
                            ps[:, 0:464],
                            lhsT=pwT[:, dt * D + et * 128:dt * D + (et + 1) * 128],
                            rhs=gin[:, dt * 928 + half * 464:
                                    dt * 928 + (half + 1) * 464],
                            start=(dt == 0), stop=(dt == NDT - 1))
                    tm = k.inc(mi, "t")
                    par2 = (et * 2 + half) % 2
                    xbuf = cin[:, par2 * 928 + 0:par2 * 928 + 464]
                    xb3 = xbuf.rearrange("p (w c) -> p w c", w=16)
                    k.wait(y, "v")
                    if ri == 0:
                        src = xhh[:, :].rearrange("p (w c) -> p w c", w=32)[
                            :, half * 16:(half + 1) * 16,
                            et * 64 + 4:et * 64 + 33]
                        ld = k.dma("y", xb3, src, sem="xb")
                    else:
                        src = xhh[:, :].rearrange("p (w c) -> p w c", w=32)[
                            :, half * 16:(half + 1) * 16,
                            et * 64 + 36:et * 64 + 64]
                        ld = k.dma("y", xb3[:, :, 0:28], src, sem="xb")
                    k.wait(v, "xb", ld)
                    if ri == 1:
                        v.memset(xb3[:, :, 28:29], 0.0)
                    k.wait(v, "t", tm)
                    gap = gsb[:, et * 32 + half * 16:et * 32 + (half + 1) * 16]
                    ahsl = ahig[:, (et * 2 + ri) * 928 + half * 464:
                                (et * 2 + ri) * 928 + (half + 1) * 464]
                    ah3 = ahsl.rearrange("p (w c) -> p w c", w=16)
                    ps3 = ps[:, 0:464].rearrange("p (w c) -> p w c", w=16)
                    gbc = gap.rearrange("p (w o) -> p w o", o=1).broadcast_to(
                        [128, 16, 29])
                    v.scalar_tensor_tensor(out=ah3, in0=ps3, scalar=1.0,
                                           in1=gbc, op0=ALU.mult, op1=ALU.mult)
                    v.tensor_tensor(out=ah3, in0=ah3, in1=xb3, op=ALU.add)
                    k.mark(v, "v")
    k.bar()

    if ATTN_PHASES < 4:
        k.bar()
        return
    k.scope("atD" + uid)
    with ExitStack() as stD:
        praw = k.sb("at_praw" + uid, [16, 8 * 128], BF16, stD)
        ptsb = k.sb("at_pt" + uid, [128, 2 * 16], BF16, stD)
        smal = k.sb("at_smal" + uid, [16, 3 * 8], F32, stD)
        k.wait(t, "xa")
        k.bar()
        itr = 0
        for h in range(16):
            for ri in range(2):
                et, po = h // 2, (h % 2) * 64
                for b in range(B):
                    ps = k.psum[b // 2][0:16, (b % 2) * 128:(b % 2) * 128 + 128]
                    q_ap = qsb[po:po + 64,
                               et * 256 + ri * 128 + b * 16:
                               et * 256 + ri * 128 + b * 16 + 16]
                    kh = h // 4
                    k_ap = ksb[po:po + 64, :].rearrange(
                        "p (r x) -> p r x", r=NCORES)[
                        :, :, kh * 256 + ri * 128 + b * 16:
                        kh * 256 + ri * 128 + b * 16 + 16]
                    t.matmul(ps, lhsT=q_ap, rhs=k_ap, start=(b % 2 == 0),
                             stop=False, skip_group_check=True)
                    mi = t.matmul(ps, lhsT=ohot[po:po + 4, :],
                                  rhs=mrow[po:po + 4, :],
                                  start=False, stop=(b % 2 == 1),
                                  skip_group_check=True)
                tm = k.inc(mi, "t")
                k.wait(v, "t", tm)
                k.wait(v, "s")
                for bb in range(4):
                    ps = k.psum[bb]
                    v.tensor_reduce(out=smal[0:16, bb * 2:bb * 2 + 2],
                                    in_=ps[0:16, 0:256].rearrange(
                                        "p (b c) -> p b c", b=2),
                                    axis=AXX, op=ALU.max, negate=True)
                vm = k.mark(v, "v")
                k.wait(s, "v", vm)
                k.wait(s, "t")
                for b in range(B):
                    ps = k.psum[b // 2][0:16, (b % 2) * 128:(b % 2) * 128 + 128]
                    s.activation(out=praw[0:16, b * 128:(b + 1) * 128], in_=ps,
                                 func=AF.Exp, bias=smal[0:16, b:b + 1],
                                 accum_out=smal[0:16, 8 + b:9 + b])
                sm = k.mark(s, "s")
                k.wait(v, "s", sm)
                v.reciprocal(out=smal[0:16, 16:24], in_=smal[0:16, 8:16])
                v.tensor_scalar(out=praw[0:16, :].rearrange(
                                    "p (b c) -> p b c", b=B),
                                in0=praw[0:16, :].rearrange(
                                    "p (b c) -> p b c", b=B),
                                scalar1=None, scalar2=None, op0=ALU.mult) \
                    if False else None
                v.tensor_tensor(out=praw[0:16, :].rearrange(
                                    "p (b c) -> p b c", b=B),
                                in0=praw[0:16, :].rearrange(
                                    "p (b c) -> p b c", b=B),
                                in1=smal[0:16, 16:24].rearrange(
                                    "p (b o) -> p b o", b=B).broadcast_to(
                                    [16, B, 128]),
                                op=ALU.mult)
                vm2 = k.mark(v, "v")
                k.wait(t, "v", vm2)
                for b in range(B):
                    par = b % 2
                    mi = t.transpose(k.psum_bf[:, par * 512:par * 512 + 16],
                                     in_=praw[0:16, b * 128:(b + 1) * 128],
                                     identity=W["identb"][0:16, 0:16])
                    tt = k.inc(mi, "t")
                    k.wait(s, "t", tt)
                    s.copy(out=ptsb[:, par * 16:(par + 1) * 16],
                           in_=k.psum_bf[:, par * 512:par * 512 + 16])
                    sm2 = k.mark(s, "s")
                    k.wait(t, "s", sm2)
                    mi = t.matmul(k.psum[4 + par][0:64, 0:16],
                                  lhsT=vtsb[:, b * 512 + ri * 256 + (h // 4) * 64:
                                            b * 512 + ri * 256 + (h // 4) * 64 + 64],
                                  rhs=ptsb[:, par * 16:(par + 1) * 16],
                                  start=True, stop=True)
                    tv = k.inc(mi, "t")
                    k.wait(s, "t", tv)
                    s.copy(out=osb[po:po + 64,
                                   et * 256 + ri * 128 + b * 16:
                                   et * 256 + ri * 128 + b * 16 + 16],
                           in_=k.psum[4 + par][0:64, 0:16])
                    k.mark(s, "s")
                k.wait(t, "s")
                itr += 1
    k.bar()

    if ATTN_PHASES < 5:
        k.bar()
        return
    k.scope("atE" + uid)
    with ExitStack() as stE:
        owT = k.sb("at_owT" + uid, [128, NDT * D], BF16, stE)
        k.dma("g", owT[:, :], W["owT"][:, :])
        k.wait(t, "gd")
        for et in range(NDT):
            ps = k.psum[et % 2]
            if et >= 2:
                k.wait(t, "v")
            for dt in range(NDT):
                mi = t.matmul(ps[:, 0:256],
                              lhsT=owT[:, dt * D + et * 128:dt * D + (et + 1) * 128],
                              rhs=osb[:, dt * 256:(dt + 1) * 256],
                              start=(dt == 0), stop=(dt == NDT - 1))
            tm = k.inc(mi, "t")
            k.wait(v, "t", tm)
            v.scalar_tensor_tensor(out=alow[:, et * 256:(et + 1) * 256],
                                   in0=ps[:, 0:256], scalar=acol[:, 0:1],
                                   in1=xlow1m[:, et * 256:(et + 1) * 256],
                                   op0=ALU.mult, op1=ALU.add)
            k.mark(v, "v")
    k.bar()

    if ATTN_PHASES < 6:
        k.bar()
        return
    k.scope("atF" + uid)
    with ExitStack() as stF:
        yh = k.sb("at_yh" + uid, [128, 2 * 256], BF16, stF)
        yht = k.sb("at_yht" + uid, [128, 256], BF16, stF)
        for dt in range(NDT):
            for b in range(B):
                par = (dt * B + b) % 2
                ybuf = yh[:, par * 256:(par + 1) * 256]
                y3 = ybuf.rearrange("p (c x) -> p c x", c=4)
                if dt * B + b >= 2:
                    k.wait(v, "t")
                v.tensor_copy(out=y3[:, :, 0:4],
                              in_=alow[:, dt * 256 + b * 16:
                                       dt * 256 + b * 16 + 16].rearrange(
                                  "p (c x) -> p c x", c=4))
                v.tensor_copy(out=y3[:, :, 33:36],
                              in_=alow[:, dt * 256 + 128 + b * 16:
                                       dt * 256 + 128 + b * 16 + 16].rearrange(
                                  "p (c x) -> p c x", c=4)[:, :, 1:4])
                v.tensor_copy(out=y3[:, :, 4:33],
                              in_=ahig[:, (dt * 2 + 0) * 928 + b * 4 * 29:
                                       (dt * 2 + 0) * 928 + (b * 4 + 4) * 29]
                              .rearrange("p (c x) -> p c x", c=4))
                v.tensor_copy(out=y3[:, :, 36:64],
                              in_=ahig[:, (dt * 2 + 1) * 928 + b * 4 * 29:
                                       (dt * 2 + 1) * 928 + (b * 4 + 4) * 29]
                              .rearrange("p (c x) -> p c x", c=4)[:, :, 0:28])
                vm = k.mark(v, "v")
                k.wait(t, "v", vm)
                for hf in range(2):
                    mi = t.transpose(k.psum_bf[:, hf * 512:hf * 512 + 128],
                                     in_=ybuf[:, hf * 128:(hf + 1) * 128],
                                     identity=W["identb"])
                tt = k.inc(mi, "t")
                k.wait(s, "t", tt)
                s.copy(out=yht[:, 0:128], in_=k.psum_bf[:, 0:128])
                s.copy(out=yht[:, 128:256], in_=k.psum_bf[:, 512:640])
                sm = k.mark(s, "s")
                if ATTN_F_MODE < 2:
                    continue
                k.wait(t, "s", sm)
                k.wait(t, "v")
                psE = k.psum[2 + par]
                psO = k.psum[4 + par]
                for ch01 in range(4):
                    off = (ch01 % 2) * 64
                    psx = psE if ch01 % 2 == 0 else psO
                    mi = t.matmul(psx[:, (ch01 // 2) * 64:(ch01 // 2) * 64 + 64],
                                  lhsT=yht[off:off + 64,
                                           (ch01 // 2) * 128:
                                           (ch01 // 2) * 128 + 128],
                                  rhs=ib[off:off + 64, :],
                                  start=(ch01 < 2), stop=(ch01 >= 2))
                tm = k.inc(mi, "t")
                if ATTN_F_MODE < 3:
                    continue
                k.wait(v, "t", tm)
                rsl = R[:, dt * NTOK + b * 256:dt * NTOK + (b + 1) * 256]
                r4 = rsl.rearrange("p (c x) -> p c x", c=4)
                v.tensor_tensor(out=r4[:, 0::2, :], in0=r4[:, 0::2, :],
                                in1=psE[:, 0:128].rearrange(
                                    "p (c x) -> p c x", c=2), op=ALU.add)
                v.tensor_tensor(out=r4[:, 1::2, :], in0=r4[:, 1::2, :],
                                in1=psO[:, 0:128].rearrange(
                                    "p (c x) -> p c x", c=2), op=ALU.add)
                k.mark(v, "v")
    k.bar()


def emit_l1_ffn(k: K, R, xn_hbm, W, st, xnmax_col, first, uid="", dbg=None):
    t, v, s, g, y = k.t, k.v, k.s, k.g, k.y
    IFF = 3072
    NIT = IFF // 128
    QM = 32767.0

    scr = k.sb("ff_scr" + uid, [1, 32], F32, st)
    scb = k.sb("ff_scb" + uid, [128, 16], F32, st)
    ones1f = k.sb("ff_o1f" + uid, [1, 128], F32, st)
    v.memset(ones1f[:, :], 1.0)
    v.memset(scr[0:1, :], 0.0)

    MX, MG, MU, M2, M3 = 0, 1, 2, 3, 4
    SX, IX, SG, IG, SU, IU, C2, C3, S3 = 5, 6, 7, 8, 9, 10, 11, 12, 13

    prow = k.sb("ff_prow" + uid, [1, 128], F32, st)

    def pcol_to_scr(col_ap, slot):
        k.wait(y, "v")
        wr = k.dma("y", W["colb"][0:1, :], col_ap, sem="xd")
        k.wait(y, "xd", wr)
        rd = k.dma("y", prow[0:1, :], W["colb"][0:1, :], sem="xd")
        k.wait(v, "xd", rd)
        v.tensor_reduce(out=scr[0:1, slot:slot + 1], in_=prow[0:1, :],
                        axis=AXX, op=ALU.max)
        return k.mark(v, "v")

    def allreduce_slots(lo, n):
        vv = k.c["v"]
        k.wait(g, "v", vv)
        k.dma("g", W["cc2_in"][0:1, 0:16], scr[0:1, 0:16])
        g.wait_ge(k.sems["gd"], k.c["gd"])
        ci = g.collective_compute(
            "AllReduce", ALU.max, replica_groups=[list(range(NCORES))],
            ins=[W["cc2_in"].opt()], outs=[W["cc2_out"].opt()])
        k.inc(ci, "cc")
        k.wait(g, "cc", k.c["cc"])
        k.dma("g", scr[0:1, lo:lo + n], W["cc2_out"][0:1, lo:lo + n])
        gd = k.c["gd"]
        k.wait(v, "gd", gd)
        k.wait(s, "gd", gd)

    def bcast_cols(slots):
        k.wait(t, "v")
        k.wait(t, "s")
        for sl in slots:
            mi = t.matmul(k.psum[6][:, sl:sl + 1], lhsT=ones1f[0:1, :],
                          rhs=scr[0:1, sl:sl + 1], start=True, stop=True)
        tm = k.inc(mi, "t")
        k.wait(v, "t", tm)
        for sl in slots:
            v.tensor_copy(out=scb[:, sl:sl + 1], in_=k.psum[6][:, sl:sl + 1])
        k.mark(v, "v")

    pcol_to_scr(xnmax_col, MX)
    allreduce_slots(MX, 1)
    v.tensor_scalar(out=scr[0:1, SX:SX + 1], in0=scr[0:1, MX:MX + 1],
                    scalar1=1.0 / QM, scalar2=1e-12, op0=ALU.mult, op1=ALU.max)
    v.reciprocal(out=scr[0:1, IX:IX + 1], in_=scr[0:1, SX:SX + 1])
    k.mark(v, "v")
    bcast_cols([SX, IX])

    k.scope("q_wq" + uid)
    if first:
        with ExitStack() as stW:
            wbuf = k.sb("ff_wbuf" + uid, [128, 2 * IFF], F32, stW)
            sa = k.sb("ff_sa" + uid, [128, 96], F32, stW)
            sr = k.sb("ff_sr" + uid, [128, 96], F32, stW)
            qi = k.sb("ff_qi" + uid, [128, IFF], I16, stW)
            wqb = k.sb("ff_wqb" + uid, [128, IFF], BF16, stW)
            stg = k.sb("ff_stg" + uid, [128, 2 * 512], BF16, stW)
            for wname, nt, ncols in (("wgate", NIT, D), ("wup", NIT, D),
                                     ("wdown", NDT, IFF)):
                nblk = ncols // 32
                for ot in range(nt):
                    par = ot % 2
                    wb = wbuf[:, par * IFF:par * IFF + ncols]
                    k.wait(y, "v")
                    ld = k.dma("y", wb, W[wname][:, ot * ncols:(ot + 1) * ncols],
                               sem="xd")
                    k.wait(v, "xd", ld)
                    v.tensor_reduce(out=sa[:, 0:nblk],
                                    in_=wb.rearrange("p (b c) -> p b c", b=nblk),
                                    axis=AXX, op=ALU.max,
                                    apply_absolute_value=True)
                    v.tensor_scalar(out=sa[:, 0:nblk], in0=sa[:, 0:nblk],
                                    scalar1=1.0 / 7.0, scalar2=1e-12,
                                    op0=ALU.mult, op1=ALU.max)
                    v.reciprocal(out=sr[:, 0:nblk], in_=sa[:, 0:nblk])
                    sr3 = sr[:, 0:nblk].rearrange("p (b o) -> p b o", o=1) \
                        .broadcast_to([128, nblk, 32])
                    sa3 = sa[:, 0:nblk].rearrange("p (b o) -> p b o", o=1) \
                        .broadcast_to([128, nblk, 32])
                    w3 = wb.rearrange("p (b c) -> p b c", b=nblk)
                    q3 = qi[:, 0:ncols].rearrange("p (b c) -> p b c", b=nblk)
                    wq3 = wqb[:, 0:ncols].rearrange("p (b c) -> p b c", b=nblk)
                    v.tensor_tensor(out=q3, in0=w3, in1=sr3, op=ALU.mult)
                    v.tensor_tensor(out=wq3, in0=q3, in1=sa3, op=ALU.mult)
                    vq = k.mark(v, "v")
                    k.wait(t, "v", vq)
                    nic = ncols // 128
                    for icg in range(nic // 4):
                        if ot > 0 or icg > 0:
                            k.wait(t, "s")
                        for j in range(4):
                            ic = icg * 4 + j
                            mi = t.transpose(
                                k.psum_bf[:, j * 128:(j + 1) * 128],
                                in_=wqb[:, ic * 128:(ic + 1) * 128],
                                identity=W["identb"])
                        tm = k.inc(mi, "t")
                        k.wait(s, "t", tm)
                        k.wait(s, "xd")
                        spar = icg % 2
                        s.copy(out=stg[:, spar * 512:(spar + 1) * 512],
                               in_=k.psum_bf[:, 0:512])
                        sm = k.mark(s, "s")
                        k.wait(y, "s", sm)
                        k.wait(y, "xd")
                        k.dma("y", W["wqT_" + wname][
                            :, ot * ncols + icg * 512:ot * ncols + (icg + 1) * 512],
                            stg[:, spar * 512:(spar + 1) * 512], sem="xd")
    k.bar()

    k.scope("q_xq" + uid)
    stQ = ExitStack()
    xq = k.sb("ff_xq" + uid, [128, NDT * NTOK], BF16, stQ)
    with ExitStack() as stX:
        xin = k.sb("ff_xin" + uid, [128, 2 * NDT * 512], BF16, stX)
        xi16 = k.sb("ff_xi16" + uid, [128, NDT * 512], I16, stX)
        xnh3 = xn_hbm[:, :].rearrange("p (a n) -> p a n", a=NDT)
        def ld_xq(cidx):
            par = cidx % 2
            dst = xin[:, par * NDT * 512:(par + 1) * NDT * 512].rearrange(
                "p (a c) -> p a c", a=NDT)
            sem = "xa" if par == 0 else "xc"
            return (sem, k.dma("y", dst,
                               xnh3[:, :, cidx * 512:(cidx + 1) * 512],
                               sem=sem))
        lds = [ld_xq(0), ld_xq(1)]
        for cidx in range(NCH):
            par = cidx % 2
            xb = xin[:, par * NDT * 512:(par + 1) * NDT * 512]
            k.wait(v, lds[cidx][0], lds[cidx][1])
            v.tensor_scalar(out=xi16[:, :], in0=xb, scalar1=scb[:, IX:IX + 1],
                            scalar2=None, op0=ALU.mult)
            for dt in range(NDT):
                v.tensor_scalar(
                    out=xq[:, dt * NTOK + cidx * 512:dt * NTOK + (cidx + 1) * 512],
                    in0=xi16[:, dt * 512:(dt + 1) * 512],
                    scalar1=scb[:, SX:SX + 1], scalar2=None, op0=ALU.mult)
            k.mark(v, "v")
            if cidx + 2 < NCH:
                k.wait(y, "v", k.c["v"])
                lds.append(ld_xq(cidx + 2))
    k.bar()

    k.scope("q_mm" + uid)
    gm = k.sb("ff_gm" + uid, [128, 2 * NIT * NCH], F32, stQ)
    with ExitStack() as stM:
        wl = k.sb("ff_wl" + uid, [128, 2 * NDT * 128], BF16, stM)
        ev = k.sb("ff_ev" + uid, [128, 2 * 512], BF16, stM)
        for wi, (wname, park) in enumerate((("wgate", "pk_gate"),
                                            ("wup", "pk_up"))):
            wq_h = W["wqT_" + wname]
            for et in range(NIT):
                par = et % 2
                wsl = wl[:, par * 1024:(par + 1) * 1024]
                sem = "xa" if par == 0 else "xc"
                k.wait(y, "t", k.c["t"])
                ldw = k.dma("y", wsl, wq_h[:, et * 1024:(et + 1) * 1024],
                            sem=sem)
                k.wait(t, sem, ldw)
                for c4 in range(NCH):
                    ps = k.psum[c4 % 2]
                    if et > 0 or c4 >= 2 or wi > 0:
                        k.wait(t, "v")
                    for dt in range(NDT):
                        mi = t.matmul(ps[:, :],
                                      lhsT=wsl[:, dt * 128:(dt + 1) * 128],
                                      rhs=xq[:, dt * NTOK + c4 * 512:
                                             dt * NTOK + (c4 + 1) * 512],
                                      start=(dt == 0), stop=(dt == NDT - 1))
                    tm = k.inc(mi, "t")
                    k.wait(v, "t", tm)
                    epar = c4 % 2
                    k.wait(v, "xd")
                    v.tensor_copy(out=ev[:, epar * 512:(epar + 1) * 512],
                                  in_=ps[:, :])
                    v.tensor_reduce(
                        out=gm[:, (wi * NIT + et) * NCH + c4:
                               (wi * NIT + et) * NCH + c4 + 1],
                        in_=ps[:, :], axis=AXX, op=ALU.max,
                        apply_absolute_value=True)
                    vv = k.mark(v, "v")
                    k.wait(y, "v", vv)
                    k.dma("y", W[park][:, et * NTOK + c4 * 512:
                                       et * NTOK + (c4 + 1) * 512],
                          ev[:, epar * 512:(epar + 1) * 512], sem="xd")
            k.wait(v, "v")
            v.tensor_reduce(out=gm[:, 0:1] if False else scb[:, 15:16],
                            in_=gm[:, wi * NIT * NCH:(wi + 1) * NIT * NCH],
                            axis=AXX, op=ALU.max)
            k.mark(v, "v")
            pcol_to_scr(scb[:, 15:16], MG + wi)
    stQ.close()
    allreduce_slots(MG, 2)
    v.tensor_scalar(out=scr[0:1, SG:SG + 1], in0=scr[0:1, MG:MG + 1],
                    scalar1=1.0 / QM, scalar2=1e-12, op0=ALU.mult, op1=ALU.max)
    v.reciprocal(out=scr[0:1, IG:IG + 1], in_=scr[0:1, SG:SG + 1])
    v.tensor_scalar(out=scr[0:1, SU:SU + 1], in0=scr[0:1, MU:MU + 1],
                    scalar1=1.0 / QM, scalar2=1e-12, op0=ALU.mult, op1=ALU.max)
    v.reciprocal(out=scr[0:1, IU:IU + 1], in_=scr[0:1, SU:SU + 1])
    k.mark(v, "v")
    bcast_cols([SG, IG, SU, IU])
    k.bar()

    k.scope("q_p234" + uid)
    CT = 256
    NCK = NTOK // CT
    NC6 = NIT * CT
    pkg = W["pk_gate"].rearrange("p (e n) -> p e n", e=NIT)
    pku = W["pk_up"].rearrange("p (e n) -> p e n", e=NIT)
    pkq = W["pk_gq"].rearrange("p (e n) -> p e n", e=NIT)
    pk2 = W["pk_g2"].rearrange("p (e n) -> p e n", e=NIT)
    with ExitStack() as stP:
        pin = k.sb("ff_pin" + uid, [128, 2 * NC6], BF16, stP)
        pq = k.sb("ff_pq" + uid, [128, 2 * NC6], I16, stP)
        pu16 = k.sb("ff_pu16" + uid, [128, NC6], I16, stP)
        psg = k.sb("ff_psg" + uid, [128, NC6], U16, stP)
        pt2 = k.sb("ff_pt2" + uid, [128, NC6], F32, stP)
        pm = k.sb("ff_pm" + uid, [128, NCK], F32, stP)

        def sig_t2(qb):
            vv = k.mark(v, "v")
            k.wait(s, "v", vv)
            s.activation(out=pt2[:, :], in_=qb, func=AF.Sigmoid,
                         scale=scb[:, SG:SG + 1])
            sm = k.mark(s, "s")
            k.wait(v, "s", sm)
            v.tensor_scalar(out=psg[:, :], in0=pt2[:, :], scalar1=65536.0,
                            scalar2=None, op0=ALU.mult)
            v.tensor_tensor(out=pt2[:, :], in0=qb, in1=psg[:, :], op=ALU.mult)

        def ld_p2(ck):
            par = ck % 2
            sem = "xa" if par == 0 else "xc"
            return (sem, k.dma(
                "y", pin[:, par * NC6:(par + 1) * NC6].rearrange(
                    "p (e n) -> p e n", e=NIT),
                pkg[:, :, ck * CT:(ck + 1) * CT], sem=sem))
        lds = [ld_p2(0), ld_p2(1)]
        pid = {0: 0, 1: 0}
        for ck in range(NCK):
            par = ck % 2
            pb = pin[:, par * NC6:(par + 1) * NC6]
            qb = pq[:, par * NC6:(par + 1) * NC6]
            k.wait(v, lds[ck][0], lds[ck][1])
            k.wait(v, "xd" if par == 0 else "xb", pid[par])
            v.tensor_scalar(out=qb, in0=pb, scalar1=scb[:, IG:IG + 1],
                            scalar2=None, op0=ALU.mult)
            sig_t2(qb)
            v.tensor_reduce(out=pm[:, ck:ck + 1], in_=pt2[:, :], axis=AXX,
                            op=ALU.max, apply_absolute_value=True)
            vv = k.mark(v, "v")
            k.wait(s, "v", vv)
            pid[par] = k.dma("s", pkq[:, :, ck * CT:(ck + 1) * CT],
                             qb.rearrange("p (e n) -> p e n", e=NIT),
                             sem="xd" if par == 0 else "xb")
            if ck + 2 < NCK:
                k.wait(y, "v", vv)
                lds.append(ld_p2(ck + 2))
        k.wait(v, "v")
        v.tensor_reduce(out=scb[:, 15:16], in_=pm[:, :], axis=AXX, op=ALU.max)
        k.mark(v, "v")
        pcol_to_scr(scb[:, 15:16], M2)
        allreduce_slots(M2, 1)
        v.tensor_tensor(out=scr[0:1, 16:17], in0=scr[0:1, M2:M2 + 1],
                        in1=scr[0:1, SG:SG + 1], op=ALU.mult)
        v.tensor_scalar(out=scr[0:1, 17:18], in0=scr[0:1, 16:17],
                        scalar1=1.0 / (65536.0 * QM), scalar2=1e-12,
                        op0=ALU.mult, op1=ALU.max)
        v.reciprocal(out=scr[0:1, 18:19], in_=scr[0:1, 17:18])
        v.tensor_tensor(out=scr[0:1, C2:C2 + 1], in0=scr[0:1, SG:SG + 1],
                        in1=scr[0:1, 18:19], op=ALU.mult)
        v.tensor_scalar(out=scr[0:1, C2:C2 + 1], in0=scr[0:1, C2:C2 + 1],
                        scalar1=1.0 / 65536.0, scalar2=None, op0=ALU.mult)
        k.mark(v, "v")
        bcast_cols([C2])
        k.bar()

        def ld_p3(ck):
            par = ck % 2
            sem = "xa" if par == 0 else "xc"
            k.dma("y", pq[:, par * NC6:(par + 1) * NC6].rearrange(
                "p (e n) -> p e n", e=NIT),
                pkq[:, :, ck * CT:(ck + 1) * CT], sem=sem)
            return (sem, k.dma(
                "y", pin[:, par * NC6:(par + 1) * NC6].rearrange(
                    "p (e n) -> p e n", e=NIT),
                pku[:, :, ck * CT:(ck + 1) * CT], sem=sem))
        lds = [ld_p3(0), ld_p3(1)]
        pid = {0: 0, 1: 0}
        for ck in range(NCK):
            par = ck % 2
            qb = pq[:, par * NC6:(par + 1) * NC6]
            ub = pin[:, par * NC6:(par + 1) * NC6]
            k.wait(v, lds[ck][0], lds[ck][1])
            k.wait(v, "xd" if par == 0 else "xb", pid[par])
            sig_t2(qb)
            v.tensor_scalar(out=qb, in0=pt2[:, :],
                            scalar1=scb[:, C2:C2 + 1], scalar2=None,
                            op0=ALU.mult)
            v.tensor_scalar(out=pu16[:, :], in0=ub, scalar1=scb[:, IU:IU + 1],
                            scalar2=None, op0=ALU.mult)
            v.tensor_tensor(out=pt2[:, :], in0=qb, in1=pu16[:, :],
                            op=ALU.mult)
            v.tensor_reduce(out=pm[:, ck:ck + 1], in_=pt2[:, :], axis=AXX,
                            op=ALU.max, apply_absolute_value=True)
            vv = k.mark(v, "v")
            k.wait(s, "v", vv)
            pid[par] = k.dma("s", pk2[:, :, ck * CT:(ck + 1) * CT],
                             qb.rearrange("p (e n) -> p e n", e=NIT),
                             sem="xd" if par == 0 else "xb")
            if ck + 2 < NCK:
                k.wait(y, "v", vv)
                k.wait(y, "xd" if par == 0 else "xb", pid[par])
                lds.append(ld_p3(ck + 2))
        k.wait(v, "v")
        v.tensor_reduce(out=scb[:, 15:16], in_=pm[:, :], axis=AXX, op=ALU.max)
        k.mark(v, "v")
        pcol_to_scr(scb[:, 15:16], M3)
        allreduce_slots(M3, 1)
        v.tensor_tensor(out=scr[0:1, 19:20], in0=scr[0:1, 17:18],
                        in1=scr[0:1, SU:SU + 1], op=ALU.mult)
        v.tensor_tensor(out=scr[0:1, 16:17], in0=scr[0:1, M3:M3 + 1],
                        in1=scr[0:1, 19:20], op=ALU.mult)
        v.tensor_scalar(out=scr[0:1, S3:S3 + 1], in0=scr[0:1, 16:17],
                        scalar1=1.0 / QM, scalar2=1e-12,
                        op0=ALU.mult, op1=ALU.max)
        v.reciprocal(out=scr[0:1, 20:21], in_=scr[0:1, S3:S3 + 1])
        v.tensor_tensor(out=scr[0:1, C3:C3 + 1], in0=scr[0:1, 19:20],
                        in1=scr[0:1, 20:21], op=ALU.mult)
        k.mark(v, "v")
        bcast_cols([C3, S3])
        k.bar()

        obuf = k.sb("ff_ob" + uid, [128, 2 * NC6], BF16, stP)
        wdl = k.sb("ff_wdl" + uid, [128, 2 * NIT * 128], BF16, stP)
        wdq = W["wqT_wdown"]
        def ld_p4(ck):
            par = ck % 2
            sem = "xa" if par == 0 else "xc"
            k.dma("y", pq[:, par * NC6:(par + 1) * NC6].rearrange(
                "p (e n) -> p e n", e=NIT),
                pk2[:, :, ck * CT:(ck + 1) * CT], sem=sem)
            return (sem, k.dma(
                "y", pin[:, par * NC6:(par + 1) * NC6].rearrange(
                    "p (e n) -> p e n", e=NIT),
                pku[:, :, ck * CT:(ck + 1) * CT], sem=sem))
        lds = [ld_p4(0), ld_p4(1)]
        for ck in range(NCK):
            par = ck % 2
            g2 = pq[:, par * NC6:(par + 1) * NC6]
            ub = pin[:, par * NC6:(par + 1) * NC6]
            ob = obuf[:, par * NC6:(par + 1) * NC6]
            k.wait(v, lds[ck][0], lds[ck][1])
            k.wait(v, "t")
            v.tensor_scalar(out=pu16[:, :], in0=ub, scalar1=scb[:, IU:IU + 1],
                            scalar2=None, op0=ALU.mult)
            v.tensor_tensor(out=pt2[:, :], in0=g2, in1=pu16[:, :], op=ALU.mult)
            v.tensor_scalar(out=pu16[:, :], in0=pt2[:, :],
                            scalar1=scb[:, C3:C3 + 1], scalar2=None,
                            op0=ALU.mult)
            v.tensor_scalar(out=ob, in0=pu16[:, :], scalar1=scb[:, S3:S3 + 1],
                            scalar2=None, op0=ALU.mult)
            vv = k.mark(v, "v")
            if ck + 2 < NCK:
                k.wait(y, "v", vv)
                lds.append(ld_p4(ck + 2))
            for et in range(NDT):
                wpar = et % 2
                wsem = "xe" if wpar == 0 else "xf"
                wsl = wdl[:, wpar * NIT * 128:(wpar + 1) * NIT * 128]
                if ck > 0 or et >= 2:
                    k.wait(g, "t")
                ldw = k.dma("g", wsl, wdq[:, et * IFF:(et + 1) * IFF],
                            sem=wsem)
                k.wait(t, wsem, ldw)
                k.wait(t, "v", vv)
                ps = k.psum[4 + et % 2]
                if ck > 0 or et >= 2:
                    k.wait(t, "v")
                for ic in range(NIT):
                    mi = t.matmul(ps[:, 0:CT],
                                  lhsT=wsl[:, ic * 128:(ic + 1) * 128],
                                  rhs=ob[:, ic * CT:(ic + 1) * CT],
                                  start=(ic == 0), stop=(ic == NIT - 1))
                tm = k.inc(mi, "t")
                k.wait(v, "t", tm)
                rsl = R[:, et * NTOK + ck * CT:et * NTOK + (ck + 1) * CT]
                v.scalar_tensor_tensor(out=rsl, in0=ps[:, 0:CT], scalar=0.5,
                                       in1=rsl, op0=ALU.mult, op1=ALU.add)
                k.mark(v, "v")
    if dbg is not None:
        k.wait(v, "v")
        v.tensor_copy(out=dbg[0:1, :], in_=scr[0:1, :])
        k.mark(v, "v")
    k.bar()


DIN = 2048
NET = DIN // 128
LCH = 64
NBT = 16


def emit_l1_attn(k: K, R, xn_hbm, W, st, uid=""):
    t, v, s, g, y = k.t, k.v, k.s, k.g, k.y
    xnh3 = xn_hbm[:, :].rearrange("p (a n) -> p a n", a=NDT)
    xih = W["xi_hbm"].rearrange("p (a n) -> p a n", a=NET)
    zsh = W["zs_hbm"].rearrange("p (a n) -> p a n", a=NET)
    slh = W["sl_hbm"].rearrange("p (a n) -> p a n", a=NET)
    xph = W["xp_hbm"]
    g1h = W["g1_hbm"]

    Bin = k.sb("la_bin" + uid, [128, DIN], F32, st)
    Cin = k.sb("la_cin" + uid, [128, DIN], F32, st)
    abp = k.sb("la_abp" + uid, [128, 5 * LCH], F32, st)
    abc = k.sb("la_abc" + uid, [128, 8], F32, st)
    abrep = k.sb("la_abrep" + uid, [128, 32], F32, st)
    dcol = k.sb("la_dcol" + uid, [128, 16], F32, st)
    thc = k.sb("la_thc" + uid, [128, B * 8 * 16], F32, st)
    aec = k.sb("la_aec" + uid, [128, B * 8 * 16], F32, st)
    tri = k.sb("la_tri" + uid, [128, 128], BF16, st)
    kmask = k.sb("la_kmask" + uid, [128, LCH], BF16, st)
    taus = k.sb("la_tau" + uid, [128, 128], F32, st)
    ab1 = k.sb("la_ab1" + uid, [128, 1], F32, st)
    ones1f = k.sb("la_o1f" + uid, [1, 128], F32, st)
    prow = k.sb("la_prow" + uid, [1, 32], F32, st)
    v.memset(ones1f[:, :], 1.0)

    k.dma("g", tri[:, :], W["tri"][:, :])
    k.dma("g", kmask[:, :], W["kmask"][:, :])
    k.dma("y", taus[:, :], W["tau"][:, :])
    k.dma("y", ab1[:, :], W["abase"][:, :])
    k.dma("y", dcol[:, :], W["dcolh"][:, :])
    k.bar()

    k.scope("a_pat" + uid)
    s.activation(out=abc[:, 0:1], in_=ab1[:, 0:1], func=AF.Sigmoid)
    sm = k.mark(s, "s")
    k.wait(s, "s", sm)
    s.activation(out=abc[:, 1:2], in_=abc[:, 0:1], func=AF.Ln)
    sm = k.mark(s, "s")
    k.wait(s, "s", sm)
    s.activation(out=abc[:, 2:3], in_=abc[:, 1:2], func=AF.Exp, scale=64.0)
    s.activation(out=abc[:, 3:4], in_=abc[:, 1:2], func=AF.Exp, scale=256.0)
    sm = k.mark(s, "s")
    k.wait(v, "s", sm)
    v.tensor_scalar(out=abp[:, 4 * LCH:5 * LCH], in0=taus[:, 0:LCH],
                    scalar1=abc[:, 1:2], scalar2=None, op0=ALU.mult)
    vv = k.mark(v, "v")
    k.wait(s, "v", vv)
    s.activation(out=abp[:, 0:LCH], in_=abp[:, 4 * LCH:5 * LCH], func=AF.Exp,
                 scale=-1.0)
    s.activation(out=abp[:, LCH:2 * LCH], in_=abp[:, 4 * LCH:5 * LCH],
                 func=AF.Exp)
    sm = k.mark(s, "s")
    k.wait(v, "s", sm)
    v.tensor_scalar(out=abp[:, 2 * LCH:3 * LCH], in0=abp[:, LCH:2 * LCH],
                    scalar1=abc[:, 0:1], scalar2=None, op0=ALU.mult)
    v.tensor_scalar(out=abp[:, 4 * LCH:5 * LCH], in0=taus[:, LCH:2 * LCH],
                    scalar1=abc[:, 1:2], scalar2=None, op0=ALU.mult)
    vv = k.mark(v, "v")
    k.wait(s, "v", vv)
    s.activation(out=abp[:, 3 * LCH:4 * LCH], in_=abp[:, 4 * LCH:5 * LCH],
                 func=AF.Exp)
    k.mark(s, "s")
    k.wait(y, "s")
    wr = k.dma("y", W["colb"][0:1, 0:32], abc[0:16, 2:4], sem="xd")
    k.wait(y, "xd", wr)
    rd = k.dma("y", prow[0:1, :], W["colb"][0:1, 0:32], sem="xd")
    k.wait(t, "xd", rd)
    mi = t.matmul(k.psum[6][:, 0:16], lhsT=ones1f[0:1, :],
                  rhs=prow[0:1, 0:32].rearrange("o (p c) -> o (c p)", p=16)[
                      :, 0:16] if False else
                  prow[0:1, 0:32].rearrange("o (p c) -> o p c", p=16)[:, :, 0],
                  start=True, stop=True)
    mi = t.matmul(k.psum[6][:, 16:32], lhsT=ones1f[0:1, :],
                  rhs=prow[0:1, 0:32].rearrange("o (p c) -> o p c", p=16)[:, :, 1],
                  start=False, stop=True, skip_group_check=True)
    tm = k.inc(mi, "t")
    k.wait(v, "t", tm)
    v.tensor_copy(out=abrep[:, :], in_=k.psum[6][:, 0:32])
    k.mark(v, "v")
    k.bar()

    k.scope("a_ip" + uid)
    with ExitStack() as stP:
        wbuf = k.sb("la_wb" + uid, [128, NDT * DIN], BF16, stP)
        xnc = k.sb("la_xnc" + uid, [128, 2 * NDT * 512], BF16, stP)
        ev = k.sb("la_ev" + uid, [128, 2 * 512], BF16, stP)
        pid = {0: 0, 1: 0}
        for half in range(2):
            k.wait(g, "t", k.c["t"])
            hw = k.dma("g", wbuf[:, :],
                       W["ipw"].rearrange("p (a n) -> p a n", a=NDT)[
                           :, :, half * DIN:(half + 1) * DIN], sem="xf")
            k.wait(t, "xf", hw)
            for c4 in range(NCH):
                par = c4 % 2
                xb = xnc[:, par * NDT * 512:(par + 1) * NDT * 512]
                sem = "xa" if par == 0 else "xc"
                k.wait(y, "t", k.c["t"])
                ld = k.dma("y", xb.rearrange("p (a c) -> p a c", a=NDT),
                           xnh3[:, :, c4 * 512:(c4 + 1) * 512], sem=sem)
                k.wait(t, sem, ld)
                for et in range(NET):
                    ps = k.psum[et % 2]
                    if et >= 2 or c4 >= 1 or half > 0:
                        k.wait(t, "s")
                    for dt in range(NDT):
                        mi = t.matmul(
                            ps[:, :],
                            lhsT=wbuf[:, dt * DIN + half * 0 + et * 128:
                                      dt * DIN + (et + 1) * 128],
                            rhs=xb[:, dt * 512:(dt + 1) * 512],
                            start=(dt == 0), stop=(dt == NDT - 1))
                    tm = k.inc(mi, "t")
                    k.wait(s, "t", tm)
                    epar = et % 2
                    k.wait(s, "xd" if epar == 0 else "xb", pid[epar])
                    s.activation(out=ev[:, epar * 512:(epar + 1) * 512],
                                 in_=ps[:, :], func=AF.Silu)
                    k.mark(s, "s")
                    dsth = xih if half == 0 else zsh
                    pid[epar] = k.dma(
                        "s", dsth[:, half * 0 + et, c4 * 512:(c4 + 1) * 512],
                        ev[:, epar * 512:(epar + 1) * 512],
                        sem="xd" if epar == 0 else "xb")
    k.bar()

    k.scope("a_ap" + uid)
    with ExitStack() as stA:
        wbuf = k.sb("la_wb2" + uid, [128, NET * 1024], BF16, stA)
        bwc = k.sb("la_bwc" + uid, [128, 2 * NET * 16], BF16, stA)
        xic = k.sb("la_xic" + uid, [128, 2 * NET * 256], BF16, stA)
        slq = k.sb("la_slq" + uid, [128, 2 * 8 * 256], BF16, stA)
        acol = k.sb("la_acol" + uid, [128, 16], F32, stA)
        k.dma("y", acol[:, :], W["acolh"][:, :])
        k.dma("g", bwc[:, 0:NET * 16], W["bw"][:, :])
        k.dma("g", bwc[:, NET * 16:2 * NET * 16], W["cw"][:, :])
        k.wait(t, "gd")
        k.wait(s, "yd")
        spid = {0: 0, 1: 0}
        for half in range(2):
            k.wait(g, "t", k.c["t"])
            hw = k.dma("g", wbuf[:, :],
                       W["apw"].rearrange("p (a n) -> p a n", a=NET)[
                           :, :, half * 1024:(half + 1) * 1024], sem="xf")
            k.wait(t, "xf", hw)
            for c8 in range(8):
                par = c8 % 2
                xb = xic[:, par * NET * 256:(par + 1) * NET * 256]
                sem = "xa" if par == 0 else "xc"
                k.wait(y, "t", k.c["t"])
                ld = k.dma("y", xb.rearrange("p (a c) -> p a c", a=NET),
                           xih[:, :, c8 * 256:(c8 + 1) * 256], sem=sem)
                k.wait(t, sem, ld)
                spar = (half * 8 + c8) % 2
                sbase = spar * 2048
                k.wait(s, "xd" if spar == 0 else "xb", spid[spar])
                for eh in range(8):
                    et = half * 8 + eh
                    ps = k.psum[eh % 2]
                    if eh >= 2 or c8 >= 1 or half > 0:
                        k.wait(t, "s")
                    for dt in range(NET):
                        mi = t.matmul(
                            ps[:, 0:256],
                            lhsT=wbuf[:, dt * 1024 + eh * 128:
                                      dt * 1024 + (eh + 1) * 128],
                            rhs=xb[:, dt * 256:(dt + 1) * 256],
                            start=(dt == 0), stop=(dt == NET - 1))
                    tm = k.inc(mi, "t")
                    k.wait(s, "t", tm)
                    s.activation(out=slq[:, sbase + eh * 256:
                                         sbase + (eh + 1) * 256],
                                 in_=ps[:, 0:256], func=AF.Sigmoid,
                                 bias=acol[:, et:et + 1])
                    k.mark(s, "s")
                for q in range(4):
                    s.activation(out=slq[:, sbase + q * 512:
                                         sbase + (q + 1) * 512],
                                 in_=slq[:, sbase + q * 512:
                                         sbase + (q + 1) * 512], func=AF.Ln)
                k.mark(s, "s")
                spid[spar] = k.dma(
                    "s", slh[:, half * 8:half * 8 + 8,
                             c8 * 256:(c8 + 1) * 256],
                    slq[:, sbase:sbase + 2048].rearrange(
                        "p (a c) -> p a c", a=8),
                    sem="xd" if spar == 0 else "xb")
                if half == 0:
                    k.wait(t, "v")
                    for which, bank in ((0, 2), (1, 3)):
                        for dt in range(NET):
                            mi = t.matmul(
                                k.psum[bank][0:16, 0:256],
                                lhsT=bwc[:, which * NET * 16 + dt * 16:
                                         which * NET * 16 + (dt + 1) * 16],
                                rhs=xb[:, dt * 256:(dt + 1) * 256],
                                start=(dt == 0), stop=(dt == NET - 1))
                        tm = k.inc(mi, "t")
                        k.wait(v, "t", tm)
                        dst = Bin if which == 0 else Cin
                        v.tensor_copy(out=dst[0:16, c8 * 256:(c8 + 1) * 256],
                                      in_=k.psum[bank][0:16, 0:256])
                        k.mark(v, "v")
        k.wait(y, "v", k.c["v"])
        k.dma("y", Bin[64:80, :], Bin[0:16, :])
        k.dma("y", Cin[64:80, :], Cin[0:16, :])
    k.bar()

    k.scope("a_phA" + uid)
    NCHB = 4
    hincD = k.sb("la_hincD" + uid, [128, B * NCHB * 256], BF16, st)
    smry = k.sb("la_smry" + uid, [128, B * 272], F32, st)
    hin = k.sb("la_hin" + uid, [128, B * 256], F32, st)
    with ExitStack() as stS:
        slb = k.sb("la_slb" + uid, [128, 2 * DIN], BF16, stS)
        xib = k.sb("la_xib" + uid, [128, 2 * DIN], BF16, stS)
        sltT = k.sb("la_sltT" + uid, [128, DIN], BF16, stS)
        xinT = k.sb("la_xinT" + uid, [128, DIN], BF16, stS)
        exb = k.sb("la_exb" + uid, [128, 2 * DIN], BF16, stS)
        xpr = k.sb("la_xpr" + uid, [128, DIN], BF16, stS)
        erb = k.sb("la_erb" + uid, [128, 512], BF16, stS)
        bb2 = k.sb("la_bb2" + uid, [128, LCH], BF16, stS)
        lhsb = k.sb("la_lhsb" + uid, [128, 32], BF16, stS)
        hl = k.sb("la_hl" + uid, [128, 256], F32, stS)
        ecol = k.sb("la_ecol" + uid, [128, 64], F32, stS)
        stg = k.sb("la_stg" + uid, [128, 512], BF16, stS)
        v.memset(lhsb[:, :], 0.0)
        k.mark(v, "v")
        for bb in range(B):
            v.memset(hl[:, :], 0.0)
            k.mark(v, "v")
            for tt in range(2):
                bt = bb * 2 + tt
                par = bt % 2
                sem = "xa" if par == 0 else "xc"
                sem2 = "xb" if par == 0 else "xd"
                sb_ = slb[:, par * DIN:(par + 1) * DIN]
                xb_ = xib[:, par * DIN:(par + 1) * DIN]
                k.wait(y, "v", k.c["v"])
                k.wait(y, "t", k.c["t"])
                ld = k.dma("y", sb_.rearrange("p (a c) -> p a c", a=NET),
                           slh[:, :, bb * 256 + tt * 128:
                               bb * 256 + tt * 128 + 128], sem=sem)
                ld2 = k.dma("y", xb_.rearrange("p (a c) -> p a c", a=NET),
                            xih[:, :, bb * 256 + tt * 128:
                                bb * 256 + tt * 128 + 128], sem=sem2)
                k.wait(s, "xe")
                k.wait(v, "xe")
                k.wait(v, sem, ld)
                for ch01 in range(2):
                    c = tt * 2 + ch01
                    base = (bb * NCHB + c) * 16
                    v.tensor_reduce(
                        out=thc[:, base:base + 16],
                        in_=sb_.rearrange("p (a c) -> p a c", a=NET)[
                            :, :, ch01 * 64:ch01 * 64 + 32],
                        axis=AXX, op=ALU.add)
                    v.tensor_reduce(
                        out=aec[:, base:base + 16],
                        in_=sb_.rearrange("p (a c) -> p a c", a=NET)[
                            :, :, ch01 * 64:(ch01 + 1) * 64],
                        axis=AXX, op=ALU.add)
                vv = k.mark(v, "v")
                k.wait(t, "v", vv)
                k.wait(t, sem2, ld2)
                for src_t, dst_t in ((sb_, sltT), (xb_, xinT)):
                    for grp in range(4):
                        k.wait(t, "v")
                        for j in range(4):
                            et = grp * 4 + j
                            mi = t.transpose(
                                k.psum_bf[:, j * 128:(j + 1) * 128],
                                in_=src_t[:, et * 128:(et + 1) * 128],
                                identity=W["identb"])
                        tm = k.inc(mi, "t")
                        k.wait(v, "t", tm)
                        v.tensor_copy(out=dst_t[:, grp * 512:(grp + 1) * 512],
                                      in_=k.psum_bf[:, 0:512])
                        k.mark(v, "v")
                k.wait(t, "v")
                for ch01 in range(2):
                    off = ch01 * 64
                    for nch in range(4):
                        ps = k.psum[2 + ch01]
                        ps_s = k.psum[4 + ch01]
                        k.wait(t, "s")
                        rh = sltT[off:off + 64, nch * 512:(nch + 1) * 512]
                        t.matmul(ps[off:off + 64, :],
                                 lhsT=tri[off:off + 64, 0:64], rhs=rh,
                                 start=True, stop=True)
                        mi = t.matmul(ps_s[off:off + 64, :],
                                      lhsT=tri[off:off + 64, 64:128], rhs=rh,
                                      start=True, stop=True)
                        tm = k.inc(mi, "t")
                        k.wait(s, "t", tm)
                        k.wait(s, "v")
                        nsl = slice(nch * 512, (nch + 1) * 512)
                        s.activation(out=exb[off:off + 64, nsl],
                                     in_=ps[off:off + 64, :], func=AF.Exp,
                                     scale=-1.0)
                        s.activation(out=exb[off:off + 64, DIN + nch * 512:
                                             DIN + (nch + 1) * 512],
                                     in_=ps[off:off + 64, :], func=AF.Exp)
                        s.activation(out=erb[off:off + 64, 0:512],
                                     in_=ps_s[off:off + 64, :],
                                     func=AF.Exp)
                        sm = k.mark(s, "s")
                        k.wait(v, "s", sm)
                        v.tensor_tensor(out=xpr[off:off + 64, nsl],
                                        in0=xinT[off:off + 64, nsl],
                                        in1=erb[off:off + 64, 0:512],
                                        op=ALU.mult)
                        k.mark(v, "v")
                k.wait(v, "v")
                for q in range(4):
                    qs = slice(q * 512, (q + 1) * 512)
                    v.tensor_tensor(out=exb[:, qs], in0=xinT[:, qs],
                                    in1=exb[:, qs], op=ALU.mult)
                vv = k.mark(v, "v")
                k.wait(y, "v", vv)
                k.dma("y", xph[:, bt * DIN:(bt + 1) * DIN], exb[:, 0:DIN],
                      sem="xe")
                k.dma("y", g1h[:, bt * DIN:(bt + 1) * DIN],
                      exb[:, DIN:2 * DIN], sem="xe")
                for ch01 in range(2):
                    off = ch01 * 64
                    k.wait(v, "t")
                    v.tensor_tensor(
                        out=bb2[off:off + 16, :],
                        in0=Bin[off:off + 16,
                                bb * 256 + tt * 128 + ch01 * 64:
                                bb * 256 + tt * 128 + (ch01 + 1) * 64],
                        in1=abp[off:off + 16, 3 * LCH:4 * LCH], op=ALU.mult)
                    vv = k.mark(v, "v")
                    k.wait(t, "v", vv)
                    mi = t.transpose(k.psum_bf[off:off + 64, 768:784],
                                     in_=bb2[off:off + 16, :],
                                     identity=W["identb"][off:off + 16,
                                                          off:off + 16])
                    tm = k.inc(mi, "t")
                    k.wait(v, "t", tm)
                    v.tensor_copy(out=lhsb[off:off + 64,
                                           ch01 * 16:(ch01 + 1) * 16],
                                  in_=k.psum_bf[off:off + 64, 768:784])
                    k.mark(v, "v")
                vv = k.c["v"]
                k.wait(t, "v", vv)
                for nch in range(4):
                    ps6 = k.psum[6]
                    k.wait(t, "s")
                    mi = t.matmul(ps6[0:32, 0:512], lhsT=lhsb[:, :],
                                  rhs=xpr[:, nch * 512:(nch + 1) * 512],
                                  start=True, stop=True)
                    tm = k.inc(mi, "t")
                    k.wait(s, "t", tm)
                    s.copy(out=stg[0:32, 0:512], in_=ps6[0:32, 0:512])
                    sm = k.mark(s, "s")
                    k.wait(t, "s", sm)
                    k.wait(t, "v")
                    for j in range(4):
                        mi = t.transpose(k.psum_bf[:, 768 + j * 32:
                                                   768 + (j + 1) * 32],
                                         in_=stg[0:32, j * 128:(j + 1) * 128],
                                         identity=W["identb"][0:32, 0:32])
                    tm = k.inc(mi, "t")
                    k.wait(v, "t", tm)
                    for j in range(4):
                        et = nch * 4 + j
                        for ch01 in range(2):
                            c = tt * 2 + ch01
                            dst = hincD[:, ((bb * NCHB + c) * 16 + et) * 16:
                                        ((bb * NCHB + c) * 16 + et) * 16 + 16]
                            v.tensor_copy(
                                out=dst,
                                in_=k.psum_bf[:, 768 + j * 32 + ch01 * 16:
                                              768 + j * 32 + (ch01 + 1) * 16])
                    k.mark(v, "v")
                for ch01 in range(2):
                    c = tt * 2 + ch01
                    base = (bb * NCHB + c) * 16
                    k.wait(s, "v")
                    s.activation(out=ecol[:, 0:16], in_=aec[:, base:base + 16],
                                 func=AF.Exp)
                    sm = k.mark(s, "s")
                    k.wait(v, "s", sm)
                    v.tensor_tensor(
                        out=hl[:, :].rearrange("p (e n) -> p e n", e=16),
                        in0=hl[:, :].rearrange("p (e n) -> p e n", e=16),
                        in1=ecol[:, 0:16].rearrange("p (e o) -> p e o", o=1)
                        .broadcast_to([128, 16, 16]), op=ALU.mult)
                    v.tensor_tensor(
                        out=hl[:, :].rearrange("p (e n) -> p e n", e=16),
                        in0=hl[:, :].rearrange("p (e n) -> p e n", e=16),
                        in1=abrep[:, 0:16].rearrange("p (o n) -> p o n", o=1)
                        .broadcast_to([128, 16, 16]), op=ALU.mult)
                    v.tensor_tensor(out=hl[:, :], in0=hl[:, :],
                                    in1=hincD[:, base * 16:base * 16 + 256],
                                    op=ALU.add)
                    k.mark(v, "v")
            base0 = (bb * NCHB) * 16
            k.wait(v, "v")
            v.tensor_tensor(out=ecol[:, 16:32], in0=aec[:, base0:base0 + 16],
                            in1=aec[:, base0 + 16:base0 + 32], op=ALU.add)
            v.tensor_tensor(out=ecol[:, 32:48],
                            in0=aec[:, base0 + 32:base0 + 48],
                            in1=aec[:, base0 + 48:base0 + 64], op=ALU.add)
            v.tensor_tensor(out=ecol[:, 16:32], in0=ecol[:, 16:32],
                            in1=ecol[:, 32:48], op=ALU.add)
            vv = k.mark(v, "v")
            k.wait(s, "v", vv)
            s.activation(out=smry[:, bb * 272:bb * 272 + 16],
                         in_=ecol[:, 16:32], func=AF.Exp)
            k.mark(s, "s")
            k.wait(v, "s")
            v.tensor_copy(out=smry[:, bb * 272 + 16:bb * 272 + 272],
                          in_=hl[:, :])
            k.mark(v, "v")
    k.bar()

    k.scope("a_fold" + uid)
    foldm = k.sb("la_foldm" + uid, [128, 16], F32, st)
    smb = k.sb("la_smb" + uid, [128, B * 272], BF16, st)
    k.dma("y", foldm[:, :], W["foldmh"][:, :])
    k.wait(v, "yd")
    for bb in range(B):
        v.tensor_copy(out=smb[:, bb * 272:(bb + 1) * 272],
                      in_=smry[:, bb * 272:(bb + 1) * 272])
    vv = k.mark(v, "v")
    k.wait(g, "v", vv)
    k.dma("g", W["cc3_in"][:, :], smb[:, :])
    g.wait_ge(k.sems["gd"], k.c["gd"])
    ci = g.collective_compute(
        "AllGather", ALU.bypass, replica_groups=[list(range(NCORES))],
        ins=[W["cc3_in"].opt()], outs=[W["cc3_out"].opt()])
    k.inc(ci, "cc")
    k.wait(y, "cc", k.c["cc"])
    with ExitStack() as stF:
        rsm = k.sb("la_rsm" + uid, [128, B * 272], BF16, stF)
        dmt = k.sb("la_dmt" + uid, [128, 256], F32, stF)
        abn = k.sb("la_abn" + uid, [128, 256], F32, stF)
        v.memset(hin[:, :], 0.0)
        for e in range(16):
            v.tensor_copy(out=abn[:, e * 16:(e + 1) * 16],
                          in_=abrep[:, 16:32])
        k.mark(v, "v")
        for r in range(NCORES - 1):
            k.wait(y, "v", k.c["v"])
            ldr = k.dma("y", rsm[:, :],
                        W["cc3_out"][128 * r:128 * (r + 1), :], sem="xa")
            k.wait(v, "xa", ldr)
            for bb in range(B):
                ap_ = rsm[:, bb * 272:bb * 272 + 16]
                hf_ = rsm[:, bb * 272 + 16:bb * 272 + 272]
                hb = hin[:, bb * 256:(bb + 1) * 256]
                v.tensor_tensor(
                    out=dmt[:, :].rearrange("p (e n) -> p e n", e=16),
                    in0=abn[:, :].rearrange("p (e n) -> p e n", e=16),
                    in1=ap_.rearrange("p (e o) -> p e o", o=1)
                    .broadcast_to([128, 16, 16]), op=ALU.mult)
                v.tensor_scalar(out=dmt[:, :], in0=dmt[:, :],
                                scalar1=foldm[:, r:r + 1],
                                scalar2=foldm[:, 8 + r:9 + r],
                                op0=ALU.mult, op1=ALU.add)
                v.tensor_tensor(out=hb, in0=hb, in1=dmt[:, :], op=ALU.mult)
                v.scalar_tensor_tensor(out=hb, in0=hf_,
                                       scalar=foldm[:, r:r + 1], in1=hb,
                                       op0=ALU.mult, op1=ALU.add)
            k.mark(v, "v")
    k.bar()

    k.scope("a_phB" + uid)
    with ExitStack() as stB:
        opw = k.sb("la_opw" + uid, [128, NET * 1024], BF16, stB)
        xpb = k.sb("la_xpb" + uid, [128, 2 * DIN], BF16, stB)
        g1b = k.sb("la_g1b" + uid, [128, 2 * DIN], BF16, stB)
        bcc = k.sb("la_bcc" + uid, [128, 2 * LCH], BF16, stB)
        c3b = k.sb("la_c3b" + uid, [128, LCH], BF16, stB)
        ktb = k.sb("la_ktb" + uid, [128, LCH], BF16, stB)
        httr = k.sb("la_httr" + uid, [128, DIN], BF16, stB)
        ecol2 = k.sb("la_ecol2" + uid, [128, 48], F32, stB)
        htl = k.sb("la_htl" + uid, [128, 256], BF16, stB)
        yrow = k.sb("la_yrow" + uid, [128, DIN], BF16, stB)
        yT = k.sb("la_yT" + uid, [128, 2 * DIN], BF16, stB)
        zsb = k.sb("la_zsb" + uid, [128, DIN], BF16, stB)
        xrb = k.sb("la_xrb" + uid, [128, DIN], BF16, stB)
        ht = k.sb("la_ht" + uid, [128, 256], F32, stB)
        k.dma("g", opw[:, :], W["opw"][:, :], sem="gd")
        for bb in range(B):
            k.wait(v, "v")
            k.wait(v, "t")
            v.tensor_copy(out=ht[:, :], in_=hin[:, bb * 256:(bb + 1) * 256])
            k.mark(v, "v")
            for tt in range(2):
                bt = bb * 2 + tt
                par = tt % 2
                xp_ = xpb[:, par * DIN:(par + 1) * DIN]
                g1_ = g1b[:, par * DIN:(par + 1) * DIN]
                k.wait(y, "t", k.c["t"])
                k.wait(y, "v", k.c["v"])
                k.dma("y", xp_, xph[:, bt * DIN:(bt + 1) * DIN], sem="xa")
                l2 = k.dma("y", g1_, g1h[:, bt * DIN:(bt + 1) * DIN], sem="xa")
                for ch01 in range(2):
                    off = ch01 * 64
                    c = tt * 2 + ch01
                    base = (bb * NCHB + c) * 16
                    tksl = slice(bb * 256 + tt * 128 + ch01 * 64,
                                 bb * 256 + tt * 128 + (ch01 + 1) * 64)
                    k.wait(v, "t")
                    v.tensor_tensor(out=bcc[off:off + 16, 0:LCH],
                                    in0=Bin[off:off + 16, tksl],
                                    in1=abp[off:off + 16, 0:LCH], op=ALU.mult)
                    v.tensor_tensor(out=bcc[off:off + 16, LCH:2 * LCH],
                                    in0=Cin[off:off + 16, tksl],
                                    in1=abp[off:off + 16, LCH:2 * LCH],
                                    op=ALU.mult)
                    v.tensor_tensor(out=c3b[off:off + 16, :],
                                    in0=Cin[off:off + 16, tksl],
                                    in1=abp[off:off + 16, 2 * LCH:3 * LCH],
                                    op=ALU.mult)
                    vv = k.mark(v, "v")
                    k.wait(s, "v", vv)
                    s.activation(out=ecol2[:, 0:16],
                                 in_=thc[:, base:base + 16], func=AF.Exp)
                    s.activation(out=ecol2[:, 16:32],
                                 in_=aec[:, base:base + 16], func=AF.Exp)
                    sm = k.mark(s, "s")
                    k.wait(v, "s", sm)
                    v.tensor_tensor(
                        out=htl[:, :].rearrange("p (e n) -> p e n", e=16),
                        in0=ht[:, :].rearrange("p (e n) -> p e n", e=16),
                        in1=ecol2[:, 0:16].rearrange("p (e o) -> p e o", o=1)
                        .broadcast_to([128, 16, 16]), op=ALU.mult)
                    v.tensor_tensor(
                        out=ht[:, :].rearrange("p (e n) -> p e n", e=16),
                        in0=ht[:, :].rearrange("p (e n) -> p e n", e=16),
                        in1=ecol2[:, 16:32].rearrange("p (e o) -> p e o", o=1)
                        .broadcast_to([128, 16, 16]), op=ALU.mult)
                    v.tensor_tensor(
                        out=ht[:, :].rearrange("p (e n) -> p e n", e=16),
                        in0=ht[:, :].rearrange("p (e n) -> p e n", e=16),
                        in1=abrep[:, 0:16].rearrange("p (o n) -> p o n", o=1)
                        .broadcast_to([128, 16, 16]), op=ALU.mult)
                    v.tensor_tensor(out=ht[:, :], in0=ht[:, :],
                                    in1=hincD[:, base * 16:base * 16 + 256],
                                    op=ALU.add)
                    vv = k.mark(v, "v")
                    k.wait(t, "v", vv)
                    for grp in range(2):
                        k.wait(t, "s")
                        for j in range(8):
                            et = grp * 8 + j
                            mi = t.transpose(
                                k.psum_bf[off:off + 16,
                                          j * 128:(j + 1) * 128],
                                in_=htl[:, et * 16:(et + 1) * 16],
                                identity=W["identb"])
                        tm = k.inc(mi, "t")
                        k.wait(s, "t", tm)
                        s.copy(out=httr[off:off + 16,
                                        grp * 1024:(grp + 1) * 1024],
                               in_=k.psum_bf[off:off + 16, 0:1024])
                        k.mark(s, "s")
                    k.wait(t, "s")
                    k.wait(t, "v")
                    mi = t.matmul(k.psum[2 + ch01][off:off + 64, 0:64],
                                  lhsT=bcc[off:off + 16, 0:LCH],
                                  rhs=bcc[off:off + 16, LCH:2 * LCH],
                                  start=True, stop=True)
                    tm = k.inc(mi, "t")
                    k.wait(v, "t", tm)
                    v.tensor_tensor(out=ktb[off:off + 64, :],
                                    in0=k.psum[2 + ch01][off:off + 64, 0:64],
                                    in1=kmask[off:off + 64, :], op=ALU.mult)
                    k.mark(v, "v")
                k.wait(t, "xa", l2)
                k.wait(t, "v")
                for nch in range(4):
                    nsl = slice(nch * 512, (nch + 1) * 512)
                    k.wait(t, "v")
                    for ch01 in range(2):
                        off = ch01 * 64
                        psy = k.psum[4 + ch01]
                        t.matmul(psy[off:off + 64, 0:512],
                                 lhsT=ktb[off:off + 64, :],
                                 rhs=xp_[off:off + 64, nsl],
                                 start=True, stop=False,
                                 skip_group_check=True)
                    for ch01 in range(2):
                        off = ch01 * 64
                        psy = k.psum[4 + ch01]
                        mi = t.matmul(psy[off:off + 64, 0:512],
                                      lhsT=c3b[off:off + 16, :],
                                      rhs=httr[off:off + 16, nsl],
                                      start=False, stop=True,
                                      skip_group_check=True)
                    tm = k.inc(mi, "t")
                    k.wait(v, "t", tm)
                    for ch01 in range(2):
                        off = ch01 * 64
                        psy = k.psum[4 + ch01]
                        v.tensor_tensor(out=yrow[off:off + 64, nsl],
                                        in0=psy[off:off + 64, 0:512],
                                        in1=g1_[off:off + 64, nsl],
                                        op=ALU.mult)
                    k.mark(v, "v")
                k.wait(t, "v")
                for grp in range(4):
                    k.wait(t, "v")
                    for j in range(4):
                        et = grp * 4 + j
                        mi = t.transpose(k.psum_bf[:, j * 128:(j + 1) * 128],
                                         in_=yrow[:, et * 128:(et + 1) * 128],
                                         identity=W["identb"])
                    tm = k.inc(mi, "t")
                    k.wait(v, "t", tm)
                    v.tensor_copy(
                        out=yT[:, tt * DIN + grp * 512:tt * DIN + (grp + 1) * 512],
                        in_=k.psum_bf[:, 0:512])
                    k.mark(v, "v")
                k.wait(y, "v", k.c["v"])
                l3 = k.dma("y", zsb[:, :].rearrange("p (a c) -> p a c", a=NET),
                           zsh[:, :, bb * 256 + tt * 128:
                               bb * 256 + tt * 128 + 128], sem="xc")
                l4 = k.dma("y", xrb[:, :].rearrange("p (a c) -> p a c", a=NET),
                           xih[:, :, bb * 256 + tt * 128:
                               bb * 256 + tt * 128 + 128], sem="xc")
                k.wait(v, "xc", l4)
                ysl = yT[:, tt * DIN:(tt + 1) * DIN]
                v.tensor_tensor(
                    out=xrb[:, :].rearrange("p (e c) -> p e c", e=NET),
                    in0=xrb[:, :].rearrange("p (e c) -> p e c", e=NET),
                    in1=dcol[:, :].rearrange("p (e o) -> p e o", o=1)
                    .broadcast_to([128, NET, 128]), op=ALU.mult)
                v.tensor_tensor(out=ysl, in0=xrb[:, :], in1=ysl, op=ALU.add)
                v.tensor_tensor(out=ysl, in0=ysl, in1=zsb[:, :], op=ALU.mult)
                k.mark(v, "v")
            k.wait(t, "gd")
            k.wait(t, "v")
            for etp in range(NDT):
                ps = k.psum[etp % 2]
                if etp >= 2:
                    k.wait(t, "v")
                for tt2 in range(2):
                    for dt in range(NET):
                        mi = t.matmul(
                            ps[:, tt2 * 128:(tt2 + 1) * 128],
                            lhsT=opw[:, dt * 1024 + etp * 128:
                                     dt * 1024 + (etp + 1) * 128],
                            rhs=yT[:, tt2 * DIN + dt * 128:
                                   tt2 * DIN + (dt + 1) * 128],
                            start=(dt == 0 and tt2 == 0),
                            stop=(dt == NET - 1),
                            skip_group_check=True)
                tm = k.inc(mi, "t")
                k.wait(v, "t", tm)
                rsl = R[:, etp * NTOK + bb * 256:etp * NTOK + (bb + 1) * 256]
                v.tensor_tensor(out=rsl, in0=rsl, in1=ps[:, 0:256], op=ALU.add)
                k.mark(v, "v")
    k.bar()


def build_nc(stage_limit=STAGE_LIMIT):
    nc = bass.Bass("TRN2", target_bir_lowering=False)
    dram = {}

    def din(name, shape, dt=F32):
        dram[name] = nc.dram_tensor(name, shape, dt, kind="ExternalInput").ap()
        return dram[name]

    x_h = din("x", [128, NDT * NTOK])
    out_h = nc.dram_tensor("out", [128, NDT * NTOK], F32,
                           kind="ExternalOutput").ap()
    dbg_h = nc.dram_tensor("dbg", [1, 32], F32, kind="ExternalOutput").ap()
    xn_hbm = nc.dram_tensor("xn_hbm", [128, NDT * NTOK], BF16).ap()
    xh_hbm = nc.dram_tensor("xh_hbm", [128, 32 * 512], BF16).ap()
    cc_in = nc.dram_tensor("cc_kv_in", [128, 1536], BF16).ap()
    cc_out = nc.dram_tensor("cc_kv_out", [128 * NCORES, 1536], BF16).ap()
    for nm in ("wfT", "wiT", "wcT", "ufT", "uiT", "ucT", "woT", "uoT"):
        din("l0f_" + nm, [128, NDT * D])
    din("l0f_bias", [128, 4 * NDT])
    din("l0f_incol", [128, NDT])
    for nm in ("qwT", "gwT", "owT", "pwrT", "pwiT"):
        din("l0a_" + nm, [128, NDT * D])
    din("l0a_kwT", [128, NDT * 512])
    din("l0a_vwT", [128, NDT * 256])
    din("l0a_dw", [128, 2 * NDT * 7])
    din("l0a_dft", [128, NBIN])
    din("l0a_ib", [128, NBIN])
    din("l0a_mask", [128, 128])
    din("l0a_onehot", [128, 16])
    din("l0a_alpha", [1, 1])
    din("l1f_wgate", [128, 24 * 1024])
    din("l1f_wup", [128, 24 * 1024])
    din("l1f_wdown", [128, 8 * 3072])
    wq_g = nc.dram_tensor("wqT_gate", [128, 24 * 1024], BF16).ap()
    wq_u = nc.dram_tensor("wqT_up", [128, 24 * 1024], BF16).ap()
    wq_d = nc.dram_tensor("wqT_down", [128, 8 * 3072], BF16).ap()
    pk_gate = nc.dram_tensor("pk_gate", [128, 24 * NTOK], BF16).ap()
    pk_up = nc.dram_tensor("pk_up", [128, 24 * NTOK], BF16).ap()
    pk_gq = nc.dram_tensor("pk_gq", [128, 24 * NTOK], I16).ap()
    pk_g2 = nc.dram_tensor("pk_g2", [128, 24 * NTOK], I16).ap()
    din("l1a_ipw", [128, NDT * 4096])
    din("l1a_apw", [128, NET * DIN])
    din("l1a_acolh", [128, 16])
    din("l1a_bw", [128, NET * 16])
    din("l1a_cw", [128, NET * 16])
    din("l1a_opw", [128, NET * 1024])
    din("l1a_tau", [128, 128])
    din("l1a_abase", [128, 1])
    din("l1a_tri", [128, 128])
    din("l1a_kmask", [128, LCH])
    din("l1a_dcolh", [128, 16])
    din("l1a_foldmh", [128, 16])
    din("l1a_nwcol", [128, NDT])
    xi_hbm = nc.dram_tensor("xi_hbm", [128, NET * NTOK], BF16).ap()
    zs_hbm = nc.dram_tensor("zs_hbm", [128, NET * NTOK], BF16).ap()
    sl_hbm = nc.dram_tensor("sl_hbm", [128, NET * NTOK], BF16).ap()
    xp_hbm = nc.dram_tensor("xp_hbm", [128, NBT * DIN], BF16).ap()
    g1_hbm = nc.dram_tensor("g1_hbm", [128, NBT * DIN], BF16).ap()
    cc3_in = nc.dram_tensor("cc3_in", [128, B * 272], BF16).ap()
    cc3_out = nc.dram_tensor("cc3_out", [128 * NCORES, B * 272], BF16).ap()
    cc2_in = nc.dram_tensor("cc2_in", [1, 16], F32).ap()
    cc2_out = nc.dram_tensor("cc2_out", [1, 16], F32).ap()
    colb = nc.dram_tensor("colb", [1, 128], F32).ap()

    with ExitStack() as stack:
        block = stack.enter_context(nc.Block())
        k = K(nc, block, stack)
        R = k.sb("R", [128, NDT * NTOK], F32)
        ones16 = k.sb("ones16", [128, 16], BF16)
        ones1 = k.sb("ones1", [1, 128], BF16)
        dbg_sb = k.sb("dbg_sb", [1, 32], F32)
        k.v.memset(dbg_sb[0:1, :], 0.0)
        cst = k.sb("cst", [128, 2], F32)
        k.v.memset(cst[:, 0:1], 1e-5)
        k.v.memset(cst[:, 1:2], 1e-6)
        identb = k.sb("identb", [128, 128], BF16)
        colidx = k.sb("colidx", [128, 128], I32)
        pcol = k.sb("pcol", [128, 1], F32)
        pcoli = k.sb("pcoli", [128, 1], I32)
        k.g.iota(colidx[:, :], pattern=[[1, 128]], base=0, channel_multiplier=0)
        k.g.iota(pcoli[:, :], pattern=[[0, 1]], base=0, channel_multiplier=1)
        gm = k.mark(k.g, "gc")
        k.wait(k.v, "gc", gm)
        k.v.tensor_copy(out=pcol[:, :], in_=pcoli[:, :])
        k.v.tensor_scalar(out=identb[:, :], in0=colidx[:, :],
                          scalar1=pcol[:, 0:1], scalar2=None, op0=ALU.is_equal)

        for dt in range(NDT):
            k.dma("y", R[:, dt * NTOK:(dt + 1) * NTOK],
                  x_h[:, dt * NTOK:(dt + 1) * NTOK])
        k.v.memset(ones16[:, :], 1.0)
        k.v.memset(ones1[:, :], 1.0)
        k.mark(k.v, "v")
        k.bar()

        W0 = {"wfT": dram["l0f_wfT"], "wiT": dram["l0f_wiT"],
              "wcT": dram["l0f_wcT"], "ufT": dram["l0f_ufT"],
              "uiT": dram["l0f_uiT"], "ucT": dram["l0f_ucT"],
              "woT": dram["l0f_woT"], "uoT": dram["l0f_uoT"],
              "bias": dram["l0f_bias"], "incol": dram["l0f_incol"]}

        if stage_limit >= 1 and STAGE_SKIP < 1:
            k.scope("s1")
            with ExitStack() as st:
                emit_layernorm(k, R[:, :], xn_hbm, ones16[:, :], ones1[:, :],
                               st, mode="ln", eps_col=cst[:, 0:1], uid="a")
            with ExitStack() as st:
                emit_l0_ffn(k, R[:, :], xn_hbm, W0, st, uid="a")

        if stage_limit >= 2 and STAGE_SKIP < 2:
            k.scope("s2")
            WA = {nm: dram["l0a_" + nm] for nm in
                  ("qwT", "kwT", "vwT", "owT", "gwT", "pwrT", "pwiT", "dw",
                   "dft", "ib", "mask", "onehot", "alpha")}
            WA["cc_in"] = cc_in
            WA["cc_out"] = cc_out
            WA["ones1"] = ones1[:, :]
            WA["identb"] = identb[:, :]
            with ExitStack() as st:
                emit_layernorm(k, R[:, :], xn_hbm, ones16[:, :], ones1[:, :],
                               st, mode="ln", eps_col=cst[:, 0:1], uid="b")
            with ExitStack() as st:
                emit_l0_attn(k, R[:, :], xn_hbm, xh_hbm, WA, st, uid="b")

        if stage_limit >= 3 and STAGE_SKIP < 3:
            k.scope("s3")
            with ExitStack() as st:
                emit_layernorm(k, R[:, :], xn_hbm, ones16[:, :], ones1[:, :],
                               st, mode="ln", eps_col=cst[:, 0:1], uid="c")
            with ExitStack() as st:
                emit_l0_ffn(k, R[:, :], xn_hbm, W0, st, uid="c")

        WF = {"wgate": dram["l1f_wgate"], "wup": dram["l1f_wup"],
              "wdown": dram["l1f_wdown"], "wqT_wgate": wq_g,
              "wqT_wup": wq_u, "wqT_wdown": wq_d,
              "pk_gate": pk_gate, "pk_up": pk_up, "pk_gq": pk_gq,
              "pk_g2": pk_g2, "cc2_in": cc2_in, "cc2_out": cc2_out,
              "colb": colb, "identb": identb[:, :]}
        if stage_limit >= 4 and STAGE_SKIP < 4:
            k.scope("s4")
            with ExitStack() as st:
                xnmax = k.sb("xnmax4", [128, 1], F32, st)
                with ExitStack() as st2:
                    emit_layernorm(k, R[:, :], xn_hbm, ones16[:, :],
                                   ones1[:, :], st2, mode="ln",
                                   eps_col=cst[:, 0:1], uid="d",
                                   absmax_out=xnmax[:, 0:1])
                emit_l1_ffn(k, R[:, :], xn_hbm, WF, st, xnmax[:, 0:1],
                            first=True, uid="d", dbg=dbg_sb)

        if stage_limit >= 5 and STAGE_SKIP < 5:
            k.scope("s5")
            WA5 = {"ipw": dram["l1a_ipw"], "apw": dram["l1a_apw"],
                   "acolh": dram["l1a_acolh"], "bw": dram["l1a_bw"],
                   "cw": dram["l1a_cw"], "opw": dram["l1a_opw"],
                   "tau": dram["l1a_tau"], "abase": dram["l1a_abase"],
                   "tri": dram["l1a_tri"], "kmask": dram["l1a_kmask"],
                   "dcolh": dram["l1a_dcolh"], "foldmh": dram["l1a_foldmh"],
                   "colb": colb, "identb": identb[:, :],
                   "xi_hbm": xi_hbm, "zs_hbm": zs_hbm, "sl_hbm": sl_hbm,
                   "xp_hbm": xp_hbm, "g1_hbm": g1_hbm,
                   "cc3_in": cc3_in, "cc3_out": cc3_out}
            with ExitStack() as st:
                nwc = k.sb("l1a_nwc", [128, NDT], F32, st)
                k.dma("y", nwc[:, :], dram["l1a_nwcol"][:, :])
                with ExitStack() as st2:
                    emit_layernorm(k, R[:, :], xn_hbm, ones16[:, :],
                                   ones1[:, :], st2, mode="lnrms",
                                   normw_col=nwc[:, :], eps_col=cst[:, 0:1],
                                   eps2_col=cst[:, 1:2], uid="e")
                emit_l1_attn(k, R[:, :], xn_hbm, WA5, st, uid="e")

        if stage_limit >= 6 and STAGE_SKIP < 6:
            k.scope("s6")
            with ExitStack() as st:
                xnmax = k.sb("xnmax6", [128, 1], F32, st)
                with ExitStack() as st2:
                    emit_layernorm(k, R[:, :], xn_hbm, ones16[:, :],
                                   ones1[:, :], st2, mode="ln",
                                   eps_col=cst[:, 0:1], uid="f",
                                   absmax_out=xnmax[:, 0:1])
                emit_l1_ffn(k, R[:, :], xn_hbm, WF, st, xnmax[:, 0:1],
                            first=False, uid="f")

        k.bar()
        for dt in range(NDT):
            k.dma("y", out_h[:, dt * NTOK:(dt + 1) * NTOK],
                  R[:, dt * NTOK:(dt + 1) * NTOK])
        k.dma("y", dbg_h[0:1, :], dbg_sb[0:1, :])
        k.wait(k.y, "yd")
        k.wait(k.g, "yd")
        k.finalize()
    return nc


def _dtmajor(w):
    Dd, E = w.shape
    return np.ascontiguousarray(
        w.reshape(Dd // 128, 128, E).transpose(1, 0, 2).reshape(
            128, (Dd // 128) * E))


def _coltile(vec):
    E = vec.shape[0]
    return np.ascontiguousarray(vec.reshape(E // 128, 128).T)


def _prep_common(inputs):
    d = {}
    for nm in ("wf", "wi", "wc", "uf", "ui", "uc", "wo", "uo"):
        d["l0f_" + nm + "T"] = _dtmajor(np.ascontiguousarray(
            inputs["L0_ffn_" + nm].T))
    d["l0f_bias"] = np.concatenate(
        [_coltile(inputs["L0_ffn_b" + g]) for g in ("f", "i", "c", "o")], axis=1)
    d["l0f_incol"] = _coltile(inputs["L0_ffn_in_c"])
    d["l0a_qwT"] = _dtmajor(np.ascontiguousarray(inputs["L0_attn_q_w"].T))
    kw2 = np.repeat(inputs["L0_attn_k_w"].reshape(4, 1, 64, D), 2,
                    axis=1).reshape(512, D)
    d["l0a_kwT"] = _dtmajor(np.ascontiguousarray(kw2.T))
    d["l0a_vwT"] = _dtmajor(np.ascontiguousarray(inputs["L0_attn_v_w"].T))
    d["l0a_owT"] = _dtmajor(np.ascontiguousarray(inputs["L0_attn_o_w"].T))
    d["l0a_gwT"] = _dtmajor(np.ascontiguousarray(inputs["L0_attn_gate_w"].T))
    d["l0a_pwrT"] = _dtmajor(np.ascontiguousarray(inputs["L0_attn_pw_r"].T))
    d["l0a_pwiT"] = _dtmajor(np.ascontiguousarray(inputs["L0_attn_pw_i"].T))
    dwr = _dtmajor(inputs["L0_attn_dw_r"].reshape(1024, 7))
    dwi = _dtmajor(inputs["L0_attn_dw_i"].reshape(1024, 7))
    d["l0a_dw"] = np.concatenate([dwr, dwi], axis=1)
    tt = np.arange(64.0)
    cols = [np.cos(2 * np.pi * m * tt / 64.0) for m in range(33)]
    cols += [-np.sin(2 * np.pi * m * tt / 64.0) for m in range(1, 32)]
    dft = np.stack(cols, axis=1).astype(np.float32)
    d["l0a_dft"] = np.concatenate([dft, dft], axis=0)
    ib = np.zeros((64, 64), np.float32)
    for j in range(64):
        vv = np.zeros(33, complex)
        if j <= 32:
            vv[j] = 1.0
        else:
            vv[j - 32] = 1j
        ib[j] = np.fft.irfft(vv, 64).astype(np.float32)
    d["l0a_ib"] = np.concatenate([ib, ib], axis=0)
    oh = np.zeros((128, 16), np.float32)
    oh[0:4] = np.repeat(np.eye(4, dtype=np.float32), 4, axis=1)
    oh[64:68] = oh[0:4]
    d["l0a_onehot"] = oh
    d["l0a_alpha"] = inputs["L0_attn_alpha_raw"].reshape(1, 1)
    d["l1a_ipw"] = _dtmajor(inputs["L1_attn_in_proj_w"])
    d["l1a_apw"] = _dtmajor(inputs["L1_attn_A_proj_w"])
    d["l1a_acolh"] = _coltile(inputs["L1_attn_A_proj_b"])
    d["l1a_bw"] = _dtmajor(inputs["L1_attn_B_proj_w"])
    d["l1a_cw"] = _dtmajor(inputs["L1_attn_C_proj_w"])
    d["l1a_opw"] = _dtmajor(inputs["L1_attn_out_proj_w"])
    d["l1a_dcolh"] = _coltile(inputs["L1_attn_D"])
    d["l1a_nwcol"] = _coltile(inputs["L1_attn_norm_w"])
    d["l1a_abase"] = np.ascontiguousarray(
        np.tile(np.asarray(inputs["L1_attn_A_base"], np.float32),
                8).reshape(128, 1))
    tauv = np.concatenate([np.arange(64.0),
                           63.0 - np.arange(64.0)]).astype(np.float32)
    d["l1a_tau"] = np.ascontiguousarray(np.tile(tauv[None, :], (128, 1)))
    jj, ss = np.meshgrid(np.arange(64), np.arange(64), indexing="ij")
    M1 = (jj <= ss).astype(np.float32) - (jj <= 31).astype(np.float32)
    M2 = (jj > ss).astype(np.float32)
    d["l1a_tri"] = np.ascontiguousarray(
        np.tile(np.concatenate([M1, M2], axis=1), (2, 1)))
    d["l1a_kmask"] = np.ascontiguousarray(
        np.tile((jj <= ss).astype(np.float32), (2, 1)))
    d["l1f_wgate"] = _dtmajor(inputs["L1_ffn_w_gate"])
    d["l1f_wup"] = _dtmajor(inputs["L1_ffn_w_up"])
    d["l1f_wdown"] = _dtmajor(inputs["L1_ffn_w_down"])
    return d




def _core_inputs(ci, x, common):
    sl = x[:, ci * TLOC:(ci + 1) * TLOC, :]
    xT = np.ascontiguousarray(sl.transpose(2, 0, 1).reshape(D, NTOK))
    m = {"x": _dtmajor(xT)}
    m.update(common)
    kchunk = np.arange(128) // 4
    mrows = np.stack(
        [np.where(kchunk <= 4 * ci + c, 0.0, -1e30).astype(np.float32)
         for c in range(4)], axis=0)
    mk = np.zeros((128, 128), np.float32)
    mk[0:4] = mrows
    mk[64:68] = mrows
    m["l0a_mask"] = mk
    fm = np.zeros((128, 16), np.float32)
    for r in range(8):
        fm[:, r] = 1.0 if r < ci else 0.0
        fm[:, 8 + r] = 1.0 - fm[0, r]
    m["l1a_foldmh"] = fm
    return m

def kernel(**inputs):
    inputs = {k_: np.asarray(v, dtype=np.float32) for k_, v in inputs.items()}
    x = inputs["L0_attn_hidden_states"]
    common = _prep_common(inputs)
    nc = build_nc(STAGE_LIMIT)
    in_maps = []
    for ci in range(NCORES):
        in_maps.append(_core_inputs(ci, x, common))
    res = run_bass_kernel_spmd(nc, in_maps, core_ids=list(range(NCORES)))
    full = np.empty((B, 2048, D), dtype=np.float32)
    for ci in range(NCORES):
        o = np.asarray(res.results[ci]["out"]).reshape(128, NDT, B, TLOC)
        full[:, ci * TLOC:(ci + 1) * TLOC, :] = (
            o.transpose(2, 3, 1, 0).reshape(B, TLOC, D))
    return full


# version 05
# speedup vs baseline: 1.1843x; 1.1076x over previous
import math
from contextlib import ExitStack

import numpy as np
import concourse.bass as bass
import concourse.mybir as mybir
from concourse.bass_utils import run_bass_kernel_spmd

F32 = mybir.dt.float32
BF16 = mybir.dt.bfloat16
I16 = mybir.dt.int16
U16 = mybir.dt.uint16
I32 = mybir.dt.int32
AF = mybir.ActivationFunctionType
ALU = mybir.AluOpType
AXX = mybir.AxisListType.X

NCORES = 8
D = 1024
NDT = D // 128
B = 8
TLOC = 256
NTOK = B * TLOC
NCH = NTOK // 512
STAGE_LIMIT = 6
STAGE_SKIP = 0
ATTN_PHASES = 6
ATTN_F_MODE = 3


class _Pending:
    def __init__(self):
        self.incs = []

    def then_inc(self, sem, amt=1):
        self.incs.append((sem, amt))
        return self


_NO_DRAIN = {"wait_ge", "nop", "drain", "dma_start", "dma_start_transpose",
             "sem_inc", "sem_clear", "collective_compute", "wait_op"}


class _EngRec:
    def __init__(self, prog, auto_drain=False):
        self._prog = prog
        self._auto = auto_drain

    def __getattr__(self, name):
        if name.startswith("_"):
            raise AttributeError(name)
        def call(*a, **kw):
            p = _Pending()
            self._prog.append((name, a, kw, p))
            if self._auto and name not in _NO_DRAIN:
                self._prog.append(("drain", (), {}, _Pending()))
            return p
        return call


class K:
    def __init__(self, nc, block, stack):
        self.nc = nc
        self.block = block
        self.stack = stack
        self.prog = {n: [] for n in "tvsgy"}
        self.t = _EngRec(self.prog["t"])
        self.v = _EngRec(self.prog["v"], auto_drain=True)
        self.s = _EngRec(self.prog["s"], auto_drain=True)
        self.g = _EngRec(self.prog["g"], auto_drain=True)
        self.y = _EngRec(self.prog["y"])
        self.sems = {}
        self.c = {}
        for name in ("t", "v", "s", "gc", "yd", "gd", "sd", "cc", "xa", "xb", "xc", "xd", "xe", "xf", "xg", "xh"):
            self.sems[name] = stack.enter_context(nc.semaphore("sem_" + name))
            self.c[name] = 0
        self.psum = [stack.enter_context(
            nc.psum_tensor(f"psbank{i}", [128, 512], F32)) for i in range(7)]
        self.psum_bf = stack.enter_context(
            nc.psum_tensor("psbank_bf", [128, 1024], BF16))

    def scope(self, name):
        for en in "tvsgy":
            self.prog[en].append(("SCOPE", name, None, None))

    def finalize(self):
        secs = {"t": self.block.tensor, "v": self.block.vector,
                "s": self.block.scalar, "g": self.block.gpsimd,
                "y": self.block.sync}
        for en, entries in self.prog.items():
            def f(e, entries=entries):
                open_sc = None
                for (name, a, kw, p) in entries:
                    if name == "SCOPE":
                        if open_sc is not None:
                            self.nc.leave_named_scope(open_sc[0], open_sc[1], False)
                        sid, _ = self.nc.enter_named_scope(a, False)
                        open_sc = (a, sid)
                        continue
                    r = getattr(e, name)(*a, **kw)
                    for (sem, amt) in p.incs:
                        r.then_inc(sem, amt)
                if open_sc is not None:
                    self.nc.leave_named_scope(open_sc[0], open_sc[1], False)
            secs[en](f)

    def inc(self, instr, sem, amt=1):
        instr.then_inc(self.sems[sem], amt)
        self.c[sem] += amt
        return self.c[sem]

    def mark(self, eng, sem):
        return self.inc(eng.drain(), sem)

    def wait(self, eng, sem, val=None):
        eng.wait_ge(self.sems[sem], self.c[sem] if val is None else val)

    def dma(self, queue, out, in_, sem=None):
        eng = {"y": self.y, "g": self.g, "s": self.s}[queue]
        if sem is None:
            sem = {"y": "yd", "g": "gd", "s": "sd"}[queue]
        instr = eng.dma_start(out=out, in_=in_)
        return self.inc(instr, sem, 16)

    def bar(self):
        engs = [("t", self.t), ("v", self.v), ("s", self.s), ("gc", self.g)]
        for sem, e in engs:
            self.mark(e, sem)
        snap = dict(self.c)
        for _, e in engs + [(None, self.y)]:
            for sem in ("t", "v", "s", "gc", "yd", "gd", "sd", "xa", "xb", "xc", "xd", "xe", "xf", "xg", "xh"):
                if snap[sem] > 0:
                    e.wait_ge(self.sems[sem], snap[sem])

    def sb(self, name, shape, dt=F32, stack=None):
        return (stack or self.stack).enter_context(
            self.nc.sbuf_tensor(name, shape, dt))


def emit_layernorm(k: K, R, xn_hbm, ones16, ones1, st, mode="ln",
                   normw_col=None, eps_col=None, eps2_col=None, uid="",
                   absmax_out=None):
    t, v, s = k.t, k.v, k.s
    if absmax_out is not None:
        am4 = k.sb("ln_am4" + uid, [128, NCH], F32, st)
    if mode == "lnrms":
        lnt = k.sb("ln_lnt" + uid, [1, NTOK], F32, st)
    xbf = k.sb("ln_xbf" + uid, [128, NDT * 512], BF16, st)
    xsq = k.sb("ln_xsq" + uid, [128, NDT * 512], BF16, st)
    stats = k.sb("ln_stats" + uid, [16, 2 * NTOK], F32, st)
    S1sq = k.sb("ln_s1sq" + uid, [1, NTOK], F32, st)
    brow = k.sb("ln_brow" + uid, [1, 4 * NTOK], BF16, st)
    xnc = k.sb("ln_xnc" + uid, [128, 2 * NDT * 512], BF16, st)
    tmp = k.sb("ln_tmp" + uid, [128, 512], F32, st)
    if mode == "lnrms":
        tmp2 = k.sb("ln_tmp2" + uid, [128, 512], F32, st)

    k.scope("ln" + uid)
    k.bar()
    for cidx in range(NCH):
        csl = [slice(dt * NTOK + cidx * 512, dt * NTOK + (cidx + 1) * 512)
               for dt in range(NDT)]
        if cidx > 0:
            k.wait(s, "t")
        for dt in range(NDT):
            if mode != "rms":
                s.copy(out=xbf[:, dt * 512:(dt + 1) * 512], in_=R[:, csl[dt]])
            s.activation(out=xsq[:, dt * 512:(dt + 1) * 512], in_=R[:, csl[dt]],
                         func=AF.Square)
        sr = k.mark(s, "s")
        k.wait(t, "s", sr)
        srcs = ([(0, xbf), (1, xsq)] if mode != "rms" else [(1, xsq)])
        for which, src in srcs:
            ps = k.psum[4 + which]
            if cidx > 0:
                k.wait(t, "v")
            for dt in range(NDT):
                mi = t.matmul(ps[0:16, :], lhsT=ones16,
                              rhs=src[:, dt * 512:(dt + 1) * 512],
                              start=(dt == 0), stop=(dt == NDT - 1))
            mm = k.inc(mi, "t")
            k.wait(v, "t", mm)
            v.tensor_copy(out=stats[0:16, which * NTOK + cidx * 512:
                                    which * NTOK + (cidx + 1) * 512],
                          in_=ps[0:16, :])
            k.mark(v, "v")

    S1r = stats[0:1, 0:NTOK]
    Qr = stats[0:1, NTOK:2 * NTOK]
    RSb = brow[0:1, 0:NTOK]
    NMb = brow[0:1, NTOK:2 * NTOK]
    RS2b = brow[0:1, 2 * NTOK:3 * NTOK]
    NM2b = brow[0:1, 3 * NTOK:4 * NTOK]
    k.bar()
    if mode != "rms":
        v.scalar_tensor_tensor(out=S1sq[0:1, :], in0=S1r, scalar=-1.0 / D,
                               in1=S1r, op0=ALU.mult, op1=ALU.mult)
        v.tensor_tensor(out=Qr, in0=Qr, in1=S1sq[0:1, :], op=ALU.add)
    vh = k.mark(v, "v")
    k.wait(s, "v", vh)
    if mode == "lnrms":
        s.activation(out=lnt[0:1, :], in_=Qr, func=AF.Ln, scale=1.0 / D,
                     bias=eps_col[0:1, 0:1])
        s.activation(out=S1sq[0:1, :], in_=lnt[0:1, :], func=AF.Exp,
                     scale=-1.0)
        sh = k.mark(s, "s")
        k.wait(v, "s", sh)
        v.scalar_tensor_tensor(out=S1sq[0:1, :], in0=Qr, scalar=1.0 / D,
                               in1=S1sq[0:1, :], op0=ALU.mult, op1=ALU.mult)
        vh = k.mark(v, "v")
        k.wait(s, "v", vh)
        s.activation(out=S1sq[0:1, :], in_=S1sq[0:1, :], func=AF.Ln,
                     bias=eps2_col[0:1, 0:1])
        sh = k.mark(s, "s")
        k.wait(v, "s", sh)
        v.tensor_tensor(out=Qr, in0=lnt[0:1, :], in1=S1sq[0:1, :], op=ALU.add)
        vh = k.mark(v, "v")
        k.wait(s, "v", vh)
        s.activation(out=Qr, in_=Qr, func=AF.Exp, scale=-0.5)
        s.copy(out=RSb, in_=Qr)
        s.activation(out=lnt[0:1, :], in_=lnt[0:1, :], func=AF.Exp, scale=-0.5)
        s.copy(out=RS2b, in_=lnt[0:1, :])
        sh = k.mark(s, "s")
        k.wait(v, "s", sh)
        v.scalar_tensor_tensor(out=lnt[0:1, :], in0=S1r, scalar=-1.0 / D,
                               in1=lnt[0:1, :], op0=ALU.mult, op1=ALU.mult)
        v.scalar_tensor_tensor(out=S1r, in0=S1r, scalar=-1.0 / D, in1=Qr,
                               op0=ALU.mult, op1=ALU.mult)
        vh2 = k.mark(v, "v")
        k.wait(s, "v", vh2)
        s.copy(out=NM2b, in_=lnt[0:1, :])
        s.copy(out=NMb, in_=S1r)
        k.mark(s, "s")
    else:
        s.activation(out=Qr, in_=Qr, func=AF.Ln, scale=1.0 / D,
                     bias=eps_col[0:1, 0:1])
        s.activation(out=Qr, in_=Qr, func=AF.Exp, scale=-0.5)
        s.copy(out=RSb, in_=Qr)
        sh = k.mark(s, "s")
    if mode == "ln":
        k.wait(v, "s", sh)
        v.scalar_tensor_tensor(out=S1r, in0=S1r, scalar=-1.0 / D, in1=Qr,
                               op0=ALU.mult, op1=ALU.mult)
        vh2 = k.mark(v, "v")
        k.wait(s, "v", vh2)
        s.copy(out=NMb, in_=S1r)
        k.mark(s, "s")
    k.bar()

    for cidx in range(NCH):
        par = cidx % 2
        if mode == "lnrms":
            ps_r, ps_m = k.psum[3], k.psum[4]
            ps_r2, ps_m2 = k.psum[5], k.psum[6]
            if cidx > 0:
                k.wait(t, "v")
            csl2 = slice(cidx * 512, (cidx + 1) * 512)
            t.matmul(ps_r[:, :], lhsT=ones1, rhs=RSb[:, csl2],
                     start=True, stop=True)
            t.matmul(ps_m[:, :], lhsT=ones1, rhs=NMb[:, csl2],
                     start=True, stop=True)
            t.matmul(ps_r2[:, :], lhsT=ones1, rhs=RS2b[:, csl2],
                     start=True, stop=True)
            mi = t.matmul(ps_m2[:, :], lhsT=ones1, rhs=NM2b[:, csl2],
                          start=True, stop=True)
        else:
            ps_r = k.psum[3 + par * 2]
            ps_m = k.psum[4 + par * 2]
            if cidx >= 2:
                k.wait(t, "v")
            mi = t.matmul(ps_r[:, :], lhsT=ones1,
                          rhs=RSb[:, cidx * 512:(cidx + 1) * 512],
                          start=True, stop=True)
            if mode != "rms":
                mi = t.matmul(ps_m[:, :], lhsT=ones1,
                              rhs=NMb[:, cidx * 512:(cidx + 1) * 512],
                              start=True, stop=True)
        mm = k.inc(mi, "t")
        k.wait(v, "t", mm)
        if cidx >= 2:
            k.wait(v, "yd")
        for dt in range(NDT):
            xsl = slice(dt * NTOK + cidx * 512, dt * NTOK + (cidx + 1) * 512)
            osl = slice(par * NDT * 512 + dt * 512, par * NDT * 512 + (dt + 1) * 512)
            if mode == "ln":
                v.tensor_tensor(out=tmp[:, :], in0=R[:, xsl], in1=ps_r[:, :],
                                op=ALU.mult)
                v.tensor_tensor(out=xnc[:, osl], in0=tmp[:, :], in1=ps_m[:, :],
                                op=ALU.add)
            elif mode == "lnrms":
                v.tensor_tensor(out=tmp[:, :], in0=R[:, xsl], in1=ps_r[:, :],
                                op=ALU.mult)
                v.tensor_tensor(out=tmp[:, :], in0=tmp[:, :], in1=ps_m[:, :],
                                op=ALU.add)
                v.tensor_scalar(out=xnc[:, osl], in0=tmp[:, :],
                                scalar1=normw_col[:, dt:dt + 1], scalar2=None,
                                op0=ALU.mult)
                v.tensor_tensor(out=tmp2[:, :], in0=R[:, xsl],
                                in1=ps_r2[:, :], op=ALU.mult)
                v.tensor_tensor(out=tmp2[:, :], in0=tmp2[:, :],
                                in1=ps_m2[:, :], op=ALU.add)
                v.tensor_tensor(out=R[:, xsl], in0=R[:, xsl], in1=tmp2[:, :],
                                op=ALU.add)
            else:
                v.scalar_tensor_tensor(out=xnc[:, osl], in0=R[:, xsl],
                                       scalar=normw_col[:, dt:dt + 1],
                                       in1=ps_r[:, :], op0=ALU.mult, op1=ALU.mult)
        if absmax_out is not None:
            v.tensor_reduce(out=am4[:, cidx:cidx + 1],
                            in_=xnc[:, par * NDT * 512:(par + 1) * NDT * 512],
                            axis=AXX, op=ALU.max, apply_absolute_value=True)
        vv = k.mark(v, "v")
        k.wait(k.y, "v", vv)
        src3 = xnc[:, par * NDT * 512:(par + 1) * NDT * 512].rearrange(
            "p (a c) -> p a c", a=NDT)
        dst3 = xn_hbm[:, :].rearrange("p (a n) -> p a n", a=NDT)[
            :, :, cidx * 512:(cidx + 1) * 512]
        k.dma("y", dst3, src3)
    if absmax_out is not None:
        v.tensor_reduce(out=absmax_out, in_=am4[:, :], axis=AXX, op=ALU.max)
        k.mark(v, "v")
    k.bar()


def emit_l0_ffn(k: K, R, xn_hbm, W, st, uid=""):
    t, v, s, g = k.t, k.v, k.s, k.g
    k.scope("f_main" + uid)

    wsb = {}
    for nm in ("wfT", "wiT", "wcT", "ufT", "uiT", "ucT"):
        wsb[nm] = k.sb("ffn_" + nm + uid, [128, NDT * D], BF16, st)
    bias = k.sb("ffn_bias" + uid, [128, 4 * NDT], F32, st)
    incol = k.sb("ffn_incol" + uid, [128, NDT], F32, st)
    cc = k.sb("ffn_c" + uid, [128, NDT * TLOC], F32, st)
    cbf = k.sb("ffn_cbf" + uid, [128, NDT * TLOC], BF16, st)
    gf = k.sb("ffn_gf" + uid, [128, 2 * TLOC], F32, st)
    gi = k.sb("ffn_gi" + uid, [128, 2 * TLOC], F32, st)
    gg = k.sb("ffn_gg" + uid, [128, 2 * TLOC], F32, st)
    cuo = k.sb("ffn_cuo" + uid, [128, NDT * TLOC], F32, st)
    xstep = k.sb("ffn_xstep" + uid, [128, 2 * NDT * TLOC], BF16, st)
    xnc2 = k.sb("ffn_xnc2" + uid, [128, NDT * 512], BF16, st)

    xnh3 = xn_hbm[:, :].rearrange("p (a n) -> p a n", a=NDT)

    def load_xstep(par, step):
        dst = xstep[:, par * NDT * TLOC:(par + 1) * NDT * TLOC].rearrange(
            "p (a c) -> p a c", a=NDT)
        sem = "xa" if par == 0 else "xb"
        return (sem, k.dma("y", dst, xnh3[:, :, step * TLOC:(step + 1) * TLOC],
                           sem=sem))

    for nm in wsb:
        k.dma("g", wsb[nm][:, :], W[nm][:, :])
    k.dma("y", bias[:, :], W["bias"][:, :])
    k.dma("y", incol[:, :], W["incol"][:, :])
    xs_done = [load_xstep(0, 0), load_xstep(1, 1)]
    wload = k.c["gd"]
    k.wait(v, "yd")
    for dt in range(NDT):
        bc = incol[:, dt:dt + 1].broadcast_to([128, TLOC])
        v.tensor_copy(out=cc[:, dt * TLOC:(dt + 1) * TLOC], in_=bc)
        v.tensor_copy(out=cbf[:, dt * TLOC:(dt + 1) * TLOC], in_=bc)
    k.mark(v, "v")
    k.wait(t, "gd", wload)
    k.bar()

    names = [("wfT", "ufT"), ("wiT", "uiT"), ("wcT", "ucT")]
    gates = [gf, gi, gg]
    funcs = [AF.Sigmoid, AF.Sigmoid, AF.Tanh]
    s_marks, v_marks = [], []
    for step in range(B):
        cr = cw = cc
        spar = step % 2
        xs = xstep[:, spar * NDT * TLOC:(spar + 1) * NDT * TLOC]
        k.wait(t, xs_done[step][0], xs_done[step][1])
        if step + 2 < B:
            pass
        for et in range(NDT):
            bank0 = (et % 2) * 3
            par = et % 2
            if len(s_marks) >= 2:
                k.wait(t, "s", s_marks[-2])
            if len(v_marks) >= 2:
                k.wait(s, "v", v_marks[-2])
            for gidx in range(3):
                ps = k.psum[bank0 + gidx]
                wn, un = names[gidx]
                for dt in range(NDT):
                    t.matmul(ps[:, 0:TLOC],
                             lhsT=wsb[wn][:, dt * D + et * 128:dt * D + (et + 1) * 128],
                             rhs=xs[:, dt * TLOC:(dt + 1) * TLOC],
                             start=(dt == 0), stop=False)
                for dt in range(NDT):
                    mi = t.matmul(ps[:, 0:TLOC],
                                  lhsT=wsb[un][:, dt * D + et * 128:dt * D + (et + 1) * 128],
                                  rhs=cbf[:, dt * TLOC:(dt + 1) * TLOC],
                                  start=False, stop=(dt == NDT - 1))
            mmv = k.inc(mi, "t")
            k.wait(s, "t", mmv)
            gsl = slice(par * TLOC, (par + 1) * TLOC)
            for gidx in range(3):
                ps = k.psum[bank0 + gidx]
                s.activation(out=gates[gidx][:, gsl], in_=ps[:, 0:TLOC],
                             func=funcs[gidx],
                             bias=bias[:, gidx * NDT + et:gidx * NDT + et + 1])
            sv = k.mark(s, "s")
            s_marks.append(sv)
            k.wait(v, "s", sv)
            esl = slice(et * TLOC, (et + 1) * TLOC)
            v.tensor_tensor(out=gg[:, gsl], in0=gi[:, gsl], in1=gg[:, gsl],
                            op=ALU.mult)
            v.tensor_tensor(out=cw[:, esl], in0=gf[:, gsl], in1=cr[:, esl],
                            op=ALU.mult)
            v.tensor_tensor(out=cw[:, esl], in0=cw[:, esl], in1=gg[:, gsl],
                            op=ALU.add)
            v_marks.append(k.mark(v, "v"))
        vv = k.mark(v, "v")
        k.wait(s, "v", vv)
        for dt in range(NDT):
            s.copy(out=cbf[:, dt * TLOC:(dt + 1) * TLOC],
                   in_=cw[:, dt * TLOC:(dt + 1) * TLOC])
        sv2 = k.mark(s, "s")
        k.wait(t, "s", sv2)
        if step + 2 < B:
            k.wait(k.y, "t", k.c["t"])
            xs_done.append(load_xstep(spar, step + 2))

    k.bar()
    woT, uoT = wsb["wfT"], wsb["wiT"]
    k.dma("g", woT[:, :], W["woT"][:, :])
    k.dma("g", uoT[:, :], W["uoT"][:, :])
    wload2 = k.c["gd"]
    k.wait(t, "gd", wload2)

    for et in range(NDT):
        ps = k.psum[5 + (et % 2)]
        if et >= 2:
            k.wait(t, "v")
        for dt in range(NDT):
            mi = t.matmul(ps[:, 0:TLOC],
                          lhsT=uoT[:, dt * D + et * 128:dt * D + (et + 1) * 128],
                          rhs=cbf[:, dt * TLOC:(dt + 1) * TLOC],
                          start=(dt == 0), stop=(dt == NDT - 1))
        mmv = k.inc(mi, "t")
        k.wait(v, "t", mmv)
        v.tensor_copy(out=cuo[:, et * TLOC:(et + 1) * TLOC], in_=ps[:, 0:TLOC])
        k.mark(v, "v")
    k.bar()

    dst3 = xnc2[:, :].rearrange("p (a c) -> p a c", a=NDT)
    xld = [k.dma("y", dst3, xnh3[:, :, 0:512], sem="xa")]
    for cidx in range(NCH):
        xch = xnc2[:, :]
        k.wait(t, "xa", xld[cidx])
        for et in range(NDT):
            bank = 2 * (et % 2) + (cidx % 2) % 2
            ps = k.psum[(et % 2) * 2 + (cidx % 2)]
            if et >= 1 or cidx >= 2:
                k.wait(t, "v")
            for dt in range(NDT):
                mi = t.matmul(ps[:, :],
                              lhsT=woT[:, dt * D + et * 128:dt * D + (et + 1) * 128],
                              rhs=xch[:, dt * 512:(dt + 1) * 512],
                              start=(dt == 0), stop=(dt == NDT - 1))
            mmv = k.inc(mi, "t")
            k.wait(v, "t", mmv)
            cub = cuo[:, et * TLOC:(et + 1) * TLOC]
            cub3 = cub.rearrange("p (o t) -> p o t", o=1).broadcast_to(
                [128, 2, TLOC])
            ps3 = ps[:, :].rearrange("p (b t) -> p b t", b=2)
            v.tensor_tensor(out=ps3, in0=ps3, in1=cub3, op=ALU.add)
            vv = k.mark(v, "v")
            k.wait(s, "v", vv)
            s.activation(out=ps[:, :], in_=ps[:, :], func=AF.Tanh,
                         bias=bias[:, 3 * NDT + et:3 * NDT + et + 1])
            sv = k.mark(s, "s")
            k.wait(v, "s", sv)
            rsl = slice(et * NTOK + cidx * 512, et * NTOK + (cidx + 1) * 512)
            v.scalar_tensor_tensor(out=R[:, rsl], in0=ps[:, :], scalar=0.5,
                                   in1=R[:, rsl], op0=ALU.mult, op1=ALU.add)
            k.mark(v, "v")
        if cidx + 1 < NCH:
            k.wait(k.y, "t", k.c["t"])
            xld.append(k.dma("y", dst3,
                             xnh3[:, :, (cidx + 1) * 512:(cidx + 2) * 512],
                             sem="xa"))
    k.bar()


NBIN = 64


def emit_l0_attn(k: K, R, xn_hbm, xh_hbm, W, st, core_id_dummy=None, uid=""):
    t, v, s, g, y = k.t, k.v, k.s, k.g, k.y
    xnh3 = xn_hbm[:, :].rearrange("p (a n) -> p a n", a=NDT)
    xhh = xh_hbm

    dft = k.sb("at_dft" + uid, [128, NBIN], BF16, st)
    ib = k.sb("at_ib" + uid, [128, NBIN], BF16, st)
    xlow = k.sb("at_xlow" + uid, [128, NDT * 256], BF16, st)
    xlow1m = k.sb("at_xlow1m" + uid, [128, NDT * 256], BF16, st)
    gsb = k.sb("at_g" + uid, [128, NDT * 32], F32, st)
    alow = k.sb("at_alow" + uid, [128, NDT * 256], BF16, st)
    acol = k.sb("at_acol" + uid, [128, 2], F32, st)
    mrow = k.sb("at_mask" + uid, [128, 128], BF16, st)
    ohot = k.sb("at_ohot" + uid, [128, 16], BF16, st)

    k.dma("g", dft[:, :], W["dft"][:, :])
    k.dma("g", ib[:, :], W["ib"][:, :])
    k.dma("g", mrow[:, :], W["mask"][:, :])
    k.dma("g", ohot[:, :], W["onehot"][:, :])

    araw = k.sb("at_araw" + uid, [1, 1], F32, st)
    k.dma("y", araw[:, :], W["alpha"][:, :])
    k.wait(s, "yd")
    s.activation(out=araw[:, :], in_=araw[:, :], func=AF.Sigmoid)
    sa = k.mark(s, "s")
    k.wait(s, "s", sa)
    s.copy(out=mrow[0:1, 0:1], in_=araw[:, :]) if False else None
    ab = k.sb("at_ab" + uid, [1, 1], BF16, st)
    s.copy(out=ab[:, :], in_=araw[:, :])
    sa2 = k.mark(s, "s")
    k.wait(t, "s", sa2)
    k.wait(t, "gd")
    ones1 = W["ones1"]
    mi = t.matmul(k.psum[6][:, 0:1], lhsT=ones1, rhs=ab[:, :], start=True,
                  stop=True)
    tb = k.inc(mi, "t")
    k.wait(v, "t", tb)
    v.tensor_copy(out=acol[:, 0:1], in_=k.psum[6][:, 0:1])
    v.tensor_scalar(out=acol[:, 1:2], in0=acol[:, 0:1], scalar1=-1.0,
                    scalar2=1.0, op0=ALU.mult, op1=ALU.add)
    v.memset(xlow[:, :], 0.0)
    k.mark(v, "v")
    k.bar()

    k.scope("atA" + uid)
    with ExitStack() as stA:
        xnw = k.sb("at_xnw" + uid, [128, 2 * NDT * 128], BF16, stA)
        xnt = k.sb("at_xnt" + uid, [128, NDT * 128], BF16, stA)
        xhe = k.sb("at_xhe" + uid, [128, 2 * 512], BF16, stA)
        pairs = [(b, cp) for b in range(B) for cp in range(2)]

        def ld_xnw(i):
            b, cp = pairs[i]
            par = i % 2
            sem = "xa" if par == 0 else "xc"
            dst = xnw[:, par * NDT * 128:(par + 1) * NDT * 128].rearrange(
                "p (a c) -> p a c", a=NDT)
            return (sem, k.dma("y", dst,
                               xnh3[:, :, b * 256 + cp * 128:
                                    b * 256 + (cp + 1) * 128], sem=sem))

        ldcnt = [ld_xnw(0), ld_xnw(1)]
        for i, (b, cp) in enumerate(pairs):
            par = i % 2
            xbuf = xnw[:, par * NDT * 128:(par + 1) * NDT * 128]
            k.wait(t, ldcnt[i][0], ldcnt[i][1])
            for half in range(2):
                if i > 0 or half > 0:
                    k.wait(t, "v")
                for j in range(4):
                    dt = half * 4 + j
                    mi = t.transpose(k.psum_bf[:, j * 128:(j + 1) * 128],
                                     in_=xbuf[:, dt * 128:(dt + 1) * 128],
                                     identity=W["identb"])
                tc = k.inc(mi, "t")
                k.wait(v, "t", tc)
                v.tensor_copy(out=xnt[:, half * 512:(half + 1) * 512],
                              in_=k.psum_bf[:, 0:512])
                k.mark(v, "v")
            ps = k.psum[i % 2]
            k.wait(t, "v")
            for ch01 in range(2):
                ch = cp * 2 + ch01
                off = ch01 * 64
                for dt in range(NDT):
                    mi = t.matmul(
                        ps[:, dt * 64:(dt + 1) * 64] if ch01 == 0 else
                        k.psum[2 + (i % 2)][:, dt * 64:(dt + 1) * 64],
                        lhsT=xnt[off:off + 64, dt * 128:(dt + 1) * 128],
                        rhs=dft[off:off + 64, :],
                        start=(dt == 0), stop=(dt == NDT - 1))
            tm = k.inc(mi, "t")
            k.wait(v, "t", tm)
            k.wait(v, "xb")
            for ch01 in range(2):
                ch = cp * 2 + ch01
                psc = ps if ch01 == 0 else k.psum[2 + (i % 2)]
                dst = xhe[:, ch01 * 512:(ch01 + 1) * 512]
                v.tensor_copy(out=dst, in_=psc[:, :])
                re_dst = xlow[:, :].rearrange("p (a c) -> p a c", a=NDT)[
                    :, :, b * 16 + ch * 4:b * 16 + ch * 4 + 4]
                re_src = psc[:, :].rearrange("p (a c) -> p a c", a=NDT)[
                    :, :, 0:4]
                v.tensor_copy(out=re_dst, in_=re_src)
                im_dst = xlow[:, :].rearrange("p (a c) -> p a c", a=NDT)[
                    :, :, 128 + b * 16 + ch * 4 + 1:128 + b * 16 + ch * 4 + 4]
                im_src = psc[:, :].rearrange("p (a c) -> p a c", a=NDT)[
                    :, :, 33:36]
                v.tensor_copy(out=im_dst, in_=im_src)
            vv = k.mark(v, "v")
            k.wait(y, "v", vv)
            k.dma("y", xhh[:, (b * 4 + cp * 2) * 512:(b * 4 + cp * 2 + 2) * 512],
                  xhe[:, :], sem="xb")
            if i + 2 < len(pairs):
                k.wait(y, "t", k.c["t"])
                ldcnt.append(ld_xnw(i + 2))
    k.bar()

    for dt in range(NDT):
        v.tensor_scalar(out=xlow1m[:, dt * 256:(dt + 1) * 256],
                        in0=xlow[:, dt * 256:(dt + 1) * 256],
                        scalar1=acol[:, 1:2], scalar2=None, op0=ALU.mult)
    k.mark(v, "v")

    if ATTN_PHASES < 2:
        k.bar()
        return
    k.scope("atB" + uid)
    qsb = k.sb("at_q" + uid, [128, NDT * 256], BF16, st)
    osb = k.sb("at_o" + uid, [128, NDT * 256], BF16, st)
    ksb = k.sb("at_k" + uid, [128, NCORES * 1024], BF16, st)
    vtsb = k.sb("at_vt" + uid, [128, B * 512], BF16, st)
    with ExitStack() as stB:
        qwT = k.sb("at_qwT" + uid, [128, NDT * D], BF16, stB)
        kwT = k.sb("at_kwT" + uid, [128, NDT * 512], BF16, stB)
        vwT = k.sb("at_vwT" + uid, [128, NDT * 256], BF16, stB)
        gwT = k.sb("at_gwT" + uid, [128, NDT * D], BF16, stB)
        kloc = k.sb("at_kloc" + uid, [128, 1024], BF16, stB)
        vtloc = k.sb("at_vtloc" + uid, [128, 512], BF16, stB)
        k.dma("g", qwT[:, :], W["qwT"][:, :])
        k.dma("g", kwT[:, :], W["kwT"][:, :])
        k.dma("g", vwT[:, :], W["vwT"][:, :])
        k.dma("g", gwT[:, :], W["gwT"][:, :])
        k.wait(t, "gd")
        for et in range(NDT):
            ps = k.psum[et % 2]
            if et >= 2:
                k.wait(t, "v")
            for dt in range(NDT):
                mi = t.matmul(ps[:, 0:256],
                              lhsT=qwT[:, dt * D + et * 128:dt * D + (et + 1) * 128],
                              rhs=xlow[:, dt * 256:(dt + 1) * 256],
                              start=(dt == 0), stop=(dt == NDT - 1))
            tm = k.inc(mi, "t")
            k.wait(v, "t", tm)
            v.tensor_scalar(out=qsb[:, et * 256:(et + 1) * 256], in0=ps[:, 0:256],
                            scalar1=0.125, scalar2=None, op0=ALU.mult)
            k.mark(v, "v")
        for et in range(4):
            ps = k.psum[2 + (et % 2)]
            if et >= 2:
                k.wait(t, "v")
            for dt in range(NDT):
                mi = t.matmul(ps[:, 0:256],
                              lhsT=kwT[:, dt * 512 + et * 128:dt * 512 + (et + 1) * 128],
                              rhs=xlow[:, dt * 256:(dt + 1) * 256],
                              start=(dt == 0), stop=(dt == NDT - 1))
            tm = k.inc(mi, "t")
            k.wait(v, "t", tm)
            v.tensor_copy(out=kloc[:, et * 256:(et + 1) * 256], in_=ps[:, 0:256])
            k.mark(v, "v")
        for ri in range(2):
            ps = k.psum[4 + ri]
            for dt in range(NDT):
                mi = t.matmul(ps[:, 0:256],
                              lhsT=xlow[:, dt * 256 + ri * 128:dt * 256 + ri * 128 + 128],
                              rhs=vwT[:, dt * 256:(dt + 1) * 256],
                              start=(dt == 0), stop=(dt == NDT - 1))
            tm = k.inc(mi, "t")
            k.wait(v, "t", tm)
            v.tensor_copy(out=vtloc[:, ri * 256:(ri + 1) * 256], in_=ps[:, 0:256])
            k.mark(v, "v")
        for et in range(NDT):
            ps = k.psum[6]
            if et > 0:
                k.wait(t, "s")
            for dt in range(NDT):
                rhs = xlow[:, :].rearrange("p (a r b c) -> p a r b c", a=NDT,
                                           r=2, b=B)[dt, :, :, :] if False else None
                rhs = xlow[:, dt * 256:dt * 256 + 128].rearrange(
                    "p (b c) -> p b c", b=B)[:, :, 0::4]
                mi = t.matmul(ps[:, 0:32],
                              lhsT=gwT[:, dt * D + et * 128:dt * D + (et + 1) * 128],
                              rhs=rhs,
                              start=(dt == 0), stop=(dt == NDT - 1))
            tm = k.inc(mi, "t")
            k.wait(s, "t", tm)
            s.activation(out=gsb[:, et * 32:(et + 1) * 32], in_=ps[:, 0:32],
                         func=AF.Sigmoid, scale=1.0 / 64.0)
            k.mark(s, "s")
        vv = k.c["v"]
        k.wait(g, "v", vv)
        k.wait(g, "s")
        k.dma("g", W["cc_in"][:, 0:1024], kloc[:, :])
        k.dma("g", W["cc_in"][:, 1024:1536], vtloc[:, :])
        g.wait_ge(k.sems["gd"], k.c["gd"])
        ci = g.collective_compute(
            "AllGather", ALU.bypass,
            replica_groups=[list(range(NCORES))],
            ins=[W["cc_in"].opt()],
            outs=[W["cc_out"].opt()],
        )
        k.inc(ci, "cc")
        ccn = k.c["cc"]
        k.wait(y, "cc", ccn)
        cco = W["cc_out"]
        for r in range(NCORES):
            k.dma("y", ksb[:, r * 1024:(r + 1) * 1024],
                  cco[128 * r:128 * (r + 1), 0:1024], sem="xa")
            dstv = vtsb[16 * r:16 * r + 16, :].rearrange(
                "p (b c) -> p b c", b=B)
            inv = cco[128 * r:128 * (r + 1), 1024:1536].rearrange(
                "(b q) c -> q b c", b=B)
            k.dma("y", dstv, inv, sem="xa")

    if ATTN_PHASES < 3:
        k.bar()
        return
    k.scope("atC" + uid)
    ahig = k.sb("at_ahig" + uid, [128, NDT * 2 * 29 * 32], BF16, st)
    with ExitStack() as stC:
        pwrT = k.sb("at_pwrT" + uid, [128, NDT * D], BF16, stC)
        pwiT = k.sb("at_pwiT" + uid, [128, NDT * D], BF16, stC)
        dwc = k.sb("at_dw" + uid, [128, 2 * NDT * 7], F32, stC)
        gin = k.sb("at_gin" + uid, [128, NDT * 29 * 32], BF16, stC)
        cin = k.sb("at_cin" + uid, [128, 2 * 29 * 32], BF16, stC)
        acc = k.sb("at_acc" + uid, [128, 29 * 32], F32, stC)
        k.dma("g", pwrT[:, :], W["pwrT"][:, :])
        k.dma("g", pwiT[:, :], W["pwiT"][:, :])
        k.dma("g", dwc[:, :], W["dw"][:, :])
        gwl = k.c["gd"]
        k.wait(v, "gd", gwl)
        k.wait(v, "xb")
        for ri in range(2):
            pwT = pwrT if ri == 0 else pwiT
            if ri == 1:
                k.wait(s, "t")
            for dt in range(NDT):
                par = dt % 2
                cbuf = cin[:, par * 928:(par + 1) * 928]
                cb3 = cbuf.rearrange("p (w c) -> p w c", w=32)
                k.wait(y, "v")
                if ri == 0:
                    src = xhh[:, :].rearrange("p (w c) -> p w c", w=32)[
                        :, :, dt * 64 + 4:dt * 64 + 33]
                    ld = k.dma("y", cb3, src, sem="xb")
                else:
                    src = xhh[:, :].rearrange("p (w c) -> p w c", w=32)[
                        :, :, dt * 64 + 36:dt * 64 + 64]
                    ld = k.dma("y", cb3[:, :, 0:28], src, sem="xb")
                k.wait(v, "xb", ld)
                if ri == 1:
                    v.memset(cb3[:, :, 28:29], 0.0)
                dcol = lambda j: dwc[:, ri * NDT * 7 + dt * 7 + j:
                                     ri * NDT * 7 + dt * 7 + j + 1]
                a3 = acc.rearrange("p (w c) -> p w c", w=32)
                k.wait(v, "s")
                v.tensor_scalar(out=a3, in0=cb3, scalar1=dcol(6), scalar2=None,
                                op0=ALU.mult)
                for j in range(5, -1, -1):
                    w = 23 + j
                    v.scalar_tensor_tensor(out=a3[:, :, 6 - j:29],
                                           in0=cb3[:, :, 0:w], scalar=dcol(j),
                                           in1=a3[:, :, 6 - j:29],
                                           op0=ALU.mult, op1=ALU.add)
                vm = k.mark(v, "v")
                k.wait(s, "v", vm)
                s.activation(out=gin[:, dt * 928:(dt + 1) * 928], in_=acc[:, :],
                             func=AF.Gelu)
                k.mark(s, "s")
            sv = k.c["s"]
            k.wait(t, "s", sv)
            for et in range(NDT):
                for half in range(2):
                    ps = k.psum[(et % 2) * 2 + half]
                    csl = slice(half * 464, (half + 1) * 464)
                    if et >= 2:
                        k.wait(t, "v")
                    for dt in range(NDT):
                        mi = t.matmul(
                            ps[:, 0:464],
                            lhsT=pwT[:, dt * D + et * 128:dt * D + (et + 1) * 128],
                            rhs=gin[:, dt * 928 + half * 464:
                                    dt * 928 + (half + 1) * 464],
                            start=(dt == 0), stop=(dt == NDT - 1))
                    tm = k.inc(mi, "t")
                    par2 = (et * 2 + half) % 2
                    xbuf = cin[:, par2 * 928 + 0:par2 * 928 + 464]
                    xb3 = xbuf.rearrange("p (w c) -> p w c", w=16)
                    k.wait(y, "v")
                    if ri == 0:
                        src = xhh[:, :].rearrange("p (w c) -> p w c", w=32)[
                            :, half * 16:(half + 1) * 16,
                            et * 64 + 4:et * 64 + 33]
                        ld = k.dma("y", xb3, src, sem="xb")
                    else:
                        src = xhh[:, :].rearrange("p (w c) -> p w c", w=32)[
                            :, half * 16:(half + 1) * 16,
                            et * 64 + 36:et * 64 + 64]
                        ld = k.dma("y", xb3[:, :, 0:28], src, sem="xb")
                    k.wait(v, "xb", ld)
                    if ri == 1:
                        v.memset(xb3[:, :, 28:29], 0.0)
                    k.wait(v, "t", tm)
                    gap = gsb[:, et * 32 + half * 16:et * 32 + (half + 1) * 16]
                    ahsl = ahig[:, (et * 2 + ri) * 928 + half * 464:
                                (et * 2 + ri) * 928 + (half + 1) * 464]
                    ah3 = ahsl.rearrange("p (w c) -> p w c", w=16)
                    ps3 = ps[:, 0:464].rearrange("p (w c) -> p w c", w=16)
                    gbc = gap.rearrange("p (w o) -> p w o", o=1).broadcast_to(
                        [128, 16, 29])
                    v.scalar_tensor_tensor(out=ah3, in0=ps3, scalar=1.0,
                                           in1=gbc, op0=ALU.mult, op1=ALU.mult)
                    v.tensor_tensor(out=ah3, in0=ah3, in1=xb3, op=ALU.add)
                    k.mark(v, "v")
    k.bar()

    if ATTN_PHASES < 4:
        k.bar()
        return
    k.scope("atD" + uid)
    with ExitStack() as stD:
        praw = k.sb("at_praw" + uid, [16, 8 * 128], BF16, stD)
        ptsb = k.sb("at_pt" + uid, [128, 2 * 16], BF16, stD)
        smal = k.sb("at_smal" + uid, [16, 3 * 8], F32, stD)
        k.wait(t, "xa")
        k.bar()
        itr = 0
        for h in range(16):
            for ri in range(2):
                et, po = h // 2, (h % 2) * 64
                for b in range(B):
                    ps = k.psum[b // 2][0:16, (b % 2) * 128:(b % 2) * 128 + 128]
                    q_ap = qsb[po:po + 64,
                               et * 256 + ri * 128 + b * 16:
                               et * 256 + ri * 128 + b * 16 + 16]
                    kh = h // 4
                    k_ap = ksb[po:po + 64, :].rearrange(
                        "p (r x) -> p r x", r=NCORES)[
                        :, :, kh * 256 + ri * 128 + b * 16:
                        kh * 256 + ri * 128 + b * 16 + 16]
                    t.matmul(ps, lhsT=q_ap, rhs=k_ap, start=(b % 2 == 0),
                             stop=False, skip_group_check=True)
                    mi = t.matmul(ps, lhsT=ohot[po:po + 4, :],
                                  rhs=mrow[po:po + 4, :],
                                  start=False, stop=(b % 2 == 1),
                                  skip_group_check=True)
                tm = k.inc(mi, "t")
                k.wait(v, "t", tm)
                k.wait(v, "s")
                for bb in range(4):
                    ps = k.psum[bb]
                    v.tensor_reduce(out=smal[0:16, bb * 2:bb * 2 + 2],
                                    in_=ps[0:16, 0:256].rearrange(
                                        "p (b c) -> p b c", b=2),
                                    axis=AXX, op=ALU.max, negate=True)
                vm = k.mark(v, "v")
                k.wait(s, "v", vm)
                k.wait(s, "t")
                for b in range(B):
                    ps = k.psum[b // 2][0:16, (b % 2) * 128:(b % 2) * 128 + 128]
                    s.activation(out=praw[0:16, b * 128:(b + 1) * 128], in_=ps,
                                 func=AF.Exp, bias=smal[0:16, b:b + 1],
                                 accum_out=smal[0:16, 8 + b:9 + b])
                sm = k.mark(s, "s")
                k.wait(v, "s", sm)
                v.reciprocal(out=smal[0:16, 16:24], in_=smal[0:16, 8:16])
                v.tensor_scalar(out=praw[0:16, :].rearrange(
                                    "p (b c) -> p b c", b=B),
                                in0=praw[0:16, :].rearrange(
                                    "p (b c) -> p b c", b=B),
                                scalar1=None, scalar2=None, op0=ALU.mult) \
                    if False else None
                v.tensor_tensor(out=praw[0:16, :].rearrange(
                                    "p (b c) -> p b c", b=B),
                                in0=praw[0:16, :].rearrange(
                                    "p (b c) -> p b c", b=B),
                                in1=smal[0:16, 16:24].rearrange(
                                    "p (b o) -> p b o", b=B).broadcast_to(
                                    [16, B, 128]),
                                op=ALU.mult)
                vm2 = k.mark(v, "v")
                k.wait(t, "v", vm2)
                for b in range(B):
                    par = b % 2
                    mi = t.transpose(k.psum_bf[:, par * 512:par * 512 + 16],
                                     in_=praw[0:16, b * 128:(b + 1) * 128],
                                     identity=W["identb"][0:16, 0:16])
                    tt = k.inc(mi, "t")
                    k.wait(s, "t", tt)
                    s.copy(out=ptsb[:, par * 16:(par + 1) * 16],
                           in_=k.psum_bf[:, par * 512:par * 512 + 16])
                    sm2 = k.mark(s, "s")
                    k.wait(t, "s", sm2)
                    mi = t.matmul(k.psum[4 + par][0:64, 0:16],
                                  lhsT=vtsb[:, b * 512 + ri * 256 + (h // 4) * 64:
                                            b * 512 + ri * 256 + (h // 4) * 64 + 64],
                                  rhs=ptsb[:, par * 16:(par + 1) * 16],
                                  start=True, stop=True)
                    tv = k.inc(mi, "t")
                    k.wait(s, "t", tv)
                    s.copy(out=osb[po:po + 64,
                                   et * 256 + ri * 128 + b * 16:
                                   et * 256 + ri * 128 + b * 16 + 16],
                           in_=k.psum[4 + par][0:64, 0:16])
                    k.mark(s, "s")
                k.wait(t, "s")
                itr += 1
    k.bar()

    if ATTN_PHASES < 5:
        k.bar()
        return
    k.scope("atE" + uid)
    with ExitStack() as stE:
        owT = k.sb("at_owT" + uid, [128, NDT * D], BF16, stE)
        k.dma("g", owT[:, :], W["owT"][:, :])
        k.wait(t, "gd")
        for et in range(NDT):
            ps = k.psum[et % 2]
            if et >= 2:
                k.wait(t, "v")
            for dt in range(NDT):
                mi = t.matmul(ps[:, 0:256],
                              lhsT=owT[:, dt * D + et * 128:dt * D + (et + 1) * 128],
                              rhs=osb[:, dt * 256:(dt + 1) * 256],
                              start=(dt == 0), stop=(dt == NDT - 1))
            tm = k.inc(mi, "t")
            k.wait(v, "t", tm)
            v.scalar_tensor_tensor(out=alow[:, et * 256:(et + 1) * 256],
                                   in0=ps[:, 0:256], scalar=acol[:, 0:1],
                                   in1=xlow1m[:, et * 256:(et + 1) * 256],
                                   op0=ALU.mult, op1=ALU.add)
            k.mark(v, "v")
    k.bar()

    if ATTN_PHASES < 6:
        k.bar()
        return
    k.scope("atF" + uid)
    with ExitStack() as stF:
        yh = k.sb("at_yh" + uid, [128, 2 * 256], BF16, stF)
        yht = k.sb("at_yht" + uid, [128, 256], BF16, stF)
        for dt in range(NDT):
            for b in range(B):
                par = (dt * B + b) % 2
                ybuf = yh[:, par * 256:(par + 1) * 256]
                y3 = ybuf.rearrange("p (c x) -> p c x", c=4)
                if dt * B + b >= 2:
                    k.wait(v, "t")
                v.tensor_copy(out=y3[:, :, 0:4],
                              in_=alow[:, dt * 256 + b * 16:
                                       dt * 256 + b * 16 + 16].rearrange(
                                  "p (c x) -> p c x", c=4))
                v.tensor_copy(out=y3[:, :, 33:36],
                              in_=alow[:, dt * 256 + 128 + b * 16:
                                       dt * 256 + 128 + b * 16 + 16].rearrange(
                                  "p (c x) -> p c x", c=4)[:, :, 1:4])
                v.tensor_copy(out=y3[:, :, 4:33],
                              in_=ahig[:, (dt * 2 + 0) * 928 + b * 4 * 29:
                                       (dt * 2 + 0) * 928 + (b * 4 + 4) * 29]
                              .rearrange("p (c x) -> p c x", c=4))
                v.tensor_copy(out=y3[:, :, 36:64],
                              in_=ahig[:, (dt * 2 + 1) * 928 + b * 4 * 29:
                                       (dt * 2 + 1) * 928 + (b * 4 + 4) * 29]
                              .rearrange("p (c x) -> p c x", c=4)[:, :, 0:28])
                vm = k.mark(v, "v")
                k.wait(t, "v", vm)
                for hf in range(2):
                    mi = t.transpose(k.psum_bf[:, hf * 512:hf * 512 + 128],
                                     in_=ybuf[:, hf * 128:(hf + 1) * 128],
                                     identity=W["identb"])
                tt = k.inc(mi, "t")
                k.wait(s, "t", tt)
                s.copy(out=yht[:, 0:128], in_=k.psum_bf[:, 0:128])
                s.copy(out=yht[:, 128:256], in_=k.psum_bf[:, 512:640])
                sm = k.mark(s, "s")
                if ATTN_F_MODE < 2:
                    continue
                k.wait(t, "s", sm)
                k.wait(t, "v")
                psE = k.psum[2 + par]
                psO = k.psum[4 + par]
                for ch01 in range(4):
                    off = (ch01 % 2) * 64
                    psx = psE if ch01 % 2 == 0 else psO
                    mi = t.matmul(psx[:, (ch01 // 2) * 64:(ch01 // 2) * 64 + 64],
                                  lhsT=yht[off:off + 64,
                                           (ch01 // 2) * 128:
                                           (ch01 // 2) * 128 + 128],
                                  rhs=ib[off:off + 64, :],
                                  start=(ch01 < 2), stop=(ch01 >= 2))
                tm = k.inc(mi, "t")
                if ATTN_F_MODE < 3:
                    continue
                k.wait(v, "t", tm)
                rsl = R[:, dt * NTOK + b * 256:dt * NTOK + (b + 1) * 256]
                r4 = rsl.rearrange("p (c x) -> p c x", c=4)
                v.tensor_tensor(out=r4[:, 0::2, :], in0=r4[:, 0::2, :],
                                in1=psE[:, 0:128].rearrange(
                                    "p (c x) -> p c x", c=2), op=ALU.add)
                v.tensor_tensor(out=r4[:, 1::2, :], in0=r4[:, 1::2, :],
                                in1=psO[:, 0:128].rearrange(
                                    "p (c x) -> p c x", c=2), op=ALU.add)
                k.mark(v, "v")
    k.bar()


def emit_l1_ffn(k: K, R, xn_hbm, W, st, xnmax_col, first, uid="", dbg=None):
    t, v, s, g, y = k.t, k.v, k.s, k.g, k.y
    IFF = 3072
    NIT = IFF // 128
    QM = 32767.0

    scr = k.sb("ff_scr" + uid, [1, 32], F32, st)
    scb = k.sb("ff_scb" + uid, [128, 16], F32, st)
    ones1f = k.sb("ff_o1f" + uid, [1, 128], F32, st)
    v.memset(ones1f[:, :], 1.0)
    v.memset(scr[0:1, :], 0.0)

    MX, MG, MU, M2, M3 = 0, 1, 2, 3, 4
    SX, IX, SG, IG, SU, IU, C2, C3, S3 = 5, 6, 7, 8, 9, 10, 11, 12, 13

    prow = k.sb("ff_prow" + uid, [1, 128], F32, st)

    def pcol_to_scr(col_ap, slot):
        k.wait(y, "v")
        wr = k.dma("y", W["colb"][0:1, :], col_ap, sem="xd")
        k.wait(y, "xd", wr)
        rd = k.dma("y", prow[0:1, :], W["colb"][0:1, :], sem="xd")
        k.wait(v, "xd", rd)
        v.tensor_reduce(out=scr[0:1, slot:slot + 1], in_=prow[0:1, :],
                        axis=AXX, op=ALU.max)
        return k.mark(v, "v")

    def allreduce_slots(lo, n):
        vv = k.c["v"]
        k.wait(g, "v", vv)
        k.dma("g", W["cc2_in"][0:1, 0:16], scr[0:1, 0:16])
        g.wait_ge(k.sems["gd"], k.c["gd"])
        ci = g.collective_compute(
            "AllReduce", ALU.max, replica_groups=[list(range(NCORES))],
            ins=[W["cc2_in"].opt()], outs=[W["cc2_out"].opt()])
        k.inc(ci, "cc")
        k.wait(g, "cc", k.c["cc"])
        k.dma("g", scr[0:1, lo:lo + n], W["cc2_out"][0:1, lo:lo + n])
        gd = k.c["gd"]
        k.wait(v, "gd", gd)
        k.wait(s, "gd", gd)

    def bcast_cols(slots):
        k.wait(t, "v")
        k.wait(t, "s")
        for sl in slots:
            mi = t.matmul(k.psum[6][:, sl:sl + 1], lhsT=ones1f[0:1, :],
                          rhs=scr[0:1, sl:sl + 1], start=True, stop=True)
        tm = k.inc(mi, "t")
        k.wait(v, "t", tm)
        for sl in slots:
            v.tensor_copy(out=scb[:, sl:sl + 1], in_=k.psum[6][:, sl:sl + 1])
        k.mark(v, "v")

    pcol_to_scr(xnmax_col, MX)
    allreduce_slots(MX, 1)
    v.tensor_scalar(out=scr[0:1, SX:SX + 1], in0=scr[0:1, MX:MX + 1],
                    scalar1=1.0 / QM, scalar2=1e-12, op0=ALU.mult, op1=ALU.max)
    v.reciprocal(out=scr[0:1, IX:IX + 1], in_=scr[0:1, SX:SX + 1])
    k.mark(v, "v")
    bcast_cols([SX, IX])

    k.scope("q_wq" + uid)
    if first:
        with ExitStack() as stW:
            wbuf = k.sb("ff_wbuf" + uid, [128, 2 * IFF], F32, stW)
            sa = k.sb("ff_sa" + uid, [128, 96], F32, stW)
            sr = k.sb("ff_sr" + uid, [128, 96], F32, stW)
            qi = k.sb("ff_qi" + uid, [128, IFF], I16, stW)
            wqb = k.sb("ff_wqb" + uid, [128, IFF], BF16, stW)
            stg = k.sb("ff_stg" + uid, [128, 2 * 512], BF16, stW)
            for wname, nt, ncols in (("wgate", NIT, D), ("wup", NIT, D),
                                     ("wdown", NDT, IFF)):
                nblk = ncols // 32
                for ot in range(nt):
                    par = ot % 2
                    wb = wbuf[:, par * IFF:par * IFF + ncols]
                    k.wait(y, "v")
                    ld = k.dma("y", wb, W[wname][:, ot * ncols:(ot + 1) * ncols],
                               sem="xd")
                    k.wait(v, "xd", ld)
                    v.tensor_reduce(out=sa[:, 0:nblk],
                                    in_=wb.rearrange("p (b c) -> p b c", b=nblk),
                                    axis=AXX, op=ALU.max,
                                    apply_absolute_value=True)
                    v.tensor_scalar(out=sa[:, 0:nblk], in0=sa[:, 0:nblk],
                                    scalar1=1.0 / 7.0, scalar2=1e-12,
                                    op0=ALU.mult, op1=ALU.max)
                    v.reciprocal(out=sr[:, 0:nblk], in_=sa[:, 0:nblk])
                    sr3 = sr[:, 0:nblk].rearrange("p (b o) -> p b o", o=1) \
                        .broadcast_to([128, nblk, 32])
                    sa3 = sa[:, 0:nblk].rearrange("p (b o) -> p b o", o=1) \
                        .broadcast_to([128, nblk, 32])
                    w3 = wb.rearrange("p (b c) -> p b c", b=nblk)
                    q3 = qi[:, 0:ncols].rearrange("p (b c) -> p b c", b=nblk)
                    wq3 = wqb[:, 0:ncols].rearrange("p (b c) -> p b c", b=nblk)
                    v.tensor_tensor(out=q3, in0=w3, in1=sr3, op=ALU.mult)
                    v.tensor_tensor(out=wq3, in0=q3, in1=sa3, op=ALU.mult)
                    vq = k.mark(v, "v")
                    k.wait(t, "v", vq)
                    nic = ncols // 128
                    for icg in range(nic // 4):
                        if ot > 0 or icg > 0:
                            k.wait(t, "s")
                        for j in range(4):
                            ic = icg * 4 + j
                            mi = t.transpose(
                                k.psum_bf[:, j * 128:(j + 1) * 128],
                                in_=wqb[:, ic * 128:(ic + 1) * 128],
                                identity=W["identb"])
                        tm = k.inc(mi, "t")
                        k.wait(s, "t", tm)
                        k.wait(s, "xd")
                        spar = icg % 2
                        s.copy(out=stg[:, spar * 512:(spar + 1) * 512],
                               in_=k.psum_bf[:, 0:512])
                        sm = k.mark(s, "s")
                        k.wait(y, "s", sm)
                        k.wait(y, "xd")
                        k.dma("y", W["wqT_" + wname][
                            :, ot * ncols + icg * 512:ot * ncols + (icg + 1) * 512],
                            stg[:, spar * 512:(spar + 1) * 512], sem="xd")
    k.bar()

    k.scope("q_xq" + uid)
    stQ = ExitStack()
    xq = k.sb("ff_xq" + uid, [128, NDT * NTOK], BF16, stQ)
    with ExitStack() as stX:
        xin = k.sb("ff_xin" + uid, [128, 2 * NDT * 512], BF16, stX)
        xi16 = k.sb("ff_xi16" + uid, [128, NDT * 512], I16, stX)
        xnh3 = xn_hbm[:, :].rearrange("p (a n) -> p a n", a=NDT)
        def ld_xq(cidx):
            par = cidx % 2
            dst = xin[:, par * NDT * 512:(par + 1) * NDT * 512].rearrange(
                "p (a c) -> p a c", a=NDT)
            sem = "xa" if par == 0 else "xc"
            return (sem, k.dma("y", dst,
                               xnh3[:, :, cidx * 512:(cidx + 1) * 512],
                               sem=sem))
        lds = [ld_xq(0), ld_xq(1)]
        for cidx in range(NCH):
            par = cidx % 2
            xb = xin[:, par * NDT * 512:(par + 1) * NDT * 512]
            k.wait(v, lds[cidx][0], lds[cidx][1])
            v.tensor_scalar(out=xi16[:, :], in0=xb, scalar1=scb[:, IX:IX + 1],
                            scalar2=None, op0=ALU.mult)
            for dt in range(NDT):
                v.tensor_scalar(
                    out=xq[:, dt * NTOK + cidx * 512:dt * NTOK + (cidx + 1) * 512],
                    in0=xi16[:, dt * 512:(dt + 1) * 512],
                    scalar1=scb[:, SX:SX + 1], scalar2=None, op0=ALU.mult)
            k.mark(v, "v")
            if cidx + 2 < NCH:
                k.wait(y, "v", k.c["v"])
                lds.append(ld_xq(cidx + 2))
    k.bar()

    k.scope("q_mm" + uid)
    gm = k.sb("ff_gm" + uid, [128, 2 * NIT * NCH], F32, stQ)
    with ExitStack() as stM:
        wl = k.sb("ff_wl" + uid, [128, 2 * NDT * 128], BF16, stM)
        ev = k.sb("ff_ev" + uid, [128, 4 * 512], BF16, stM)
        pid = [0, 0, 0, 0]
        psem = ["xd", "xb", "xg", "xh"]
        vv_bank = [0, 0, 0, 0]
        wl_tm = {0: 0, 1: 0}
        seq = [(wi, et) for wi in range(2) for et in range(NIT)]
        wnames = ("wgate", "wup")
        parks = ("pk_gate", "pk_up")
        wl_ld = {0: k.dma("y", wl[:, 0:1024], W["wqT_wgate"][:, 0:1024],
                          sem="xa"), 1: 0}
        for idx, (wi, et) in enumerate(seq):
            par = idx % 2
            wsl = wl[:, par * 1024:(par + 1) * 1024]
            sem = "xa" if par == 0 else "xc"
            if idx + 1 < len(seq):
                nwi, net = seq[idx + 1]
                npar = (idx + 1) % 2
                k.wait(y, "t", wl_tm[npar])
                wl_ld[npar] = k.dma(
                    "y", wl[:, npar * 1024:(npar + 1) * 1024],
                    W["wqT_" + wnames[nwi]][:, net * 1024:(net + 1) * 1024],
                    sem=("xa" if npar == 0 else "xc"))
            k.wait(t, sem, wl_ld[par])
            for c4 in range(NCH):
                ps = k.psum[c4]
                k.wait(t, "v", vv_bank[c4])
                for dt in range(NDT):
                    mi = t.matmul(ps[:, :],
                                  lhsT=wsl[:, dt * 128:(dt + 1) * 128],
                                  rhs=xq[:, dt * NTOK + c4 * 512:
                                         dt * NTOK + (c4 + 1) * 512],
                                  start=(dt == 0), stop=(dt == NDT - 1))
                tm = k.inc(mi, "t")
                if c4 == NCH - 1:
                    wl_tm[par] = tm
                k.wait(v, "t", tm)
                k.wait(v, psem[c4], pid[c4])
                v.tensor_copy(out=ev[:, c4 * 512:(c4 + 1) * 512], in_=ps[:, :])
                v.tensor_reduce(
                    out=gm[:, (wi * NIT + et) * NCH + c4:
                           (wi * NIT + et) * NCH + c4 + 1],
                    in_=ps[:, :], axis=AXX, op=ALU.max,
                    apply_absolute_value=True)
                vv_bank[c4] = k.mark(v, "v")
                k.wait(s, "v", vv_bank[c4])
                pid[c4] = k.dma(
                    "s", W[parks[wi]][:, et * NTOK + c4 * 512:
                                      et * NTOK + (c4 + 1) * 512],
                    ev[:, c4 * 512:(c4 + 1) * 512], sem=psem[c4])
        for wi in range(2):
            k.wait(v, "v")
            v.tensor_reduce(out=scb[:, 15:16],
                            in_=gm[:, wi * NIT * NCH:(wi + 1) * NIT * NCH],
                            axis=AXX, op=ALU.max)
            k.mark(v, "v")
            pcol_to_scr(scb[:, 15:16], MG + wi)
    stQ.close()
    allreduce_slots(MG, 2)
    v.tensor_scalar(out=scr[0:1, SG:SG + 1], in0=scr[0:1, MG:MG + 1],
                    scalar1=1.0 / QM, scalar2=1e-12, op0=ALU.mult, op1=ALU.max)
    v.reciprocal(out=scr[0:1, IG:IG + 1], in_=scr[0:1, SG:SG + 1])
    v.tensor_scalar(out=scr[0:1, SU:SU + 1], in0=scr[0:1, MU:MU + 1],
                    scalar1=1.0 / QM, scalar2=1e-12, op0=ALU.mult, op1=ALU.max)
    v.reciprocal(out=scr[0:1, IU:IU + 1], in_=scr[0:1, SU:SU + 1])
    k.mark(v, "v")
    bcast_cols([SG, IG, SU, IU])
    k.bar()

    k.scope("q_p234" + uid)
    CT = 256
    NCK = NTOK // CT
    NC6 = NIT * CT
    pkg = W["pk_gate"].rearrange("p (e n) -> p e n", e=NIT)
    pku = W["pk_up"].rearrange("p (e n) -> p e n", e=NIT)
    pkq = W["pk_gq"].rearrange("p (e n) -> p e n", e=NIT)
    pk2 = W["pk_g2"].rearrange("p (e n) -> p e n", e=NIT)
    with ExitStack() as stP:
        pin = k.sb("ff_pin" + uid, [128, 2 * NC6], BF16, stP)
        pq = k.sb("ff_pq" + uid, [128, 2 * NC6], I16, stP)
        pu16 = k.sb("ff_pu16" + uid, [128, NC6], I16, stP)
        psg = k.sb("ff_psg" + uid, [128, NC6], U16, stP)
        pt2 = k.sb("ff_pt2" + uid, [128, NC6], F32, stP)
        pm = k.sb("ff_pm" + uid, [128, NCK], F32, stP)

        def sig_t2(qb):
            vv = k.mark(v, "v")
            k.wait(s, "v", vv)
            s.activation(out=pt2[:, :], in_=qb, func=AF.Sigmoid,
                         scale=scb[:, SG:SG + 1])
            sm = k.mark(s, "s")
            k.wait(v, "s", sm)
            v.tensor_scalar(out=psg[:, :], in0=pt2[:, :], scalar1=65536.0,
                            scalar2=None, op0=ALU.mult)
            v.tensor_tensor(out=pt2[:, :], in0=qb, in1=psg[:, :], op=ALU.mult)

        def ld_p2(ck):
            par = ck % 2
            sem = "xa" if par == 0 else "xc"
            return (sem, k.dma(
                "y", pin[:, par * NC6:(par + 1) * NC6].rearrange(
                    "p (e n) -> p e n", e=NIT),
                pkg[:, :, ck * CT:(ck + 1) * CT], sem=sem))
        lds = [ld_p2(0), ld_p2(1)]
        pid = {0: 0, 1: 0}
        for ck in range(NCK):
            par = ck % 2
            pb = pin[:, par * NC6:(par + 1) * NC6]
            qb = pq[:, par * NC6:(par + 1) * NC6]
            k.wait(v, lds[ck][0], lds[ck][1])
            k.wait(v, "xd" if par == 0 else "xb", pid[par])
            v.tensor_scalar(out=qb, in0=pb, scalar1=scb[:, IG:IG + 1],
                            scalar2=None, op0=ALU.mult)
            sig_t2(qb)
            v.tensor_reduce(out=pm[:, ck:ck + 1], in_=pt2[:, :], axis=AXX,
                            op=ALU.max, apply_absolute_value=True)
            vv = k.mark(v, "v")
            k.wait(s, "v", vv)
            pid[par] = k.dma("s", pkq[:, :, ck * CT:(ck + 1) * CT],
                             qb.rearrange("p (e n) -> p e n", e=NIT),
                             sem="xd" if par == 0 else "xb")
            if ck + 2 < NCK:
                k.wait(y, "v", vv)
                lds.append(ld_p2(ck + 2))
        k.wait(v, "v")
        v.tensor_reduce(out=scb[:, 15:16], in_=pm[:, :], axis=AXX, op=ALU.max)
        k.mark(v, "v")
        pcol_to_scr(scb[:, 15:16], M2)
        allreduce_slots(M2, 1)
        v.tensor_tensor(out=scr[0:1, 16:17], in0=scr[0:1, M2:M2 + 1],
                        in1=scr[0:1, SG:SG + 1], op=ALU.mult)
        v.tensor_scalar(out=scr[0:1, 17:18], in0=scr[0:1, 16:17],
                        scalar1=1.0 / (65536.0 * QM), scalar2=1e-12,
                        op0=ALU.mult, op1=ALU.max)
        v.reciprocal(out=scr[0:1, 18:19], in_=scr[0:1, 17:18])
        v.tensor_tensor(out=scr[0:1, C2:C2 + 1], in0=scr[0:1, SG:SG + 1],
                        in1=scr[0:1, 18:19], op=ALU.mult)
        v.tensor_scalar(out=scr[0:1, C2:C2 + 1], in0=scr[0:1, C2:C2 + 1],
                        scalar1=1.0 / 65536.0, scalar2=None, op0=ALU.mult)
        k.mark(v, "v")
        bcast_cols([C2])
        k.bar()

        def ld_p3(ck):
            par = ck % 2
            sem = "xa" if par == 0 else "xc"
            k.dma("y", pq[:, par * NC6:(par + 1) * NC6].rearrange(
                "p (e n) -> p e n", e=NIT),
                pkq[:, :, ck * CT:(ck + 1) * CT], sem=sem)
            return (sem, k.dma(
                "y", pin[:, par * NC6:(par + 1) * NC6].rearrange(
                    "p (e n) -> p e n", e=NIT),
                pku[:, :, ck * CT:(ck + 1) * CT], sem=sem))
        lds = [ld_p3(0), ld_p3(1)]
        pid = {0: 0, 1: 0}
        for ck in range(NCK):
            par = ck % 2
            qb = pq[:, par * NC6:(par + 1) * NC6]
            ub = pin[:, par * NC6:(par + 1) * NC6]
            k.wait(v, lds[ck][0], lds[ck][1])
            k.wait(v, "xd" if par == 0 else "xb", pid[par])
            sig_t2(qb)
            v.tensor_scalar(out=qb, in0=pt2[:, :],
                            scalar1=scb[:, C2:C2 + 1], scalar2=None,
                            op0=ALU.mult)
            v.tensor_scalar(out=pu16[:, :], in0=ub, scalar1=scb[:, IU:IU + 1],
                            scalar2=None, op0=ALU.mult)
            v.tensor_tensor(out=pt2[:, :], in0=qb, in1=pu16[:, :],
                            op=ALU.mult)
            v.tensor_reduce(out=pm[:, ck:ck + 1], in_=pt2[:, :], axis=AXX,
                            op=ALU.max, apply_absolute_value=True)
            vv = k.mark(v, "v")
            k.wait(s, "v", vv)
            pid[par] = k.dma("s", pk2[:, :, ck * CT:(ck + 1) * CT],
                             qb.rearrange("p (e n) -> p e n", e=NIT),
                             sem="xd" if par == 0 else "xb")
            if ck + 2 < NCK:
                k.wait(y, "v", vv)
                k.wait(y, "xd" if par == 0 else "xb", pid[par])
                lds.append(ld_p3(ck + 2))
        k.wait(v, "v")
        v.tensor_reduce(out=scb[:, 15:16], in_=pm[:, :], axis=AXX, op=ALU.max)
        k.mark(v, "v")
        pcol_to_scr(scb[:, 15:16], M3)
        allreduce_slots(M3, 1)
        v.tensor_tensor(out=scr[0:1, 19:20], in0=scr[0:1, 17:18],
                        in1=scr[0:1, SU:SU + 1], op=ALU.mult)
        v.tensor_tensor(out=scr[0:1, 16:17], in0=scr[0:1, M3:M3 + 1],
                        in1=scr[0:1, 19:20], op=ALU.mult)
        v.tensor_scalar(out=scr[0:1, S3:S3 + 1], in0=scr[0:1, 16:17],
                        scalar1=1.0 / QM, scalar2=1e-12,
                        op0=ALU.mult, op1=ALU.max)
        v.reciprocal(out=scr[0:1, 20:21], in_=scr[0:1, S3:S3 + 1])
        v.tensor_tensor(out=scr[0:1, C3:C3 + 1], in0=scr[0:1, 19:20],
                        in1=scr[0:1, 20:21], op=ALU.mult)
        k.mark(v, "v")
        bcast_cols([C3, S3])
        k.bar()

        obuf = k.sb("ff_ob" + uid, [128, 2 * NC6], BF16, stP)
        wdl = k.sb("ff_wdl" + uid, [128, 2 * NIT * 128], BF16, stP)
        wdq = W["wqT_wdown"]
        def ld_p4(ck):
            par = ck % 2
            sem = "xa" if par == 0 else "xc"
            k.dma("y", pq[:, par * NC6:(par + 1) * NC6].rearrange(
                "p (e n) -> p e n", e=NIT),
                pk2[:, :, ck * CT:(ck + 1) * CT], sem=sem)
            return (sem, k.dma(
                "y", pin[:, par * NC6:(par + 1) * NC6].rearrange(
                    "p (e n) -> p e n", e=NIT),
                pku[:, :, ck * CT:(ck + 1) * CT], sem=sem))
        lds = [ld_p4(0), ld_p4(1)]
        for ck in range(NCK):
            par = ck % 2
            g2 = pq[:, par * NC6:(par + 1) * NC6]
            ub = pin[:, par * NC6:(par + 1) * NC6]
            ob = obuf[:, par * NC6:(par + 1) * NC6]
            k.wait(v, lds[ck][0], lds[ck][1])
            k.wait(v, "t")
            v.tensor_scalar(out=pu16[:, :], in0=ub, scalar1=scb[:, IU:IU + 1],
                            scalar2=None, op0=ALU.mult)
            v.tensor_tensor(out=pt2[:, :], in0=g2, in1=pu16[:, :], op=ALU.mult)
            v.tensor_scalar(out=pu16[:, :], in0=pt2[:, :],
                            scalar1=scb[:, C3:C3 + 1], scalar2=None,
                            op0=ALU.mult)
            v.tensor_scalar(out=ob, in0=pu16[:, :], scalar1=scb[:, S3:S3 + 1],
                            scalar2=None, op0=ALU.mult)
            vv = k.mark(v, "v")
            if ck + 2 < NCK:
                k.wait(y, "v", vv)
                lds.append(ld_p4(ck + 2))
            for et in range(NDT):
                wpar = et % 2
                wsem = "xe" if wpar == 0 else "xf"
                wsl = wdl[:, wpar * NIT * 128:(wpar + 1) * NIT * 128]
                if ck > 0 or et >= 2:
                    k.wait(g, "t")
                ldw = k.dma("g", wsl, wdq[:, et * IFF:(et + 1) * IFF],
                            sem=wsem)
                k.wait(t, wsem, ldw)
                k.wait(t, "v", vv)
                ps = k.psum[4 + et % 2]
                if ck > 0 or et >= 2:
                    k.wait(t, "v")
                for ic in range(NIT):
                    mi = t.matmul(ps[:, 0:CT],
                                  lhsT=wsl[:, ic * 128:(ic + 1) * 128],
                                  rhs=ob[:, ic * CT:(ic + 1) * CT],
                                  start=(ic == 0), stop=(ic == NIT - 1))
                tm = k.inc(mi, "t")
                k.wait(v, "t", tm)
                rsl = R[:, et * NTOK + ck * CT:et * NTOK + (ck + 1) * CT]
                v.scalar_tensor_tensor(out=rsl, in0=ps[:, 0:CT], scalar=0.5,
                                       in1=rsl, op0=ALU.mult, op1=ALU.add)
                k.mark(v, "v")
    if dbg is not None:
        k.wait(v, "v")
        v.tensor_copy(out=dbg[0:1, :], in_=scr[0:1, :])
        k.mark(v, "v")
    k.bar()


DIN = 2048
NET = DIN // 128
LCH = 64
NBT = 16


def emit_l1_attn(k: K, R, xn_hbm, W, st, uid=""):
    t, v, s, g, y = k.t, k.v, k.s, k.g, k.y
    xnh3 = xn_hbm[:, :].rearrange("p (a n) -> p a n", a=NDT)
    xih = W["xi_hbm"].rearrange("p (a n) -> p a n", a=NET)
    zsh = W["zs_hbm"].rearrange("p (a n) -> p a n", a=NET)
    slh = W["sl_hbm"].rearrange("p (a n) -> p a n", a=NET)
    xph = W["xp_hbm"]
    g1h = W["g1_hbm"]

    Bin = k.sb("la_bin" + uid, [128, DIN], F32, st)
    Cin = k.sb("la_cin" + uid, [128, DIN], F32, st)
    abp = k.sb("la_abp" + uid, [128, 5 * LCH], F32, st)
    abc = k.sb("la_abc" + uid, [128, 8], F32, st)
    abrep = k.sb("la_abrep" + uid, [128, 32], F32, st)
    dcol = k.sb("la_dcol" + uid, [128, 16], F32, st)
    thc = k.sb("la_thc" + uid, [128, B * 8 * 16], F32, st)
    aec = k.sb("la_aec" + uid, [128, B * 8 * 16], F32, st)
    tri = k.sb("la_tri" + uid, [128, 128], BF16, st)
    kmask = k.sb("la_kmask" + uid, [128, LCH], BF16, st)
    taus = k.sb("la_tau" + uid, [128, 128], F32, st)
    ab1 = k.sb("la_ab1" + uid, [128, 1], F32, st)
    ones1f = k.sb("la_o1f" + uid, [1, 128], F32, st)
    prow = k.sb("la_prow" + uid, [1, 32], F32, st)
    v.memset(ones1f[:, :], 1.0)

    k.dma("g", tri[:, :], W["tri"][:, :])
    k.dma("g", kmask[:, :], W["kmask"][:, :])
    k.dma("y", taus[:, :], W["tau"][:, :])
    k.dma("y", ab1[:, :], W["abase"][:, :])
    k.dma("y", dcol[:, :], W["dcolh"][:, :])
    k.bar()

    k.scope("a_pat" + uid)
    s.activation(out=abc[:, 0:1], in_=ab1[:, 0:1], func=AF.Sigmoid)
    sm = k.mark(s, "s")
    k.wait(s, "s", sm)
    s.activation(out=abc[:, 1:2], in_=abc[:, 0:1], func=AF.Ln)
    sm = k.mark(s, "s")
    k.wait(s, "s", sm)
    s.activation(out=abc[:, 2:3], in_=abc[:, 1:2], func=AF.Exp, scale=64.0)
    s.activation(out=abc[:, 3:4], in_=abc[:, 1:2], func=AF.Exp, scale=256.0)
    sm = k.mark(s, "s")
    k.wait(v, "s", sm)
    v.tensor_scalar(out=abp[:, 4 * LCH:5 * LCH], in0=taus[:, 0:LCH],
                    scalar1=abc[:, 1:2], scalar2=None, op0=ALU.mult)
    vv = k.mark(v, "v")
    k.wait(s, "v", vv)
    s.activation(out=abp[:, 0:LCH], in_=abp[:, 4 * LCH:5 * LCH], func=AF.Exp,
                 scale=-1.0)
    s.activation(out=abp[:, LCH:2 * LCH], in_=abp[:, 4 * LCH:5 * LCH],
                 func=AF.Exp)
    sm = k.mark(s, "s")
    k.wait(v, "s", sm)
    v.tensor_scalar(out=abp[:, 2 * LCH:3 * LCH], in0=abp[:, LCH:2 * LCH],
                    scalar1=abc[:, 0:1], scalar2=None, op0=ALU.mult)
    v.tensor_scalar(out=abp[:, 4 * LCH:5 * LCH], in0=taus[:, LCH:2 * LCH],
                    scalar1=abc[:, 1:2], scalar2=None, op0=ALU.mult)
    vv = k.mark(v, "v")
    k.wait(s, "v", vv)
    s.activation(out=abp[:, 3 * LCH:4 * LCH], in_=abp[:, 4 * LCH:5 * LCH],
                 func=AF.Exp)
    k.mark(s, "s")
    k.wait(y, "s")
    wr = k.dma("y", W["colb"][0:1, 0:32], abc[0:16, 2:4], sem="xd")
    k.wait(y, "xd", wr)
    rd = k.dma("y", prow[0:1, :], W["colb"][0:1, 0:32], sem="xd")
    k.wait(t, "xd", rd)
    mi = t.matmul(k.psum[6][:, 0:16], lhsT=ones1f[0:1, :],
                  rhs=prow[0:1, 0:32].rearrange("o (p c) -> o (c p)", p=16)[
                      :, 0:16] if False else
                  prow[0:1, 0:32].rearrange("o (p c) -> o p c", p=16)[:, :, 0],
                  start=True, stop=True)
    mi = t.matmul(k.psum[6][:, 16:32], lhsT=ones1f[0:1, :],
                  rhs=prow[0:1, 0:32].rearrange("o (p c) -> o p c", p=16)[:, :, 1],
                  start=False, stop=True, skip_group_check=True)
    tm = k.inc(mi, "t")
    k.wait(v, "t", tm)
    v.tensor_copy(out=abrep[:, :], in_=k.psum[6][:, 0:32])
    k.mark(v, "v")
    k.bar()

    k.scope("a_ip" + uid)
    with ExitStack() as stP:
        wbuf = k.sb("la_wb" + uid, [128, NDT * DIN], BF16, stP)
        xnc = k.sb("la_xnc" + uid, [128, 2 * NDT * 512], BF16, stP)
        ev = k.sb("la_ev" + uid, [128, 2 * 512], BF16, stP)
        pid = {0: 0, 1: 0}
        for half in range(2):
            k.wait(g, "t", k.c["t"])
            hw = k.dma("g", wbuf[:, :],
                       W["ipw"].rearrange("p (a n) -> p a n", a=NDT)[
                           :, :, half * DIN:(half + 1) * DIN], sem="xf")
            k.wait(t, "xf", hw)
            for c4 in range(NCH):
                par = c4 % 2
                xb = xnc[:, par * NDT * 512:(par + 1) * NDT * 512]
                sem = "xa" if par == 0 else "xc"
                k.wait(y, "t", k.c["t"])
                ld = k.dma("y", xb.rearrange("p (a c) -> p a c", a=NDT),
                           xnh3[:, :, c4 * 512:(c4 + 1) * 512], sem=sem)
                k.wait(t, sem, ld)
                for et in range(NET):
                    ps = k.psum[et % 2]
                    if et >= 2 or c4 >= 1 or half > 0:
                        k.wait(t, "s")
                    for dt in range(NDT):
                        mi = t.matmul(
                            ps[:, :],
                            lhsT=wbuf[:, dt * DIN + half * 0 + et * 128:
                                      dt * DIN + (et + 1) * 128],
                            rhs=xb[:, dt * 512:(dt + 1) * 512],
                            start=(dt == 0), stop=(dt == NDT - 1))
                    tm = k.inc(mi, "t")
                    k.wait(s, "t", tm)
                    epar = et % 2
                    k.wait(s, "xd" if epar == 0 else "xb", pid[epar])
                    s.activation(out=ev[:, epar * 512:(epar + 1) * 512],
                                 in_=ps[:, :], func=AF.Silu)
                    k.mark(s, "s")
                    dsth = xih if half == 0 else zsh
                    pid[epar] = k.dma(
                        "s", dsth[:, half * 0 + et, c4 * 512:(c4 + 1) * 512],
                        ev[:, epar * 512:(epar + 1) * 512],
                        sem="xd" if epar == 0 else "xb")
    k.bar()

    k.scope("a_ap" + uid)
    with ExitStack() as stA:
        wbuf = k.sb("la_wb2" + uid, [128, NET * 1024], BF16, stA)
        bwc = k.sb("la_bwc" + uid, [128, 2 * NET * 16], BF16, stA)
        xic = k.sb("la_xic" + uid, [128, 2 * NET * 256], BF16, stA)
        slq = k.sb("la_slq" + uid, [128, 2 * 8 * 256], BF16, stA)
        acol = k.sb("la_acol" + uid, [128, 16], F32, stA)
        k.dma("y", acol[:, :], W["acolh"][:, :])
        k.dma("g", bwc[:, 0:NET * 16], W["bw"][:, :])
        k.dma("g", bwc[:, NET * 16:2 * NET * 16], W["cw"][:, :])
        k.wait(t, "gd")
        k.wait(s, "yd")
        spid = {0: 0, 1: 0}
        for half in range(2):
            k.wait(g, "t", k.c["t"])
            hw = k.dma("g", wbuf[:, :],
                       W["apw"].rearrange("p (a n) -> p a n", a=NET)[
                           :, :, half * 1024:(half + 1) * 1024], sem="xf")
            k.wait(t, "xf", hw)
            for c8 in range(8):
                par = c8 % 2
                xb = xic[:, par * NET * 256:(par + 1) * NET * 256]
                sem = "xa" if par == 0 else "xc"
                k.wait(y, "t", k.c["t"])
                ld = k.dma("y", xb.rearrange("p (a c) -> p a c", a=NET),
                           xih[:, :, c8 * 256:(c8 + 1) * 256], sem=sem)
                k.wait(t, sem, ld)
                spar = (half * 8 + c8) % 2
                sbase = spar * 2048
                k.wait(s, "xd" if spar == 0 else "xb", spid[spar])
                for eh in range(8):
                    et = half * 8 + eh
                    ps = k.psum[eh % 2]
                    if eh >= 2 or c8 >= 1 or half > 0:
                        k.wait(t, "s")
                    for dt in range(NET):
                        mi = t.matmul(
                            ps[:, 0:256],
                            lhsT=wbuf[:, dt * 1024 + eh * 128:
                                      dt * 1024 + (eh + 1) * 128],
                            rhs=xb[:, dt * 256:(dt + 1) * 256],
                            start=(dt == 0), stop=(dt == NET - 1))
                    tm = k.inc(mi, "t")
                    k.wait(s, "t", tm)
                    s.activation(out=slq[:, sbase + eh * 256:
                                         sbase + (eh + 1) * 256],
                                 in_=ps[:, 0:256], func=AF.Sigmoid,
                                 bias=acol[:, et:et + 1])
                    k.mark(s, "s")
                for q in range(4):
                    s.activation(out=slq[:, sbase + q * 512:
                                         sbase + (q + 1) * 512],
                                 in_=slq[:, sbase + q * 512:
                                         sbase + (q + 1) * 512], func=AF.Ln)
                k.mark(s, "s")
                spid[spar] = k.dma(
                    "s", slh[:, half * 8:half * 8 + 8,
                             c8 * 256:(c8 + 1) * 256],
                    slq[:, sbase:sbase + 2048].rearrange(
                        "p (a c) -> p a c", a=8),
                    sem="xd" if spar == 0 else "xb")
                if half == 0:
                    k.wait(t, "v")
                    for which, bank in ((0, 2), (1, 3)):
                        for dt in range(NET):
                            mi = t.matmul(
                                k.psum[bank][0:16, 0:256],
                                lhsT=bwc[:, which * NET * 16 + dt * 16:
                                         which * NET * 16 + (dt + 1) * 16],
                                rhs=xb[:, dt * 256:(dt + 1) * 256],
                                start=(dt == 0), stop=(dt == NET - 1))
                        tm = k.inc(mi, "t")
                        k.wait(v, "t", tm)
                        dst = Bin if which == 0 else Cin
                        v.tensor_copy(out=dst[0:16, c8 * 256:(c8 + 1) * 256],
                                      in_=k.psum[bank][0:16, 0:256])
                        k.mark(v, "v")
        k.wait(y, "v", k.c["v"])
        k.dma("y", Bin[64:80, :], Bin[0:16, :])
        k.dma("y", Cin[64:80, :], Cin[0:16, :])
    k.bar()

    k.scope("a_phA" + uid)
    NCHB = 4
    hincD = k.sb("la_hincD" + uid, [128, B * NCHB * 256], BF16, st)
    smry = k.sb("la_smry" + uid, [128, B * 272], F32, st)
    hin = k.sb("la_hin" + uid, [128, B * 256], F32, st)
    with ExitStack() as stS:
        slb = k.sb("la_slb" + uid, [128, 2 * DIN], BF16, stS)
        xib = k.sb("la_xib" + uid, [128, 2 * DIN], BF16, stS)
        sltT = k.sb("la_sltT" + uid, [128, DIN], BF16, stS)
        xinT = k.sb("la_xinT" + uid, [128, DIN], BF16, stS)
        exb = k.sb("la_exb" + uid, [128, 2 * DIN], BF16, stS)
        xpr = k.sb("la_xpr" + uid, [128, DIN], BF16, stS)
        erb = k.sb("la_erb" + uid, [128, 512], BF16, stS)
        bb2 = k.sb("la_bb2" + uid, [128, LCH], BF16, stS)
        lhsb = k.sb("la_lhsb" + uid, [128, 32], BF16, stS)
        hl = k.sb("la_hl" + uid, [128, 256], F32, stS)
        ecol = k.sb("la_ecol" + uid, [128, 64], F32, stS)
        stg = k.sb("la_stg" + uid, [128, 512], BF16, stS)
        v.memset(lhsb[:, :], 0.0)
        k.mark(v, "v")
        for bb in range(B):
            v.memset(hl[:, :], 0.0)
            k.mark(v, "v")
            for tt in range(2):
                bt = bb * 2 + tt
                par = bt % 2
                sem = "xa" if par == 0 else "xc"
                sem2 = "xb" if par == 0 else "xd"
                sb_ = slb[:, par * DIN:(par + 1) * DIN]
                xb_ = xib[:, par * DIN:(par + 1) * DIN]
                k.wait(y, "v", k.c["v"])
                k.wait(y, "t", k.c["t"])
                ld = k.dma("y", sb_.rearrange("p (a c) -> p a c", a=NET),
                           slh[:, :, bb * 256 + tt * 128:
                               bb * 256 + tt * 128 + 128], sem=sem)
                ld2 = k.dma("y", xb_.rearrange("p (a c) -> p a c", a=NET),
                            xih[:, :, bb * 256 + tt * 128:
                                bb * 256 + tt * 128 + 128], sem=sem2)
                k.wait(s, "xg")
                k.wait(v, "xg")
                k.wait(v, sem, ld)
                for ch01 in range(2):
                    c = tt * 2 + ch01
                    base = (bb * NCHB + c) * 16
                    v.tensor_reduce(
                        out=thc[:, base:base + 16],
                        in_=sb_.rearrange("p (a c) -> p a c", a=NET)[
                            :, :, ch01 * 64:ch01 * 64 + 32],
                        axis=AXX, op=ALU.add)
                    v.tensor_reduce(
                        out=aec[:, base:base + 16],
                        in_=sb_.rearrange("p (a c) -> p a c", a=NET)[
                            :, :, ch01 * 64:(ch01 + 1) * 64],
                        axis=AXX, op=ALU.add)
                vv = k.mark(v, "v")
                k.wait(t, "v", vv)
                k.wait(t, sem2, ld2)
                for src_t, dst_t in ((sb_, sltT), (xb_, xinT)):
                    for grp in range(4):
                        k.wait(t, "v")
                        for j in range(4):
                            et = grp * 4 + j
                            mi = t.transpose(
                                k.psum_bf[:, j * 128:(j + 1) * 128],
                                in_=src_t[:, et * 128:(et + 1) * 128],
                                identity=W["identb"])
                        tm = k.inc(mi, "t")
                        k.wait(v, "t", tm)
                        v.tensor_copy(out=dst_t[:, grp * 512:(grp + 1) * 512],
                                      in_=k.psum_bf[:, 0:512])
                        k.mark(v, "v")
                k.wait(t, "v")
                for ch01 in range(2):
                    off = ch01 * 64
                    for nch in range(4):
                        ps = k.psum[2 + ch01]
                        ps_s = k.psum[4 + ch01]
                        k.wait(t, "s")
                        rh = sltT[off:off + 64, nch * 512:(nch + 1) * 512]
                        t.matmul(ps[off:off + 64, :],
                                 lhsT=tri[off:off + 64, 0:64], rhs=rh,
                                 start=True, stop=True)
                        mi = t.matmul(ps_s[off:off + 64, :],
                                      lhsT=tri[off:off + 64, 64:128], rhs=rh,
                                      start=True, stop=True)
                        tm = k.inc(mi, "t")
                        k.wait(s, "t", tm)
                        k.wait(s, "v")
                        nsl = slice(nch * 512, (nch + 1) * 512)
                        s.activation(out=exb[off:off + 64, nsl],
                                     in_=ps[off:off + 64, :], func=AF.Exp,
                                     scale=-1.0)
                        s.activation(out=exb[off:off + 64, DIN + nch * 512:
                                             DIN + (nch + 1) * 512],
                                     in_=ps[off:off + 64, :], func=AF.Exp)
                        s.activation(out=erb[off:off + 64, 0:512],
                                     in_=ps_s[off:off + 64, :],
                                     func=AF.Exp)
                        sm = k.mark(s, "s")
                        k.wait(v, "s", sm)
                        v.tensor_tensor(out=xpr[off:off + 64, nsl],
                                        in0=xinT[off:off + 64, nsl],
                                        in1=erb[off:off + 64, 0:512],
                                        op=ALU.mult)
                        k.mark(v, "v")
                k.wait(v, "v")
                for q in range(4):
                    qs = slice(q * 512, (q + 1) * 512)
                    v.tensor_tensor(out=exb[:, qs], in0=xinT[:, qs],
                                    in1=exb[:, qs], op=ALU.mult)
                vv = k.mark(v, "v")
                k.wait(y, "v", vv)
                k.dma("y", xph[:, bt * DIN:(bt + 1) * DIN], exb[:, 0:DIN],
                      sem="xg")
                k.dma("y", g1h[:, bt * DIN:(bt + 1) * DIN],
                      exb[:, DIN:2 * DIN], sem="xg")
                for ch01 in range(2):
                    off = ch01 * 64
                    k.wait(v, "t")
                    v.tensor_tensor(
                        out=bb2[off:off + 16, :],
                        in0=Bin[off:off + 16,
                                bb * 256 + tt * 128 + ch01 * 64:
                                bb * 256 + tt * 128 + (ch01 + 1) * 64],
                        in1=abp[off:off + 16, 3 * LCH:4 * LCH], op=ALU.mult)
                    vv = k.mark(v, "v")
                    k.wait(t, "v", vv)
                    mi = t.transpose(k.psum_bf[off:off + 64, 768:784],
                                     in_=bb2[off:off + 16, :],
                                     identity=W["identb"][off:off + 16,
                                                          off:off + 16])
                    tm = k.inc(mi, "t")
                    k.wait(v, "t", tm)
                    v.tensor_copy(out=lhsb[off:off + 64,
                                           ch01 * 16:(ch01 + 1) * 16],
                                  in_=k.psum_bf[off:off + 64, 768:784])
                    k.mark(v, "v")
                vv = k.c["v"]
                k.wait(t, "v", vv)
                for nch in range(4):
                    ps6 = k.psum[6]
                    k.wait(t, "s")
                    mi = t.matmul(ps6[0:32, 0:512], lhsT=lhsb[:, :],
                                  rhs=xpr[:, nch * 512:(nch + 1) * 512],
                                  start=True, stop=True)
                    tm = k.inc(mi, "t")
                    k.wait(s, "t", tm)
                    s.copy(out=stg[0:32, 0:512], in_=ps6[0:32, 0:512])
                    sm = k.mark(s, "s")
                    k.wait(t, "s", sm)
                    k.wait(t, "v")
                    for j in range(4):
                        mi = t.transpose(k.psum_bf[:, 768 + j * 32:
                                                   768 + (j + 1) * 32],
                                         in_=stg[0:32, j * 128:(j + 1) * 128],
                                         identity=W["identb"][0:32, 0:32])
                    tm = k.inc(mi, "t")
                    k.wait(v, "t", tm)
                    for j in range(4):
                        et = nch * 4 + j
                        for ch01 in range(2):
                            c = tt * 2 + ch01
                            dst = hincD[:, ((bb * NCHB + c) * 16 + et) * 16:
                                        ((bb * NCHB + c) * 16 + et) * 16 + 16]
                            v.tensor_copy(
                                out=dst,
                                in_=k.psum_bf[:, 768 + j * 32 + ch01 * 16:
                                              768 + j * 32 + (ch01 + 1) * 16])
                    k.mark(v, "v")
                for ch01 in range(2):
                    c = tt * 2 + ch01
                    base = (bb * NCHB + c) * 16
                    k.wait(s, "v")
                    s.activation(out=ecol[:, 0:16], in_=aec[:, base:base + 16],
                                 func=AF.Exp)
                    sm = k.mark(s, "s")
                    k.wait(v, "s", sm)
                    v.tensor_tensor(
                        out=hl[:, :].rearrange("p (e n) -> p e n", e=16),
                        in0=hl[:, :].rearrange("p (e n) -> p e n", e=16),
                        in1=ecol[:, 0:16].rearrange("p (e o) -> p e o", o=1)
                        .broadcast_to([128, 16, 16]), op=ALU.mult)
                    v.tensor_tensor(
                        out=hl[:, :].rearrange("p (e n) -> p e n", e=16),
                        in0=hl[:, :].rearrange("p (e n) -> p e n", e=16),
                        in1=abrep[:, 0:16].rearrange("p (o n) -> p o n", o=1)
                        .broadcast_to([128, 16, 16]), op=ALU.mult)
                    v.tensor_tensor(out=hl[:, :], in0=hl[:, :],
                                    in1=hincD[:, base * 16:base * 16 + 256],
                                    op=ALU.add)
                    k.mark(v, "v")
            base0 = (bb * NCHB) * 16
            k.wait(v, "v")
            v.tensor_tensor(out=ecol[:, 16:32], in0=aec[:, base0:base0 + 16],
                            in1=aec[:, base0 + 16:base0 + 32], op=ALU.add)
            v.tensor_tensor(out=ecol[:, 32:48],
                            in0=aec[:, base0 + 32:base0 + 48],
                            in1=aec[:, base0 + 48:base0 + 64], op=ALU.add)
            v.tensor_tensor(out=ecol[:, 16:32], in0=ecol[:, 16:32],
                            in1=ecol[:, 32:48], op=ALU.add)
            vv = k.mark(v, "v")
            k.wait(s, "v", vv)
            s.activation(out=smry[:, bb * 272:bb * 272 + 16],
                         in_=ecol[:, 16:32], func=AF.Exp)
            k.mark(s, "s")
            k.wait(v, "s")
            v.tensor_copy(out=smry[:, bb * 272 + 16:bb * 272 + 272],
                          in_=hl[:, :])
            k.mark(v, "v")
    k.bar()

    k.scope("a_fold" + uid)
    foldm = k.sb("la_foldm" + uid, [128, 16], F32, st)
    smb = k.sb("la_smb" + uid, [128, B * 272], BF16, st)
    k.dma("y", foldm[:, :], W["foldmh"][:, :])
    k.wait(v, "yd")
    for bb in range(B):
        v.tensor_copy(out=smb[:, bb * 272:(bb + 1) * 272],
                      in_=smry[:, bb * 272:(bb + 1) * 272])
    vv = k.mark(v, "v")
    k.wait(g, "v", vv)
    k.dma("g", W["cc3_in"][:, :], smb[:, :])
    g.wait_ge(k.sems["gd"], k.c["gd"])
    ci = g.collective_compute(
        "AllGather", ALU.bypass, replica_groups=[list(range(NCORES))],
        ins=[W["cc3_in"].opt()], outs=[W["cc3_out"].opt()])
    k.inc(ci, "cc")
    k.wait(y, "cc", k.c["cc"])
    with ExitStack() as stF:
        rsm = k.sb("la_rsm" + uid, [128, B * 272], BF16, stF)
        dmt = k.sb("la_dmt" + uid, [128, 256], F32, stF)
        abn = k.sb("la_abn" + uid, [128, 256], F32, stF)
        v.memset(hin[:, :], 0.0)
        for e in range(16):
            v.tensor_copy(out=abn[:, e * 16:(e + 1) * 16],
                          in_=abrep[:, 16:32])
        k.mark(v, "v")
        for r in range(NCORES - 1):
            k.wait(y, "v", k.c["v"])
            ldr = k.dma("y", rsm[:, :],
                        W["cc3_out"][128 * r:128 * (r + 1), :], sem="xa")
            k.wait(v, "xa", ldr)
            for bb in range(B):
                ap_ = rsm[:, bb * 272:bb * 272 + 16]
                hf_ = rsm[:, bb * 272 + 16:bb * 272 + 272]
                hb = hin[:, bb * 256:(bb + 1) * 256]
                v.tensor_tensor(
                    out=dmt[:, :].rearrange("p (e n) -> p e n", e=16),
                    in0=abn[:, :].rearrange("p (e n) -> p e n", e=16),
                    in1=ap_.rearrange("p (e o) -> p e o", o=1)
                    .broadcast_to([128, 16, 16]), op=ALU.mult)
                v.tensor_scalar(out=dmt[:, :], in0=dmt[:, :],
                                scalar1=foldm[:, r:r + 1],
                                scalar2=foldm[:, 8 + r:9 + r],
                                op0=ALU.mult, op1=ALU.add)
                v.tensor_tensor(out=hb, in0=hb, in1=dmt[:, :], op=ALU.mult)
                v.scalar_tensor_tensor(out=hb, in0=hf_,
                                       scalar=foldm[:, r:r + 1], in1=hb,
                                       op0=ALU.mult, op1=ALU.add)
            k.mark(v, "v")
    k.bar()

    k.scope("a_phB" + uid)
    with ExitStack() as stB:
        opw = k.sb("la_opw" + uid, [128, NET * 1024], BF16, stB)
        xpb = k.sb("la_xpb" + uid, [128, 2 * DIN], BF16, stB)
        g1b = k.sb("la_g1b" + uid, [128, 2 * DIN], BF16, stB)
        bcc = k.sb("la_bcc" + uid, [128, 2 * LCH], BF16, stB)
        c3b = k.sb("la_c3b" + uid, [128, LCH], BF16, stB)
        ktb = k.sb("la_ktb" + uid, [128, LCH], BF16, stB)
        httr = k.sb("la_httr" + uid, [128, DIN], BF16, stB)
        ecol2 = k.sb("la_ecol2" + uid, [128, 48], F32, stB)
        htl = k.sb("la_htl" + uid, [128, 256], BF16, stB)
        yrow = k.sb("la_yrow" + uid, [128, DIN], BF16, stB)
        yT = k.sb("la_yT" + uid, [128, 2 * DIN], BF16, stB)
        zsb = k.sb("la_zsb" + uid, [128, DIN], BF16, stB)
        xrb = k.sb("la_xrb" + uid, [128, DIN], BF16, stB)
        ht = k.sb("la_ht" + uid, [128, 256], F32, stB)
        k.dma("g", opw[:, :], W["opw"][:, :], sem="gd")
        for bb in range(B):
            k.wait(v, "v")
            k.wait(v, "t")
            v.tensor_copy(out=ht[:, :], in_=hin[:, bb * 256:(bb + 1) * 256])
            k.mark(v, "v")
            for tt in range(2):
                bt = bb * 2 + tt
                par = tt % 2
                xp_ = xpb[:, par * DIN:(par + 1) * DIN]
                g1_ = g1b[:, par * DIN:(par + 1) * DIN]
                k.wait(y, "t", k.c["t"])
                k.wait(y, "v", k.c["v"])
                k.dma("y", xp_, xph[:, bt * DIN:(bt + 1) * DIN], sem="xa")
                l2 = k.dma("y", g1_, g1h[:, bt * DIN:(bt + 1) * DIN], sem="xa")
                for ch01 in range(2):
                    off = ch01 * 64
                    c = tt * 2 + ch01
                    base = (bb * NCHB + c) * 16
                    tksl = slice(bb * 256 + tt * 128 + ch01 * 64,
                                 bb * 256 + tt * 128 + (ch01 + 1) * 64)
                    k.wait(v, "t")
                    v.tensor_tensor(out=bcc[off:off + 16, 0:LCH],
                                    in0=Bin[off:off + 16, tksl],
                                    in1=abp[off:off + 16, 0:LCH], op=ALU.mult)
                    v.tensor_tensor(out=bcc[off:off + 16, LCH:2 * LCH],
                                    in0=Cin[off:off + 16, tksl],
                                    in1=abp[off:off + 16, LCH:2 * LCH],
                                    op=ALU.mult)
                    v.tensor_tensor(out=c3b[off:off + 16, :],
                                    in0=Cin[off:off + 16, tksl],
                                    in1=abp[off:off + 16, 2 * LCH:3 * LCH],
                                    op=ALU.mult)
                    vv = k.mark(v, "v")
                    k.wait(s, "v", vv)
                    s.activation(out=ecol2[:, 0:16],
                                 in_=thc[:, base:base + 16], func=AF.Exp)
                    s.activation(out=ecol2[:, 16:32],
                                 in_=aec[:, base:base + 16], func=AF.Exp)
                    sm = k.mark(s, "s")
                    k.wait(v, "s", sm)
                    v.tensor_tensor(
                        out=htl[:, :].rearrange("p (e n) -> p e n", e=16),
                        in0=ht[:, :].rearrange("p (e n) -> p e n", e=16),
                        in1=ecol2[:, 0:16].rearrange("p (e o) -> p e o", o=1)
                        .broadcast_to([128, 16, 16]), op=ALU.mult)
                    v.tensor_tensor(
                        out=ht[:, :].rearrange("p (e n) -> p e n", e=16),
                        in0=ht[:, :].rearrange("p (e n) -> p e n", e=16),
                        in1=ecol2[:, 16:32].rearrange("p (e o) -> p e o", o=1)
                        .broadcast_to([128, 16, 16]), op=ALU.mult)
                    v.tensor_tensor(
                        out=ht[:, :].rearrange("p (e n) -> p e n", e=16),
                        in0=ht[:, :].rearrange("p (e n) -> p e n", e=16),
                        in1=abrep[:, 0:16].rearrange("p (o n) -> p o n", o=1)
                        .broadcast_to([128, 16, 16]), op=ALU.mult)
                    v.tensor_tensor(out=ht[:, :], in0=ht[:, :],
                                    in1=hincD[:, base * 16:base * 16 + 256],
                                    op=ALU.add)
                    vv = k.mark(v, "v")
                    k.wait(t, "v", vv)
                    for grp in range(2):
                        k.wait(t, "s")
                        for j in range(8):
                            et = grp * 8 + j
                            mi = t.transpose(
                                k.psum_bf[off:off + 16,
                                          j * 128:(j + 1) * 128],
                                in_=htl[:, et * 16:(et + 1) * 16],
                                identity=W["identb"])
                        tm = k.inc(mi, "t")
                        k.wait(s, "t", tm)
                        s.copy(out=httr[off:off + 16,
                                        grp * 1024:(grp + 1) * 1024],
                               in_=k.psum_bf[off:off + 16, 0:1024])
                        k.mark(s, "s")
                    k.wait(t, "s")
                    k.wait(t, "v")
                    mi = t.matmul(k.psum[2 + ch01][off:off + 64, 0:64],
                                  lhsT=bcc[off:off + 16, 0:LCH],
                                  rhs=bcc[off:off + 16, LCH:2 * LCH],
                                  start=True, stop=True)
                    tm = k.inc(mi, "t")
                    k.wait(v, "t", tm)
                    v.tensor_tensor(out=ktb[off:off + 64, :],
                                    in0=k.psum[2 + ch01][off:off + 64, 0:64],
                                    in1=kmask[off:off + 64, :], op=ALU.mult)
                    k.mark(v, "v")
                k.wait(t, "xa", l2)
                k.wait(t, "v")
                for nch in range(4):
                    nsl = slice(nch * 512, (nch + 1) * 512)
                    k.wait(t, "v")
                    for ch01 in range(2):
                        off = ch01 * 64
                        psy = k.psum[4 + ch01]
                        t.matmul(psy[off:off + 64, 0:512],
                                 lhsT=ktb[off:off + 64, :],
                                 rhs=xp_[off:off + 64, nsl],
                                 start=True, stop=False,
                                 skip_group_check=True)
                    for ch01 in range(2):
                        off = ch01 * 64
                        psy = k.psum[4 + ch01]
                        mi = t.matmul(psy[off:off + 64, 0:512],
                                      lhsT=c3b[off:off + 16, :],
                                      rhs=httr[off:off + 16, nsl],
                                      start=False, stop=True,
                                      skip_group_check=True)
                    tm = k.inc(mi, "t")
                    k.wait(v, "t", tm)
                    for ch01 in range(2):
                        off = ch01 * 64
                        psy = k.psum[4 + ch01]
                        v.tensor_tensor(out=yrow[off:off + 64, nsl],
                                        in0=psy[off:off + 64, 0:512],
                                        in1=g1_[off:off + 64, nsl],
                                        op=ALU.mult)
                    k.mark(v, "v")
                k.wait(t, "v")
                for grp in range(4):
                    k.wait(t, "v")
                    for j in range(4):
                        et = grp * 4 + j
                        mi = t.transpose(k.psum_bf[:, j * 128:(j + 1) * 128],
                                         in_=yrow[:, et * 128:(et + 1) * 128],
                                         identity=W["identb"])
                    tm = k.inc(mi, "t")
                    k.wait(v, "t", tm)
                    v.tensor_copy(
                        out=yT[:, tt * DIN + grp * 512:tt * DIN + (grp + 1) * 512],
                        in_=k.psum_bf[:, 0:512])
                    k.mark(v, "v")
                k.wait(y, "v", k.c["v"])
                l3 = k.dma("y", zsb[:, :].rearrange("p (a c) -> p a c", a=NET),
                           zsh[:, :, bb * 256 + tt * 128:
                               bb * 256 + tt * 128 + 128], sem="xc")
                l4 = k.dma("y", xrb[:, :].rearrange("p (a c) -> p a c", a=NET),
                           xih[:, :, bb * 256 + tt * 128:
                               bb * 256 + tt * 128 + 128], sem="xc")
                k.wait(v, "xc", l4)
                ysl = yT[:, tt * DIN:(tt + 1) * DIN]
                v.tensor_tensor(
                    out=xrb[:, :].rearrange("p (e c) -> p e c", e=NET),
                    in0=xrb[:, :].rearrange("p (e c) -> p e c", e=NET),
                    in1=dcol[:, :].rearrange("p (e o) -> p e o", o=1)
                    .broadcast_to([128, NET, 128]), op=ALU.mult)
                v.tensor_tensor(out=ysl, in0=xrb[:, :], in1=ysl, op=ALU.add)
                v.tensor_tensor(out=ysl, in0=ysl, in1=zsb[:, :], op=ALU.mult)
                k.mark(v, "v")
            k.wait(t, "gd")
            k.wait(t, "v")
            for etp in range(NDT):
                ps = k.psum[etp % 2]
                if etp >= 2:
                    k.wait(t, "v")
                for tt2 in range(2):
                    for dt in range(NET):
                        mi = t.matmul(
                            ps[:, tt2 * 128:(tt2 + 1) * 128],
                            lhsT=opw[:, dt * 1024 + etp * 128:
                                     dt * 1024 + (etp + 1) * 128],
                            rhs=yT[:, tt2 * DIN + dt * 128:
                                   tt2 * DIN + (dt + 1) * 128],
                            start=(dt == 0 and tt2 == 0),
                            stop=(dt == NET - 1),
                            skip_group_check=True)
                tm = k.inc(mi, "t")
                k.wait(v, "t", tm)
                rsl = R[:, etp * NTOK + bb * 256:etp * NTOK + (bb + 1) * 256]
                v.tensor_tensor(out=rsl, in0=rsl, in1=ps[:, 0:256], op=ALU.add)
                k.mark(v, "v")
    k.bar()


def build_nc(stage_limit=STAGE_LIMIT):
    nc = bass.Bass("TRN2", target_bir_lowering=False)
    dram = {}

    def din(name, shape, dt=F32):
        dram[name] = nc.dram_tensor(name, shape, dt, kind="ExternalInput").ap()
        return dram[name]

    x_h = din("x", [128, NDT * NTOK])
    out_h = nc.dram_tensor("out", [128, NDT * NTOK], F32,
                           kind="ExternalOutput").ap()
    dbg_h = nc.dram_tensor("dbg", [1, 32], F32, kind="ExternalOutput").ap()
    xn_hbm = nc.dram_tensor("xn_hbm", [128, NDT * NTOK], BF16).ap()
    xh_hbm = nc.dram_tensor("xh_hbm", [128, 32 * 512], BF16).ap()
    cc_in = nc.dram_tensor("cc_kv_in", [128, 1536], BF16).ap()
    cc_out = nc.dram_tensor("cc_kv_out", [128 * NCORES, 1536], BF16).ap()
    for nm in ("wfT", "wiT", "wcT", "ufT", "uiT", "ucT", "woT", "uoT"):
        din("l0f_" + nm, [128, NDT * D])
    din("l0f_bias", [128, 4 * NDT])
    din("l0f_incol", [128, NDT])
    for nm in ("qwT", "gwT", "owT", "pwrT", "pwiT"):
        din("l0a_" + nm, [128, NDT * D])
    din("l0a_kwT", [128, NDT * 512])
    din("l0a_vwT", [128, NDT * 256])
    din("l0a_dw", [128, 2 * NDT * 7])
    din("l0a_dft", [128, NBIN])
    din("l0a_ib", [128, NBIN])
    din("l0a_mask", [128, 128])
    din("l0a_onehot", [128, 16])
    din("l0a_alpha", [1, 1])
    din("l1f_wgate", [128, 24 * 1024])
    din("l1f_wup", [128, 24 * 1024])
    din("l1f_wdown", [128, 8 * 3072])
    wq_g = nc.dram_tensor("wqT_gate", [128, 24 * 1024], BF16).ap()
    wq_u = nc.dram_tensor("wqT_up", [128, 24 * 1024], BF16).ap()
    wq_d = nc.dram_tensor("wqT_down", [128, 8 * 3072], BF16).ap()
    pk_gate = nc.dram_tensor("pk_gate", [128, 24 * NTOK], BF16).ap()
    pk_up = nc.dram_tensor("pk_up", [128, 24 * NTOK], BF16).ap()
    pk_gq = nc.dram_tensor("pk_gq", [128, 24 * NTOK], I16).ap()
    pk_g2 = nc.dram_tensor("pk_g2", [128, 24 * NTOK], I16).ap()
    din("l1a_ipw", [128, NDT * 4096])
    din("l1a_apw", [128, NET * DIN])
    din("l1a_acolh", [128, 16])
    din("l1a_bw", [128, NET * 16])
    din("l1a_cw", [128, NET * 16])
    din("l1a_opw", [128, NET * 1024])
    din("l1a_tau", [128, 128])
    din("l1a_abase", [128, 1])
    din("l1a_tri", [128, 128])
    din("l1a_kmask", [128, LCH])
    din("l1a_dcolh", [128, 16])
    din("l1a_foldmh", [128, 16])
    din("l1a_nwcol", [128, NDT])
    xi_hbm = nc.dram_tensor("xi_hbm", [128, NET * NTOK], BF16).ap()
    zs_hbm = nc.dram_tensor("zs_hbm", [128, NET * NTOK], BF16).ap()
    sl_hbm = nc.dram_tensor("sl_hbm", [128, NET * NTOK], BF16).ap()
    xp_hbm = nc.dram_tensor("xp_hbm", [128, NBT * DIN], BF16).ap()
    g1_hbm = nc.dram_tensor("g1_hbm", [128, NBT * DIN], BF16).ap()
    cc3_in = nc.dram_tensor("cc3_in", [128, B * 272], BF16).ap()
    cc3_out = nc.dram_tensor("cc3_out", [128 * NCORES, B * 272], BF16).ap()
    cc2_in = nc.dram_tensor("cc2_in", [1, 16], F32).ap()
    cc2_out = nc.dram_tensor("cc2_out", [1, 16], F32).ap()
    colb = nc.dram_tensor("colb", [1, 128], F32).ap()

    with ExitStack() as stack:
        block = stack.enter_context(nc.Block())
        k = K(nc, block, stack)
        R = k.sb("R", [128, NDT * NTOK], F32)
        ones16 = k.sb("ones16", [128, 16], BF16)
        ones1 = k.sb("ones1", [1, 128], BF16)
        dbg_sb = k.sb("dbg_sb", [1, 32], F32)
        k.v.memset(dbg_sb[0:1, :], 0.0)
        cst = k.sb("cst", [128, 2], F32)
        k.v.memset(cst[:, 0:1], 1e-5)
        k.v.memset(cst[:, 1:2], 1e-6)
        identb = k.sb("identb", [128, 128], BF16)
        colidx = k.sb("colidx", [128, 128], I32)
        pcol = k.sb("pcol", [128, 1], F32)
        pcoli = k.sb("pcoli", [128, 1], I32)
        k.g.iota(colidx[:, :], pattern=[[1, 128]], base=0, channel_multiplier=0)
        k.g.iota(pcoli[:, :], pattern=[[0, 1]], base=0, channel_multiplier=1)
        gm = k.mark(k.g, "gc")
        k.wait(k.v, "gc", gm)
        k.v.tensor_copy(out=pcol[:, :], in_=pcoli[:, :])
        k.v.tensor_scalar(out=identb[:, :], in0=colidx[:, :],
                          scalar1=pcol[:, 0:1], scalar2=None, op0=ALU.is_equal)

        for dt in range(NDT):
            k.dma("y", R[:, dt * NTOK:(dt + 1) * NTOK],
                  x_h[:, dt * NTOK:(dt + 1) * NTOK])
        k.v.memset(ones16[:, :], 1.0)
        k.v.memset(ones1[:, :], 1.0)
        k.mark(k.v, "v")
        k.bar()

        W0 = {"wfT": dram["l0f_wfT"], "wiT": dram["l0f_wiT"],
              "wcT": dram["l0f_wcT"], "ufT": dram["l0f_ufT"],
              "uiT": dram["l0f_uiT"], "ucT": dram["l0f_ucT"],
              "woT": dram["l0f_woT"], "uoT": dram["l0f_uoT"],
              "bias": dram["l0f_bias"], "incol": dram["l0f_incol"]}

        if stage_limit >= 1 and STAGE_SKIP < 1:
            k.scope("s1")
            with ExitStack() as st:
                emit_layernorm(k, R[:, :], xn_hbm, ones16[:, :], ones1[:, :],
                               st, mode="ln", eps_col=cst[:, 0:1], uid="a")
            with ExitStack() as st:
                emit_l0_ffn(k, R[:, :], xn_hbm, W0, st, uid="a")

        if stage_limit >= 2 and STAGE_SKIP < 2:
            k.scope("s2")
            WA = {nm: dram["l0a_" + nm] for nm in
                  ("qwT", "kwT", "vwT", "owT", "gwT", "pwrT", "pwiT", "dw",
                   "dft", "ib", "mask", "onehot", "alpha")}
            WA["cc_in"] = cc_in
            WA["cc_out"] = cc_out
            WA["ones1"] = ones1[:, :]
            WA["identb"] = identb[:, :]
            with ExitStack() as st:
                emit_layernorm(k, R[:, :], xn_hbm, ones16[:, :], ones1[:, :],
                               st, mode="ln", eps_col=cst[:, 0:1], uid="b")
            with ExitStack() as st:
                emit_l0_attn(k, R[:, :], xn_hbm, xh_hbm, WA, st, uid="b")

        if stage_limit >= 3 and STAGE_SKIP < 3:
            k.scope("s3")
            with ExitStack() as st:
                emit_layernorm(k, R[:, :], xn_hbm, ones16[:, :], ones1[:, :],
                               st, mode="ln", eps_col=cst[:, 0:1], uid="c")
            with ExitStack() as st:
                emit_l0_ffn(k, R[:, :], xn_hbm, W0, st, uid="c")

        WF = {"wgate": dram["l1f_wgate"], "wup": dram["l1f_wup"],
              "wdown": dram["l1f_wdown"], "wqT_wgate": wq_g,
              "wqT_wup": wq_u, "wqT_wdown": wq_d,
              "pk_gate": pk_gate, "pk_up": pk_up, "pk_gq": pk_gq,
              "pk_g2": pk_g2, "cc2_in": cc2_in, "cc2_out": cc2_out,
              "colb": colb, "identb": identb[:, :]}
        if stage_limit >= 4 and STAGE_SKIP < 4:
            k.scope("s4")
            with ExitStack() as st:
                xnmax = k.sb("xnmax4", [128, 1], F32, st)
                with ExitStack() as st2:
                    emit_layernorm(k, R[:, :], xn_hbm, ones16[:, :],
                                   ones1[:, :], st2, mode="ln",
                                   eps_col=cst[:, 0:1], uid="d",
                                   absmax_out=xnmax[:, 0:1])
                emit_l1_ffn(k, R[:, :], xn_hbm, WF, st, xnmax[:, 0:1],
                            first=True, uid="d", dbg=dbg_sb)

        if stage_limit >= 5 and STAGE_SKIP < 5:
            k.scope("s5")
            WA5 = {"ipw": dram["l1a_ipw"], "apw": dram["l1a_apw"],
                   "acolh": dram["l1a_acolh"], "bw": dram["l1a_bw"],
                   "cw": dram["l1a_cw"], "opw": dram["l1a_opw"],
                   "tau": dram["l1a_tau"], "abase": dram["l1a_abase"],
                   "tri": dram["l1a_tri"], "kmask": dram["l1a_kmask"],
                   "dcolh": dram["l1a_dcolh"], "foldmh": dram["l1a_foldmh"],
                   "colb": colb, "identb": identb[:, :],
                   "xi_hbm": xi_hbm, "zs_hbm": zs_hbm, "sl_hbm": sl_hbm,
                   "xp_hbm": xp_hbm, "g1_hbm": g1_hbm,
                   "cc3_in": cc3_in, "cc3_out": cc3_out}
            with ExitStack() as st:
                nwc = k.sb("l1a_nwc", [128, NDT], F32, st)
                k.dma("y", nwc[:, :], dram["l1a_nwcol"][:, :])
                with ExitStack() as st2:
                    emit_layernorm(k, R[:, :], xn_hbm, ones16[:, :],
                                   ones1[:, :], st2, mode="lnrms",
                                   normw_col=nwc[:, :], eps_col=cst[:, 0:1],
                                   eps2_col=cst[:, 1:2], uid="e")
                emit_l1_attn(k, R[:, :], xn_hbm, WA5, st, uid="e")

        if stage_limit >= 6 and STAGE_SKIP < 6:
            k.scope("s6")
            with ExitStack() as st:
                xnmax = k.sb("xnmax6", [128, 1], F32, st)
                with ExitStack() as st2:
                    emit_layernorm(k, R[:, :], xn_hbm, ones16[:, :],
                                   ones1[:, :], st2, mode="ln",
                                   eps_col=cst[:, 0:1], uid="f",
                                   absmax_out=xnmax[:, 0:1])
                emit_l1_ffn(k, R[:, :], xn_hbm, WF, st, xnmax[:, 0:1],
                            first=False, uid="f")

        k.bar()
        for dt in range(NDT):
            k.dma("y", out_h[:, dt * NTOK:(dt + 1) * NTOK],
                  R[:, dt * NTOK:(dt + 1) * NTOK])
        k.dma("y", dbg_h[0:1, :], dbg_sb[0:1, :])
        k.wait(k.y, "yd")
        k.wait(k.g, "yd")
        k.finalize()
    return nc


def _dtmajor(w):
    Dd, E = w.shape
    return np.ascontiguousarray(
        w.reshape(Dd // 128, 128, E).transpose(1, 0, 2).reshape(
            128, (Dd // 128) * E))


def _coltile(vec):
    E = vec.shape[0]
    return np.ascontiguousarray(vec.reshape(E // 128, 128).T)


def _prep_common(inputs):
    d = {}
    for nm in ("wf", "wi", "wc", "uf", "ui", "uc", "wo", "uo"):
        d["l0f_" + nm + "T"] = _dtmajor(np.ascontiguousarray(
            inputs["L0_ffn_" + nm].T))
    d["l0f_bias"] = np.concatenate(
        [_coltile(inputs["L0_ffn_b" + g]) for g in ("f", "i", "c", "o")], axis=1)
    d["l0f_incol"] = _coltile(inputs["L0_ffn_in_c"])
    d["l0a_qwT"] = _dtmajor(np.ascontiguousarray(inputs["L0_attn_q_w"].T))
    kw2 = np.repeat(inputs["L0_attn_k_w"].reshape(4, 1, 64, D), 2,
                    axis=1).reshape(512, D)
    d["l0a_kwT"] = _dtmajor(np.ascontiguousarray(kw2.T))
    d["l0a_vwT"] = _dtmajor(np.ascontiguousarray(inputs["L0_attn_v_w"].T))
    d["l0a_owT"] = _dtmajor(np.ascontiguousarray(inputs["L0_attn_o_w"].T))
    d["l0a_gwT"] = _dtmajor(np.ascontiguousarray(inputs["L0_attn_gate_w"].T))
    d["l0a_pwrT"] = _dtmajor(np.ascontiguousarray(inputs["L0_attn_pw_r"].T))
    d["l0a_pwiT"] = _dtmajor(np.ascontiguousarray(inputs["L0_attn_pw_i"].T))
    dwr = _dtmajor(inputs["L0_attn_dw_r"].reshape(1024, 7))
    dwi = _dtmajor(inputs["L0_attn_dw_i"].reshape(1024, 7))
    d["l0a_dw"] = np.concatenate([dwr, dwi], axis=1)
    tt = np.arange(64.0)
    cols = [np.cos(2 * np.pi * m * tt / 64.0) for m in range(33)]
    cols += [-np.sin(2 * np.pi * m * tt / 64.0) for m in range(1, 32)]
    dft = np.stack(cols, axis=1).astype(np.float32)
    d["l0a_dft"] = np.concatenate([dft, dft], axis=0)
    ib = np.zeros((64, 64), np.float32)
    for j in range(64):
        vv = np.zeros(33, complex)
        if j <= 32:
            vv[j] = 1.0
        else:
            vv[j - 32] = 1j
        ib[j] = np.fft.irfft(vv, 64).astype(np.float32)
    d["l0a_ib"] = np.concatenate([ib, ib], axis=0)
    oh = np.zeros((128, 16), np.float32)
    oh[0:4] = np.repeat(np.eye(4, dtype=np.float32), 4, axis=1)
    oh[64:68] = oh[0:4]
    d["l0a_onehot"] = oh
    d["l0a_alpha"] = inputs["L0_attn_alpha_raw"].reshape(1, 1)
    d["l1a_ipw"] = _dtmajor(inputs["L1_attn_in_proj_w"])
    d["l1a_apw"] = _dtmajor(inputs["L1_attn_A_proj_w"])
    d["l1a_acolh"] = _coltile(inputs["L1_attn_A_proj_b"])
    d["l1a_bw"] = _dtmajor(inputs["L1_attn_B_proj_w"])
    d["l1a_cw"] = _dtmajor(inputs["L1_attn_C_proj_w"])
    d["l1a_opw"] = _dtmajor(inputs["L1_attn_out_proj_w"])
    d["l1a_dcolh"] = _coltile(inputs["L1_attn_D"])
    d["l1a_nwcol"] = _coltile(inputs["L1_attn_norm_w"])
    d["l1a_abase"] = np.ascontiguousarray(
        np.tile(np.asarray(inputs["L1_attn_A_base"], np.float32),
                8).reshape(128, 1))
    tauv = np.concatenate([np.arange(64.0),
                           63.0 - np.arange(64.0)]).astype(np.float32)
    d["l1a_tau"] = np.ascontiguousarray(np.tile(tauv[None, :], (128, 1)))
    jj, ss = np.meshgrid(np.arange(64), np.arange(64), indexing="ij")
    M1 = (jj <= ss).astype(np.float32) - (jj <= 31).astype(np.float32)
    M2 = (jj > ss).astype(np.float32)
    d["l1a_tri"] = np.ascontiguousarray(
        np.tile(np.concatenate([M1, M2], axis=1), (2, 1)))
    d["l1a_kmask"] = np.ascontiguousarray(
        np.tile((jj <= ss).astype(np.float32), (2, 1)))
    d["l1f_wgate"] = _dtmajor(inputs["L1_ffn_w_gate"])
    d["l1f_wup"] = _dtmajor(inputs["L1_ffn_w_up"])
    d["l1f_wdown"] = _dtmajor(inputs["L1_ffn_w_down"])
    return d




def _core_inputs(ci, x, common):
    sl = x[:, ci * TLOC:(ci + 1) * TLOC, :]
    xT = np.ascontiguousarray(sl.transpose(2, 0, 1).reshape(D, NTOK))
    m = {"x": _dtmajor(xT)}
    m.update(common)
    kchunk = np.arange(128) // 4
    mrows = np.stack(
        [np.where(kchunk <= 4 * ci + c, 0.0, -1e30).astype(np.float32)
         for c in range(4)], axis=0)
    mk = np.zeros((128, 128), np.float32)
    mk[0:4] = mrows
    mk[64:68] = mrows
    m["l0a_mask"] = mk
    fm = np.zeros((128, 16), np.float32)
    for r in range(8):
        fm[:, r] = 1.0 if r < ci else 0.0
        fm[:, 8 + r] = 1.0 - fm[0, r]
    m["l1a_foldmh"] = fm
    return m

def kernel(**inputs):
    inputs = {k_: np.asarray(v, dtype=np.float32) for k_, v in inputs.items()}
    x = inputs["L0_attn_hidden_states"]
    common = _prep_common(inputs)
    nc = build_nc(STAGE_LIMIT)
    in_maps = []
    for ci in range(NCORES):
        in_maps.append(_core_inputs(ci, x, common))
    res = run_bass_kernel_spmd(nc, in_maps, core_ids=list(range(NCORES)))
    full = np.empty((B, 2048, D), dtype=np.float32)
    for ci in range(NCORES):
        o = np.asarray(res.results[ci]["out"]).reshape(128, NDT, B, TLOC)
        full[:, ci * TLOC:(ci + 1) * TLOC, :] = (
            o.transpose(2, 3, 1, 0).reshape(B, TLOC, D))
    return full
